# version 09
# baseline (speedup 1.0000x reference)
import contextlib
import numpy as np
import concourse.bass as bass
import concourse.mybir as mybir

F32 = mybir.dt.float32
BF16 = mybir.dt.bfloat16
I32 = mybir.dt.int32
AF = mybir.ActivationFunctionType
ALU = mybir.AluOpType

S = 4096
H = 2048
D = 128
NH = 16
KVH = 4
WIN = 1024
INTER = 5504
NIC = INTER // 128
NKC = H // 128
TT = 512
NTILE = S // TT
SCALE = 1.0 / float(np.sqrt(D))
NEG = -1.0e9
EPS = 1e-5

ENGINES = ("sync", "vector", "scalar", "tensor", "gpsimd")


class Sem:
    def __init__(self, es, nc, name):
        self.h = es.enter_context(nc.semaphore(name))
        self.v = 0


class Prog:
    def __init__(self):
        self.ops = {e: [] for e in ENGINES}

    def on(self, eng, fn):
        self.ops[eng].append(fn)

    def wait(self, eng, sem, val):
        if val > 0:
            self.ops[eng].append(lambda e, s=sem.h, v=val: e.wait_ge(s, v))

    def emit(self, blk):
        for eng in ENGINES:
            if not self.ops[eng]:
                continue
            def body(e, lst=tuple(self.ops[eng])):
                for fn in lst:
                    fn(e)
            getattr(blk, eng)(body)


def build(phases=("p0", "p1a", "p1b", "p2", "p3", "p4"), dbg=()):
    nc = bass.Bass("TRN2", target_bir_lowering=False)

    x = nc.dram_tensor("x", [S, H], F32, kind="ExternalInput").ap()
    wq = nc.dram_tensor("wq", [H, H], F32, kind="ExternalInput").ap()
    wk = nc.dram_tensor("wk", [H, KVH * D], F32, kind="ExternalInput").ap()
    wv = nc.dram_tensor("wv", [H, KVH * D], F32, kind="ExternalInput").ap()
    wo = nc.dram_tensor("wo", [H, H], F32, kind="ExternalInput").ap()
    lutr = nc.dram_tensor("lutr", [128, 16], F32, kind="ExternalInput").ap()
    gidx = nc.dram_tensor("gidx", [INTER, H], I32, kind="ExternalInput").ap()
    uidx = nc.dram_tensor("uidx", [INTER, H], I32, kind="ExternalInput").ap()
    didx = nc.dram_tensor("didx", [H, INTER], I32, kind="ExternalInput").ap()
    out = nc.dram_tensor("out", [S, H], F32, kind="ExternalOutput").ap()

    xln_s = nc.dram_tensor("xln_s", [H, S], BF16, kind="Internal").ap()
    q_s = nc.dram_tensor("q_s", [H, S], BF16, kind="Internal").ap()
    ao_s = nc.dram_tensor("ao_s", [H, S], BF16, kind="Internal").ap()
    p4_standalone = ("p4" in phases and "p3" not in phases)
    x1_s = nc.dram_tensor("x1_s", [S, H], F32, kind="ExternalInput" if p4_standalone else "Internal").ap()
    y_s = nc.dram_tensor("y_s", [H, S], BF16, kind="ExternalInput" if p4_standalone else "Internal").ap()
    wg_s = nc.dram_tensor("wg_s", [NKC, 128, INTER], BF16, kind="Internal").ap()
    wu_s = nc.dram_tensor("wu_s", [NKC, 128, INTER], BF16, kind="Internal").ap()
    wd_s = nc.dram_tensor("wd_s", [NKC, 128, INTER], BF16, kind="Internal").ap()

    dbg_aps = {}
    with contextlib.ExitStack() as top:
        ident = top.enter_context(nc.sbuf_tensor("ident", [128, 128], BF16))
        slut = top.enter_context(nc.sbuf_tensor("slut", [128, 16], F32))
        ones_col = top.enter_context(nc.sbuf_tensor("ones_col", [128, 1], BF16))
        ones_row = top.enter_context(nc.sbuf_tensor("ones_row", [1, 128], F32))

        _setup(nc, lutr, slut, ident, ones_col, ones_row)

        gath = None
        if "p0" in phases:
            g_es = top.enter_context(contextlib.ExitStack())
            gath = Gather(nc, g_es, slut, ident, gidx, uidx, didx, wg_s, wu_s, wd_s)
        with contextlib.ExitStack() as kvscope:
            k_cache = kvscope.enter_context(nc.sbuf_tensor("k_cache", [128, KVH * S], BF16))
            v_cache = kvscope.enter_context(nc.sbuf_tensor("v_cache", [128, KVH * S], BF16))
            if "p1a" in phases:
                _p1a(nc, x, wk, wv, xln_s, k_cache, v_cache, ident)
            if "p1b" in phases:
                _p1b(nc, wq, xln_s, q_s)
            if "p2" in phases:
                _p2(nc, q_s, ao_s, k_cache, v_cache, ones_col, ones_row, gath)
            if "kv" in dbg:
                for nm, t in (("k_cache", k_cache), ("v_cache", v_cache)):
                    dst = nc.dram_tensor("dbg_" + nm, [128, KVH * S], BF16, kind="ExternalOutput").ap()
                    dbg_aps[nm] = dst
                    _sbuf_dump(nc, dst, t)
        if "p3" in phases:
            _p3(nc, x, wo, ao_s, x1_s, y_s, ident, gath)
        if gath is not None:
            _p0(nc, gath)
        if "p4" in phases:
            _p4(nc, y_s, x1_s, wg_s, wu_s, wd_s, out, ident)
        else:
            _dump_stub(nc, out)

        for name, src, shp, dt in (
            ("xln_s", xln_s, [H, S], BF16), ("q_s", q_s, [H, S], BF16),
            ("ao_s", ao_s, [H, S], BF16), ("x1_s", x1_s, [S, H], F32),
            ("y_s", y_s, [H, S], BF16),
            ("wg_s", wg_s, [NKC, 128, INTER], BF16),
            ("wu_s", wu_s, [NKC, 128, INTER], BF16),
            ("wd_s", wd_s, [NKC, 128, INTER], BF16),
        ):
            if name in dbg:
                dst = nc.dram_tensor("dbg_" + name, shp, dt, kind="ExternalOutput").ap()
                dbg_aps[name] = dst
                _dram_copy(nc, dst, src, shp, dt)
    return nc, dbg_aps


def _setup(nc, lutr, slut, ident, ones_col, ones_row):
    with contextlib.ExitStack() as es:
        sidi = es.enter_context(nc.sbuf_tensor("sidi", [128, 128], I32))
        ld = Sem(es, nc, "su_ld")
        dn = Sem(es, nc, "su_dn")
        p = Prog()
        def f(e):
            e.dma_start(out=slut[:], in_=lutr[:, :]).then_inc(ld.h, 16)
        p.on("sync", f); ld.v += 16
        def f(e):
            e.iota(sidi[:], pattern=[[-1, 128]], base=0, channel_multiplier=1).then_inc(dn.h, 1)
        p.on("gpsimd", f); dn.v += 1
        p.wait("vector", dn, dn.v)
        p.wait("vector", ld, ld.v)
        def f(e):
            e.tensor_scalar(out=ident[:], in0=sidi[:], scalar1=0, scalar2=None, op0=ALU.is_equal)
            e.memset(ones_col[:], 1.0)
            e.memset(ones_row[:], 1.0)
        p.on("vector", f)
        blk = es.enter_context(nc.Block("setup"))
        p.emit(blk)


def _dump_stub(nc, out):
    with contextlib.ExitStack() as es:
        z = es.enter_context(nc.sbuf_tensor("zz", [128, H], F32))
        st = Sem(es, nc, "z_st")
        dn = Sem(es, nc, "z_dn")
        p = Prog()
        def f(e):
            e.memset(z[:], 0.0).then_inc(dn.h, 1)
        p.on("vector", f); dn.v += 1
        p.wait("gpsimd", dn, dn.v)
        def f(e):
            for i in range(S // 128):
                e.dma_start(out=out[i * 128:(i + 1) * 128, :], in_=z[:]).then_inc(st.h, 16)
            e.wait_ge(st.h, 16 * (S // 128))
        p.on("gpsimd", f); st.v += 16 * (S // 128)
        blk = es.enter_context(nc.Block("zout"))
        p.emit(blk)


_DP_N = [0]
def _sbuf_dump(nc, dst, src):
    _DP_N[0] += 1
    uid = _DP_N[0]
    with contextlib.ExitStack() as es:
        st = Sem(es, nc, f"dmp_st{uid}")
        p = Prog()
        def f(e):
            e.dma_start(out=dst[:, :], in_=src[:]).then_inc(st.h, 16)
            e.wait_ge(st.h, 16)
        p.on("gpsimd", f)
        blk = es.enter_context(nc.Block(f"dmp{uid}"))
        p.emit(blk)


_CP_N = [0]
def _dram_copy(nc, dst, src, shp, dt):
    total = int(np.prod(shp))
    FREE = 8192
    _CP_N[0] += 1
    uid = _CP_N[0]
    with contextlib.ExitStack() as es:
        buf = es.enter_context(nc.sbuf_tensor(f"cp_buf{uid}", [128, FREE], dt))
        ld = Sem(es, nc, f"cp_ld{uid}")
        st = Sem(es, nc, f"cp_st{uid}")
        p = Prog()
        if len(shp) == 2:
            fsrc = src.rearrange("a b -> (a b)")
            fdst = dst.rearrange("a b -> (a b)")
        else:
            fsrc = src.rearrange("a b c -> (a b c)")
            fdst = dst.rearrange("a b c -> (a b c)")
        chunk = 128 * FREE
        nch = (total + chunk - 1) // chunk
        def f(e):
            for i in range(nch):
                n = min(chunk, total - i * chunk)
                rows = n // FREE
                e.wait_ge(st.h, 16 * i)
                e.dma_start(out=buf[0:rows, :], in_=fsrc[i * chunk:i * chunk + n].rearrange("(pp ff) -> pp ff", ff=FREE)).then_inc(ld.h, 16)
                e.wait_ge(ld.h, 16 * (i + 1))
                e.dma_start(out=fdst[i * chunk:i * chunk + n].rearrange("(pp ff) -> pp ff", ff=FREE), in_=buf[0:rows, :]).then_inc(st.h, 16)
            e.wait_ge(st.h, 16 * nch)
        p.on("gpsimd", f)
        blk = es.enter_context(nc.Block(f"cp{uid}"))
        p.emit(blk)


class Gather:
    def __init__(self, nc, es, slut, ident, gidx, uidx, didx, wg_s, wu_s, wd_s):
        self.nc = nc
        self.slut = slut
        self.ident = ident
        units = []
        for b_ in range(NIC):
            units.append((gidx[b_ * 128:(b_ + 1) * 128, :], wg_s, b_, 0, H, "gu"))
        for b_ in range(NIC):
            units.append((uidx[b_ * 128:(b_ + 1) * 128, :], wu_s, b_, 0, H, "gu"))
        for b_ in range(NKC):
            for c0, w in ((0, 2048), (2048, 2048), (4096, 1408)):
                units.append((didx[b_ * 128:(b_ + 1) * 128, c0:c0 + w], wd_s, b_, c0, w, "d"))
        self.units = units
        self.cursor = 0
        self.ld = Sem(es, nc, "G_ld")
        self.gat = Sem(es, nc, "G_gat")
        self.trn = Sem(es, nc, "G_trn")
        self.evc = Sem(es, nc, "G_evc")
        self.stg = Sem(es, nc, "G_stg")
        self.npc_hist = []
        self.phase_first = 0
        self._t = None

    def start_phase(self, es, width=2048):
        nc = self.nc
        W = width
        self.bw = W
        self.sidx = [es.enter_context(nc.sbuf_tensor(f"G_idx{i}_{self.cursor}", [128, W], I32)) for i in range(2)]
        self.sidxb = [es.enter_context(nc.sbuf_tensor(f"G_idxb{i}_{self.cursor}", [128, W], BF16)) for i in range(2)]
        self.smask = [es.enter_context(nc.sbuf_tensor(f"G_msk{i}_{self.cursor}", [128, W // 2], BF16)) for i in range(2)]
        self.smlut = [es.enter_context(nc.sbuf_tensor(f"G_mlt{i}_{self.cursor}", [128, W // 2], BF16)) for i in range(2)]
        self.sacc = [es.enter_context(nc.sbuf_tensor(f"G_acc{i}_{self.cursor}", [128, W], BF16)) for i in range(2)]
        self.sstage = [es.enter_context(nc.sbuf_tensor(f"G_stg{i}_{self.cursor}", [128, W], BF16)) for i in range(2)]
        self.pt = es.enter_context(nc.psum_tensor(f"G_pt_{self.cursor}", [128, 2048], BF16))
        self.phase_first = self.cursor

    def remaining(self):
        return len(self.units) - self.cursor

    def emit_units(self, p, n):
        for _ in range(min(n, self.remaining())):
            self._unit_head(p)
            while self._t < 16:
                self._emit_slice(p, 16)
            self._unit_tail(p)

    def emit_slice(self, p, nt=4):
        if self.cursor >= len(self.units):
            return False
        if self._t is None:
            self._unit_head(p)
        self._emit_slice(p, nt)
        if self._t >= 16:
            self._unit_tail(p)
        return True

    def flush_phase(self, p):
        if self._t is not None:
            while self._t < 16:
                self._emit_slice(p, 16)
            self._unit_tail(p)

    def _unit_head(self, p):
        u = self.cursor
        src_ap, dst, b_, c0, fw, kind = self.units[u]
        assert fw <= self.bw
        pl = u - self.phase_first
        slot = pl % 2
        if pl >= 2:
            p.wait("sync", self.gat, u - 1)
        def f(e, src=src_ap, buf=self.sidx[slot], fw=fw, ld=self.ld):
            e.dma_start(out=buf[:, 0:fw], in_=src).then_inc(ld.h, 16)
        p.on("sync", f); self.ld.v += 16
        p.wait("vector", self.ld, self.ld.v)
        if pl >= 2:
            p.wait("vector", self.trn, sum(self.npc_hist) - self.npc_hist[-1])
        self._t = 0

    def _emit_slice(self, p, nt):
        u = self.cursor
        src_ap, dst, b_, c0, fw, kind = self.units[u]
        hw = fw // 2
        pl = u - self.phase_first
        slot = pl % 2
        t0, t1 = self._t, min(16, self._t + nt)
        slut = self.slut
        def f(e, idx=self.sidx[slot], idxb=self.sidxb[slot], acc=self.sacc[slot],
              hw=hw, msk=self.smask, mlt=self.smlut, gat=self.gat, t0=t0, t1=t1):
            for t in range(t0, t1):
                if t == 0:
                    for hh in range(2):
                        e.tensor_scalar(out=idxb[:, hh * hw:hh * hw + hw], in0=idx[:, hh * hw:hh * hw + hw],
                                        scalar1=0.0, scalar2=None, op0=ALU.add)
                for hh in range(2):
                    e.tensor_scalar(out=msk[hh][:, 0:hw], in0=idxb[:, hh * hw:hh * hw + hw],
                                    scalar1=float(t), scalar2=None, op0=ALU.is_equal)
                for hh in range(2):
                    e.tensor_scalar(out=mlt[hh][:, 0:hw], in0=msk[hh][:, 0:hw],
                                    scalar1=slut[:, t:t + 1], scalar2=None, op0=ALU.mult)
                for hh in range(2):
                    if t == 0:
                        e.tensor_copy(acc[:, hh * hw:hh * hw + hw], mlt[hh][:, 0:hw])
                    else:
                        i_ = e.tensor_tensor(out=acc[:, hh * hw:hh * hw + hw],
                                             in0=acc[:, hh * hw:hh * hw + hw],
                                             in1=mlt[hh][:, 0:hw], op=ALU.add)
                        if t == 15 and hh == 1:
                            i_.then_inc(gat.h, 1)
        p.on("vector", f)
        if t1 == 16:
            self.gat.v += 1
        self._t = t1

    def _unit_tail(self, p):
        u = self.cursor
        src_ap, dst, b_, c0, fw, kind = self.units[u]
        npc = fw // 128
        pl = u - self.phase_first
        slot = pl % 2
        prior_npc = sum(self.npc_hist)
        ident = self.ident
        p.wait("tensor", self.gat, self.gat.v)
        for pc in range(npc):
            ti = prior_npc + pc
            if ti >= 2:
                p.wait("tensor", self.evc, ti - 1)
            def f(e, acc=self.sacc[slot], pc=pc, pslot=ti % 2, pt=self.pt):
                e.transpose(pt[:, pslot * 1024:pslot * 1024 + 128],
                            acc[:, pc * 128:pc * 128 + 128], ident[:]).then_inc(self.trn.h, 1)
            p.on("tensor", f)
        self.trn.v = prior_npc + npc
        if pl >= 2:
            p.wait("scalar", self.stg, 16 * (u - 1))
        for pc in range(npc):
            ti = prior_npc + pc
            p.wait("scalar", self.trn, ti + 1)
            def f(e, stage=self.sstage[slot], pc=pc, pslot=ti % 2, pt=self.pt):
                e.copy(stage[:, pc * 128:pc * 128 + 128],
                       pt[:, pslot * 1024:pslot * 1024 + 128]).then_inc(self.evc.h, 1)
            p.on("scalar", f)
        self.evc.v = prior_npc + npc
        if kind == "gu":
            def f(e, dst=dst, b_=b_, stage=self.sstage[slot]):
                e.dma_start(out=dst[:, :, b_ * 128:(b_ + 1) * 128].rearrange("kc pp j -> pp kc j"),
                            in_=stage[:, 0:H].rearrange("pp (kc j) -> pp kc j", kc=NKC)).then_inc(self.stg.h, 16)
        else:
            def f(e, dst=dst, b_=b_, c0=c0, fw=fw, stage=self.sstage[slot]):
                e.dma_start(out=dst[b_, :, c0:c0 + fw], in_=stage[:, 0:fw]).then_inc(self.stg.h, 16)
        p.on("scalar", f); self.stg.v += 16
        self.npc_hist.append(npc)
        self.cursor += 1
        self._t = None

    def finish_wait(self, p, eng="scalar"):
        p.wait(eng, self.stg, self.stg.v)


def _p0(nc, g, n_units=None):
    if g.remaining() == 0:
        return
    n = g.remaining() if n_units is None else min(n_units, g.remaining())
    with contextlib.ExitStack() as es:
        g.start_phase(es, width=2048)
        p = Prog()
        g.emit_units(p, n)
        if g.remaining() == 0:
            g.finish_wait(p)
        blk = es.enter_context(nc.Block("p0"))
        p.emit(blk)


def _p1a(nc, x, wk, wv, xln_s, k_cache, v_cache, ident):
    with contextlib.ExitStack() as es:
        wkb = es.enter_context(nc.sbuf_tensor("wkb", [128, NKC * KVH * D], BF16))
        wvb = es.enter_context(nc.sbuf_tensor("wvb", [128, NKC * KVH * D], BF16))
        wtmp = [es.enter_context(nc.sbuf_tensor(f"wtmp{i}", [128, KVH * D], F32)) for i in range(2)]
        xbuf = [es.enter_context(nc.sbuf_tensor(f"xbuf{i}", [128, 4 * H], F32)) for i in range(2)]
        xstat = es.enter_context(nc.sbuf_tensor("xstat", [128, 24], F32))
        xaggr = es.enter_context(nc.sbuf_tensor("xaggr", [128, 2], F32))
        nmean = es.enter_context(nc.sbuf_tensor("nmean", [128, 1], F32))
        rstd = es.enter_context(nc.sbuf_tensor("rstd", [128, 1], F32))
        xln_tm = es.enter_context(nc.sbuf_tensor("xln_tm", [128, 2 * H], BF16))
        xlnfm = es.enter_context(nc.sbuf_tensor("xlnfm", [128, NKC * TT], BF16))
        vtmp = es.enter_context(nc.sbuf_tensor("vtmp", [128, TT], BF16))
        ptx = es.enter_context(nc.psum_tensor("p1_ptx", [128, 2048], BF16))
        ptv = es.enter_context(nc.psum_tensor("p1_ptv", [128, 2048], BF16))
        pkv = es.enter_context(nc.psum_tensor("p1_pkv", [128, 2 * TT], F32))
        ld = Sem(es, nc, "p1_ld")
        ldx = Sem(es, nc, "p1_ldx")
        cst = Sem(es, nc, "p1_cst")
        lnq = Sem(es, nc, "p1_lnq")
        lnr = Sem(es, nc, "p1_lnr")
        ntm = Sem(es, nc, "p1_ntm")
        trn = Sem(es, nc, "p1_trn")
        evc = Sem(es, nc, "p1_evc")
        mmk = Sem(es, nc, "p1_mmk")
        kve = Sem(es, nc, "p1_kve")
        vtr = Sem(es, nc, "p1_vtr")
        vev = Sem(es, nc, "p1_vev")
        sx = Sem(es, nc, "p1_sx")
        p = Prog()
        NCH = 4

        for w_i, (wsrc, wdst) in enumerate(((wk, wkb), (wv, wvb))):
            for kc in range(NKC):
                j = w_i * NKC + kc
                if j >= 2:
                    p.wait("sync", cst, j - 1)
                def f(e, wsrc=wsrc, kc=kc, buf=wtmp[j % 2], ld=ld):
                    e.dma_start(out=buf[:], in_=wsrc[kc * 128:(kc + 1) * 128, :]).then_inc(ld.h, 16)
                p.on("sync", f); ld.v += 16
                p.wait("vector", ld, 16 * (j + 1))
                def f(e, wdst=wdst, kc=kc, buf=wtmp[j % 2], cst=cst):
                    e.tensor_copy(wdst[:, kc * (KVH * D):(kc + 1) * (KVH * D)], buf[:]).then_inc(cst.h, 1)
                p.on("vector", f); cst.v += 1

        for tt in range(NTILE):
            if tt >= 2:
                p.wait("sync", ntm, (tt - 1) * NCH)
            def f(e, tt=tt, buf=xbuf[tt % 2], ldx=ldx):
                e.dma_start(out=buf[:].rearrange("pp (c hh) -> pp c hh", c=4), in_=x[tt * TT:(tt + 1) * TT, :].rearrange("(c pp) hh -> pp c hh", pp=128)).then_inc(ldx.h, 16)
            p.on("sync", f); ldx.v += 16
            p.wait("vector", ldx, 16 * (tt + 1))
            for c in range(NCH):
                k = tt * NCH + c
                def f(e, xc_buf=xbuf[tt % 2], c=c):
                    xc = xc_buf[:, c * H:(c + 1) * H]
                    for i in range(4):
                        e.bn_stats(xstat[:, 6 * i:6 * (i + 1)], xc[:, i * 512:(i + 1) * 512])
                    e.drain()
                    e.bn_aggr(xaggr[:], xstat[:])
                    e.drain()
                    e.tensor_scalar(out=nmean[:], in0=xaggr[:, 0:1], scalar1=-1.0, scalar2=None, op0=ALU.mult)
                    e.tensor_scalar(out=rstd[:], in0=xaggr[:, 1:2], scalar1=EPS, scalar2=None, op0=ALU.add).then_inc(lnq.h, 1)
                p.on("vector", f); lnq.v += 1
                p.wait("scalar", lnq, lnq.v)
                def f(e):
                    e.activation(out=rstd[:], in_=rstd[:], func=AF.Ln)
                    e.drain()
                    e.activation(out=rstd[:], in_=rstd[:], func=AF.Exp, scale=-0.5).then_inc(lnr.h, 1)
                p.on("scalar", f); lnr.v += 1
                p.wait("vector", lnr, lnr.v)
                if k >= 2:
                    p.wait("vector", trn, (k - 1) * NKC)
                def f(e, xc_buf=xbuf[tt % 2], c=c):
                    xc = xc_buf[:, c * H:(c + 1) * H]
                    e.tensor_scalar(out=xln_tm[:, (c % 2) * H:(c % 2) * H + H], in0=xc,
                                    scalar1=nmean[:], scalar2=rstd[:], op0=ALU.add, op1=ALU.mult).then_inc(ntm.h, 1)
                p.on("vector", f); ntm.v += 1
                p.wait("tensor", ntm, ntm.v)
                for kc in range(NKC):
                    ti = k * NKC + kc
                    if ti >= 2:
                        p.wait("tensor", evc, ti - 1)
                    def f(e, c=c, kc=kc, slot=ti % 2):
                        e.transpose(ptx[:, slot * 1024:slot * 1024 + 128],
                                    xln_tm[:, (c % 2) * H + kc * 128:(c % 2) * H + (kc + 1) * 128],
                                    ident[:]).then_inc(trn.h, 1)
                    p.on("tensor", f)
                trn.v = (k + 1) * NKC
                for kc in range(NKC):
                    ti = k * NKC + kc
                    p.wait("scalar", trn, ti + 1)
                    def f(e, c=c, kc=kc, slot=ti % 2):
                        e.copy(xlnfm[:, kc * TT + c * 128:kc * TT + c * 128 + 128],
                               ptx[:, slot * 1024:slot * 1024 + 128]).then_inc(evc.h, 1)
                    p.on("scalar", f)
                evc.v = (k + 1) * NKC
            p.wait("tensor", evc, evc.v)
            if tt == 0:
                p.wait("tensor", cst, 2 * NKC)
            for w_i, wsb in enumerate((wkb, wvb)):
                for kvh in range(KVH):
                    mi = tt * 2 * KVH + w_i * KVH + kvh
                    if mi >= 2:
                        p.wait("tensor", kve, mi - 1)
                    for kc in range(NKC):
                        last = (kc == NKC - 1)
                        def f(e, wsb=wsb, kvh=kvh, kc=kc, slot=mi % 2, last=last):
                            i_ = e.matmul(pkv[:, slot * TT:slot * TT + TT],
                                          wsb[:, kc * (KVH * D) + kvh * D:kc * (KVH * D) + (kvh + 1) * D],
                                          xlnfm[:, kc * TT:(kc + 1) * TT], start=(kc == 0), stop=last)
                            if last:
                                i_.then_inc(mmk.h, 1)
                        p.on("tensor", f)
                    mmk.v += 1
                    p.wait("scalar", mmk, mmk.v)
                    if w_i == 0:
                        def f(e, kvh=kvh, tt=tt, slot=mi % 2):
                            e.copy(k_cache[:, kvh * S + tt * TT:kvh * S + (tt + 1) * TT],
                                   pkv[:, slot * TT:slot * TT + TT]).then_inc(kve.h, 1)
                        p.on("scalar", f); kve.v += 1
                    else:
                        def f(e, slot=mi % 2):
                            e.copy(vtmp[:], pkv[:, slot * TT:slot * TT + TT]).then_inc(kve.h, 1)
                        p.on("scalar", f); kve.v += 1
                        p.wait("tensor", kve, kve.v)
                        for c in range(NCH):
                            vi = vtr.v
                            if vi >= 2:
                                p.wait("tensor", vev, vi - 1)
                            def f(e, c=c, slot=vi % 2):
                                e.transpose(ptv[:, slot * 1024:slot * 1024 + 128],
                                            vtmp[:, c * 128:(c + 1) * 128], ident[:]).then_inc(vtr.h, 1)
                            p.on("tensor", f); vtr.v += 1
                            p.wait("scalar", vtr, vtr.v)
                            def f(e, kvh=kvh, tt=tt, c=c, slot=vi % 2):
                                e.copy(v_cache[:, kvh * S + tt * TT + c * 128:kvh * S + tt * TT + (c + 1) * 128],
                                       ptv[:, slot * 1024:slot * 1024 + 128]).then_inc(vev.h, 1)
                            p.on("scalar", f); vev.v += 1
                        p.wait("scalar", vtr, vtr.v)
            p.wait("gpsimd", evc, evc.v)
            def f(e, tt=tt):
                e.dma_start(out=xln_s[:, tt * TT:(tt + 1) * TT].rearrange("(kc pp) t -> pp kc t", pp=128),
                            in_=xlnfm[:].rearrange("pp (kc t) -> pp kc t", kc=NKC)).then_inc(sx.h, 16)
            p.on("gpsimd", f); sx.v += 16
            p.wait("scalar", sx, sx.v)
            p.wait("scalar", mmk, mmk.v)
        p.wait("gpsimd", sx, sx.v)
        blk = es.enter_context(nc.Block("p1a"))
        p.emit(blk)


def _p1b(nc, wq, xln_s, q_s):
    with contextlib.ExitStack() as es:
        wqb = es.enter_context(nc.sbuf_tensor("wqb", [128, NKC * H], BF16))
        wtmp = [es.enter_context(nc.sbuf_tensor(f"qwt{i}", [128, H], F32)) for i in range(2)]
        xt = [es.enter_context(nc.sbuf_tensor(f"qxt{i}", [128, NKC * TT], BF16)) for i in range(2)]
        qstage = es.enter_context(nc.sbuf_tensor("qstage", [128, NH * TT], BF16))
        pq = es.enter_context(nc.psum_tensor("pq", [128, 2 * TT], F32))
        ld = Sem(es, nc, "q_ld")
        cst = Sem(es, nc, "q_cst")
        ldx = Sem(es, nc, "q_ldx")
        mm = Sem(es, nc, "q_mm")
        qe = Sem(es, nc, "q_qe")
        st = Sem(es, nc, "q_st")
        p = Prog()
        for kc in range(NKC):
            if kc >= 2:
                p.wait("sync", cst, kc - 1)
            def f(e, kc=kc, buf=wtmp[kc % 2], ld=ld):
                e.dma_start(out=buf[:], in_=wq[kc * 128:(kc + 1) * 128, :]).then_inc(ld.h, 16)
            p.on("sync", f); ld.v += 16
            p.wait("vector", ld, 16 * (kc + 1))
            def f(e, kc=kc, buf=wtmp[kc % 2], cst=cst):
                e.tensor_copy(wqb[:, kc * H:(kc + 1) * H], buf[:]).then_inc(cst.h, 1)
            p.on("vector", f); cst.v += 1

        for tt in range(NTILE):
            if tt >= 2:
                p.wait("sync", mm, (tt - 1) * NH)
            def f(e, tt=tt, buf=xt[tt % 2], ldx=ldx):
                e.dma_start(out=buf[:].rearrange("pp (kc t) -> pp kc t", kc=NKC), in_=xln_s[:, tt * TT:(tt + 1) * TT].rearrange("(kc pp) t -> pp kc t", pp=128)).then_inc(ldx.h, 16)
            p.on("sync", f); ldx.v += 16
            p.wait("tensor", ldx, 16 * (tt + 1))
            if tt == 0:
                p.wait("tensor", cst, NKC)
            if tt >= 1:
                p.wait("scalar", st, 16 * tt)
            for m in range(NH):
                mi = tt * NH + m
                if mi >= 2:
                    p.wait("tensor", qe, mi - 1)
                for kc in range(NKC):
                    last = (kc == NKC - 1)
                    def f(e, m=m, kc=kc, buf=xt[tt % 2], slot=mi % 2, last=last):
                        i_ = e.matmul(pq[:, slot * TT:slot * TT + TT],
                                      wqb[:, kc * H + m * 128:kc * H + (m + 1) * 128],
                                      buf[:, kc * TT:(kc + 1) * TT], start=(kc == 0), stop=last)
                        if last:
                            i_.then_inc(mm.h, 1)
                    p.on("tensor", f)
                mm.v += 1
                p.wait("scalar", mm, mm.v)
                def f(e, m=m, slot=mi % 2):
                    e.copy(qstage[:, m * TT:(m + 1) * TT], pq[:, slot * TT:slot * TT + TT]).then_inc(qe.h, 1)
                p.on("scalar", f); qe.v += 1
            p.wait("gpsimd", qe, qe.v)
            def f(e, tt=tt):
                e.dma_start(out=q_s[:, tt * TT:(tt + 1) * TT].rearrange("(kc pp) t -> pp kc t", pp=128),
                            in_=qstage[:].rearrange("pp (kc t) -> pp kc t", kc=NKC)).then_inc(st.h, 16)
            p.on("gpsimd", f); st.v += 16
        p.wait("gpsimd", st, st.v)
        blk = es.enter_context(nc.Block("p1b"))
        p.emit(blk)


def _p2(nc, q_s, ao_s, k_cache, v_cache, ones_col, ones_row, gath=None):
    partial_D = (-384, -256, -128, 0, 640, 768, 896, 1024)
    with contextlib.ExitStack() as es:
        qt_b = [es.enter_context(nc.sbuf_tensor(f"qt{i}", [128, NH * TT], BF16)) for i in range(2)]
        probs = es.enter_context(nc.sbuf_tensor("probs", [128, 3 * TT], BF16))
        masks = es.enter_context(nc.sbuf_tensor("masks", [128, 8 * TT], F32))
        m_io = es.enter_context(nc.sbuf_tensor("m_io", [128, TT], I32))
        m_t1 = es.enter_context(nc.sbuf_tensor("m_t1", [128, TT], F32))
        m_t2 = es.enter_context(nc.sbuf_tensor("m_t2", [128, TT], F32))
        onehot = es.enter_context(nc.sbuf_tensor("onehot", [128, NH * NH], BF16))
        oh_io = es.enter_context(nc.sbuf_tensor("oh_io", [128, NH], I32))
        ohrow = es.enter_context(nc.sbuf_tensor("ohrow", [16, NH * 128], F32))
        ohr_io = es.enter_context(nc.sbuf_tensor("ohr_io", [16, 128], I32))
        srec = es.enter_context(nc.sbuf_tensor("srec", [NH, TT], F32))
        ao_stage = es.enter_context(nc.sbuf_tensor("ao_stage", [128, NH * TT], BF16))
        pS = es.enter_context(nc.psum_tensor("pS", [128, 2 * TT], F32))
        pAV = es.enter_context(nc.psum_tensor("pAV", [128, 2 * TT], F32))
        pDen = es.enter_context(nc.psum_tensor("pDen", [128, TT], F32))
        pB = es.enter_context(nc.psum_tensor("pB", [128, TT], F32))
        ldq = Sem(es, nc, "a_ldq")
        msk = Sem(es, nc, "a_msk")
        sS = Sem(es, nc, "a_sS")
        sM = Sem(es, nc, "a_sM")
        sP = Sem(es, nc, "a_sP")
        sAV = Sem(es, nc, "a_sAV")
        sAVe = Sem(es, nc, "a_sAVe")
        sDen = Sem(es, nc, "a_sDen")
        sRec = Sem(es, nc, "a_sRec")
        sB = Sem(es, nc, "a_sB")
        sSc = Sem(es, nc, "a_sSc")
        sAOD = Sem(es, nc, "a_sAOD")
        p = Prog()
        if gath is not None:
            gath.start_phase(es, width=2048)

        D_list = sorted(partial_D)
        mask_off = {Dv: i * TT for i, Dv in enumerate(D_list)}
        def f(e):
            e.iota(m_io[:], pattern=[[1, TT]], base=128, channel_multiplier=-1).then_inc(msk.h, 1)
            e.iota(oh_io[:], pattern=[[1, NH]], base=0, channel_multiplier=0).then_inc(msk.h, 1)
            e.iota(ohr_io[:], pattern=[[0, 128]], base=0, channel_multiplier=1).then_inc(msk.h, 1)
        p.on("gpsimd", f); msk.v += 3
        p.wait("vector", msk, msk.v)
        def f(e):
            for i, Dv in enumerate(D_list):
                e.tensor_scalar(out=m_t1[:], in0=m_io[:], scalar1=float(128 - Dv), scalar2=None, op0=ALU.is_ge)
                e.tensor_scalar(out=m_t2[:], in0=m_io[:], scalar1=float(1152 - Dv), scalar2=None, op0=ALU.is_le)
                e.drain()
                e.tensor_tensor(out=m_t1[:], in0=m_t1[:], in1=m_t2[:], op=ALU.mult)
                e.drain()
                e.tensor_scalar(out=masks[:, i * TT:(i + 1) * TT], in0=m_t1[:],
                                scalar1=-1.0, scalar2=-NEG, op0=ALU.add, op1=ALU.mult)
                e.drain()
            for h in range(NH):
                e.tensor_scalar(out=onehot[:, h * NH:(h + 1) * NH], in0=oh_io[:],
                                scalar1=float(h), scalar2=None, op0=ALU.is_equal)
                e.tensor_scalar(out=ohrow[:, h * 128:(h + 1) * 128], in0=ohr_io[:],
                                scalar1=float(h), scalar2=None, op0=ALU.is_equal)
            e.drain()
            e.nop().then_inc(msk.h, 1)
        p.on("vector", f); msk.v += 1

        chunk_i = 0
        for qt in range(NTILE):
            qt0 = qt * TT
            if qt >= 2:
                p.wait("sync", sAOD, 16 * (qt - 1))
            def f(e, qt=qt, buf=qt_b[qt % 2], ldq=ldq):
                e.dma_start(out=buf[:].rearrange("pp (kc t) -> pp kc t", kc=NKC), in_=q_s[:, qt * TT:(qt + 1) * TT].rearrange("(kc pp) t -> pp kc t", pp=128)).then_inc(ldq.h, 16)
            p.on("sync", f); ldq.v += 16
            p.wait("tensor", ldq, 16 * (qt + 1))
            if qt == 0:
                p.wait("tensor", msk, msk.v)
            if qt >= 1:
                p.wait("scalar", sAOD, 16 * qt)
            k0s = list(range(max(0, qt0 - WIN), qt0 + TT - 128 + 1, 128))
            nch = len(k0s)
            for h in range(NH):
                kv = h // (NH // KVH)
                av_slot = h % 2
                if qt * NH + h >= 2:
                    p.wait("tensor", sAVe, sAVe.v - 1)

                def emit_qk(ci, base_ci):
                    k0 = k0s[ci]
                    gci = base_ci + ci
                    slot = gci % 2
                    pslot = gci % 3
                    Dv = qt0 - k0
                    if gci >= 2:
                        p.wait("tensor", sP, gci - 1)
                    def f(e, kv=kv, k0=k0, h=h, buf=qt_b[qt % 2], slot=slot):
                        e.matmul(pS[:, slot * TT:slot * TT + TT],
                                 k_cache[:, kv * S + k0:kv * S + k0 + 128],
                                 buf[:, h * TT:(h + 1) * TT], start=True, stop=True).then_inc(sS.h, 1)
                    p.on("tensor", f); sS.v += 1
                    use_sem, use_val = sS, sS.v
                    if Dv in partial_D:
                        p.wait("vector", sS, sS.v)
                        def f(e, slot=slot, off=mask_off[Dv]):
                            e.tensor_tensor(out=pS[:, slot * TT:slot * TT + TT],
                                            in0=pS[:, slot * TT:slot * TT + TT],
                                            in1=masks[:, off:off + TT], op=ALU.add).then_inc(sM.h, 1)
                        p.on("vector", f); sM.v += 1
                        use_sem, use_val = sM, sM.v
                    p.wait("scalar", use_sem, use_val)
                    if gci >= 3:
                        p.wait("scalar", sAV, gci - 2)
                    def f(e, slot=slot, pslot=pslot):
                        e.activation(out=probs[:, pslot * TT:pslot * TT + TT],
                                     in_=pS[:, slot * TT:slot * TT + TT], func=AF.Exp,
                                     scale=SCALE).then_inc(sP.h, 1)
                    p.on("scalar", f); sP.v += 1

                def emit_av(ci, base_ci):
                    k0 = k0s[ci]
                    gci = base_ci + ci
                    slot = gci % 3
                    p.wait("tensor", sP, gci + 1)
                    is_first = (ci == 0)
                    is_last = (ci == nch - 1)
                    def f(e, slot=slot, kv=kv, k0=k0, h=h, av_slot=av_slot,
                          is_first=is_first, is_last=is_last):
                        i_d = e.matmul(pDen[0:NH, :], onehot[:, h * NH:(h + 1) * NH],
                                       probs[:, slot * TT:slot * TT + TT],
                                       start=(h == 0 and is_first), stop=(h == NH - 1 and is_last))
                        if h == NH - 1 and is_last:
                            i_d.then_inc(sDen.h, 1)
                        e.matmul(pAV[:, av_slot * TT:av_slot * TT + TT],
                                 v_cache[:, kv * S + k0:kv * S + k0 + 128],
                                 probs[:, slot * TT:slot * TT + TT],
                                 start=is_first, stop=is_last).then_inc(sAV.h, 1)
                    p.on("tensor", f); sAV.v += 1
                    if h == NH - 1 and is_last:
                        sDen.v += 1

                base_ci = chunk_i
                emit_qk(0, base_ci)
                for ci in range(nch):
                    if ci + 1 < nch:
                        emit_qk(ci + 1, base_ci)
                    emit_av(ci, base_ci)
                chunk_i += nch
                p.wait("scalar", sAV, sAV.v)
                def f(e, h=h, av_slot=av_slot):
                    e.copy(ao_stage[:, h * TT:(h + 1) * TT],
                           pAV[:, av_slot * TT:av_slot * TT + TT]).then_inc(sAVe.h, 1)
                p.on("scalar", f); sAVe.v += 1
                if gath is not None:
                    gath.emit_slice(p, 2)
                    gath.emit_slice(p, 2)
            p.wait("vector", sDen, sDen.v)
            def f(e):
                e.reciprocal(srec[:], pDen[0:NH, :]).then_inc(sRec.h, 1)
            p.on("vector", f); sRec.v += 1
            p.wait("tensor", sRec, sRec.v)
            for h in range(NH):
                if sB.v >= 1:
                    p.wait("tensor", sSc, sB.v)
                def f(e, h=h):
                    e.matmul(pB[:, :], ohrow[:, h * 128:(h + 1) * 128], srec[:],
                             start=True, stop=True).then_inc(sB.h, 1)
                p.on("tensor", f); sB.v += 1
                p.wait("vector", sB, sB.v)
                p.wait("vector", sAVe, sAVe.v - (NH - 1 - h))
                def f(e, h=h):
                    e.tensor_tensor(out=ao_stage[:, h * TT:(h + 1) * TT],
                                    in0=ao_stage[:, h * TT:(h + 1) * TT],
                                    in1=pB[:, :], op=ALU.mult).then_inc(sSc.h, 1)
                p.on("vector", f); sSc.v += 1
            p.wait("gpsimd", sSc, sSc.v)
            def f(e, qt=qt):
                e.dma_start(out=ao_s[:, qt * TT:(qt + 1) * TT].rearrange("(kc pp) t -> pp kc t", pp=128),
                            in_=ao_stage[:].rearrange("pp (kc t) -> pp kc t", kc=NKC)).then_inc(sAOD.h, 16)
            p.on("gpsimd", f); sAOD.v += 16
            p.wait("vector", sAOD, sAOD.v)
        if gath is not None:
            gath.flush_phase(p)
        p.wait("gpsimd", sAOD, sAOD.v)
        blk = es.enter_context(nc.Block("p2"))
        p.emit(blk)


def _p3(nc, x, wo, ao_s, x1_s, y_s, ident, gath=None):
    with contextlib.ExitStack() as es:
        wob = es.enter_context(nc.sbuf_tensor("wob", [128, NKC * H], BF16))
        wtmp = [es.enter_context(nc.sbuf_tensor(f"owt{i}", [128, H], F32)) for i in range(2)]
        aot = [es.enter_context(nc.sbuf_tensor(f"aot{i}", [128, NKC * TT], BF16)) for i in range(2)]
        obuf = es.enter_context(nc.sbuf_tensor("obuf", [128, NH * TT], BF16))
        oT = es.enter_context(nc.sbuf_tensor("oT", [128, H], BF16))
        xc_b = [es.enter_context(nc.sbuf_tensor(f"p3x{i}", [128, H], F32)) for i in range(2)]
        x1c = [es.enter_context(nc.sbuf_tensor(f"x1c{i}", [128, H], F32)) for i in range(2)]
        ytm = es.enter_context(nc.sbuf_tensor("ytm", [128, H], BF16))
        yfm = es.enter_context(nc.sbuf_tensor("yfm", [128, NKC * TT], BF16))
        xstat = es.enter_context(nc.sbuf_tensor("p3stat", [128, 24], F32))
        xaggr = es.enter_context(nc.sbuf_tensor("p3aggr", [128, 2], F32))
        nmean = es.enter_context(nc.sbuf_tensor("p3nm", [128, 1], F32))
        rstd = es.enter_context(nc.sbuf_tensor("p3rs", [128, 1], F32))
        po = es.enter_context(nc.psum_tensor("po", [128, 2 * TT], F32))
        pto = es.enter_context(nc.psum_tensor("p3pto", [128, 2048], BF16))
        pty = es.enter_context(nc.psum_tensor("p3pty", [128, 2048], BF16))
        ld = Sem(es, nc, "o_ld")
        cst = Sem(es, nc, "o_cst")
        lda = Sem(es, nc, "o_lda")
        ldxc = Sem(es, nc, "o_ldxc")
        mm = Sem(es, nc, "o_mm")
        oev = Sem(es, nc, "o_oev")
        otr = Sem(es, nc, "o_otr")
        ote = Sem(es, nc, "o_ote")
        sx1 = Sem(es, nc, "o_sx1")
        sx1d = Sem(es, nc, "o_sx1d")
        lnq = Sem(es, nc, "o_lnq")
        lnr = Sem(es, nc, "o_lnr")
        syt = Sem(es, nc, "o_syt")
        ytr = Sem(es, nc, "o_ytr")
        yev = Sem(es, nc, "o_yev")
        syd = Sem(es, nc, "o_syd")
        p = Prog()
        NCH = 4

        for kc in range(NKC):
            if kc >= 2:
                p.wait("sync", cst, kc - 1)
            def f(e, kc=kc, buf=wtmp[kc % 2], ld=ld):
                e.dma_start(out=buf[:], in_=wo[kc * 128:(kc + 1) * 128, :]).then_inc(ld.h, 16)
            p.on("sync", f); ld.v += 16
            p.wait("vector", ld, 16 * (kc + 1))
            def f(e, kc=kc, buf=wtmp[kc % 2], cst=cst):
                e.tensor_copy(wob[:, kc * H:(kc + 1) * H], buf[:]).then_inc(cst.h, 1)
            p.on("vector", f); cst.v += 1

        for tt in range(NTILE):
            if tt >= 2:
                p.wait("sync", mm, (tt - 1) * NH)
            def f(e, tt=tt, buf=aot[tt % 2], lda=lda):
                e.dma_start(out=buf[:].rearrange("pp (kc t) -> pp kc t", kc=NKC), in_=ao_s[:, tt * TT:(tt + 1) * TT].rearrange("(kc pp) t -> pp kc t", pp=128)).then_inc(lda.h, 16)
            p.on("sync", f); lda.v += 16
            for c in range(NCH):
                k = tt * NCH + c
                if k >= 2:
                    p.wait("sync", sx1, k - 1)
                def f(e, tt=tt, c=c, buf=xc_b[k % 2], ldxc=ldxc):
                    e.dma_start(out=buf[:], in_=x[tt * TT + c * 128:tt * TT + (c + 1) * 128, :]).then_inc(ldxc.h, 16)
                p.on("sync", f); ldxc.v += 16
            p.wait("tensor", lda, 16 * (tt + 1))
            if tt == 0:
                p.wait("tensor", cst, NKC)
            if tt >= 1:
                p.wait("scalar", otr, tt * NCH * NKC)
            for m in range(NH):
                mi = tt * NH + m
                if mi >= 2:
                    p.wait("tensor", oev, mi - 1)
                for kc in range(NKC):
                    last = (kc == NKC - 1)
                    def f(e, m=m, kc=kc, buf=aot[tt % 2], slot=mi % 2, last=last):
                        i_ = e.matmul(po[:, slot * TT:slot * TT + TT],
                                      wob[:, kc * H + m * 128:kc * H + (m + 1) * 128],
                                      buf[:, kc * TT:(kc + 1) * TT], start=(kc == 0), stop=last)
                        if last:
                            i_.then_inc(mm.h, 1)
                    p.on("tensor", f)
                mm.v += 1
                p.wait("scalar", mm, mm.v)
                def f(e, m=m, slot=mi % 2):
                    e.copy(obuf[:, m * TT:(m + 1) * TT], po[:, slot * TT:slot * TT + TT]).then_inc(oev.h, 1)
                p.on("scalar", f); oev.v += 1
            p.wait("tensor", oev, oev.v)
            for c in range(NCH):
                k = tt * NCH + c
                for mch in range(NKC):
                    ti = k * NKC + mch
                    if ti >= 2:
                        p.wait("tensor", ote, ti - 1)
                    def f(e, c=c, mch=mch, slot=ti % 2):
                        e.transpose(pto[:, slot * 1024:slot * 1024 + 128],
                                    obuf[:, mch * TT + c * 128:mch * TT + c * 128 + 128],
                                    ident[:]).then_inc(otr.h, 1)
                    p.on("tensor", f)
                    p.wait("scalar", otr, ti + 1)
                    def f(e, mch=mch, slot=ti % 2):
                        e.copy(oT[:, mch * 128:(mch + 1) * 128],
                               pto[:, slot * 1024:slot * 1024 + 128]).then_inc(ote.h, 1)
                    p.on("scalar", f); ote.v += 1
                otr.v = (k + 1) * NKC
                p.wait("vector", ote, ote.v)
                p.wait("vector", ldxc, 16 * (k + 1))
                if k >= 2:
                    p.wait("vector", sx1d, 16 * (k - 1))
                def f(e, k=k):
                    e.tensor_tensor(out=x1c[k % 2][:], in0=xc_b[k % 2][:], in1=oT[:], op=ALU.add).then_inc(sx1.h, 1)
                    e.drain()
                    for i in range(4):
                        e.bn_stats(xstat[:, 6 * i:6 * (i + 1)], x1c[k % 2][:, i * 512:(i + 1) * 512])
                    e.drain()
                    e.bn_aggr(xaggr[:], xstat[:])
                    e.drain()
                    e.tensor_scalar(out=nmean[:], in0=xaggr[:, 0:1], scalar1=-1.0, scalar2=None, op0=ALU.mult)
                    e.tensor_scalar(out=rstd[:], in0=xaggr[:, 1:2], scalar1=EPS, scalar2=None, op0=ALU.add).then_inc(lnq.h, 1)
                p.on("vector", f); sx1.v += 1; lnq.v += 1
                p.wait("scalar", sx1, sx1.v)
                p.wait("scalar", lnq, lnq.v)
                def f(e):
                    e.activation(out=rstd[:], in_=rstd[:], func=AF.Ln)
                    e.drain()
                    e.activation(out=rstd[:], in_=rstd[:], func=AF.Exp, scale=-0.5).then_inc(lnr.h, 1)
                p.on("scalar", f); lnr.v += 1
                p.wait("vector", lnr, lnr.v)
                if k >= 1:
                    p.wait("vector", ytr, k * NKC)
                def f(e, k=k):
                    e.tensor_scalar(out=ytm[:], in0=x1c[k % 2][:], scalar1=nmean[:], scalar2=rstd[:],
                                    op0=ALU.add, op1=ALU.mult).then_inc(syt.h, 1)
                p.on("vector", f); syt.v += 1
                p.wait("gpsimd", sx1, sx1.v)
                def f(e, tt=tt, c=c, k=k):
                    e.dma_start(out=x1_s[tt * TT + c * 128:tt * TT + (c + 1) * 128, :], in_=x1c[k % 2][:]).then_inc(sx1d.h, 16)
                p.on("gpsimd", f); sx1d.v += 16
                p.wait("tensor", syt, syt.v)
                for kc in range(NKC):
                    ti = k * NKC + kc
                    if ti >= 2:
                        p.wait("tensor", yev, ti - 1)
                    def f(e, kc=kc, slot=ti % 2):
                        e.transpose(pty[:, slot * 1024:slot * 1024 + 128],
                                    ytm[:, kc * 128:(kc + 1) * 128], ident[:]).then_inc(ytr.h, 1)
                    p.on("tensor", f)
                    p.wait("scalar", ytr, ti + 1)
                    def f(e, kc=kc, c=c, slot=ti % 2):
                        e.copy(yfm[:, kc * TT + c * 128:kc * TT + c * 128 + 128],
                               pty[:, slot * 1024:slot * 1024 + 128]).then_inc(yev.h, 1)
                    p.on("scalar", f); yev.v += 1
                ytr.v = (k + 1) * NKC
            p.wait("gpsimd", yev, yev.v)
            def f(e, tt=tt):
                e.dma_start(out=y_s[:, tt * TT:(tt + 1) * TT].rearrange("(kc pp) t -> pp kc t", pp=128),
                            in_=yfm[:].rearrange("pp (kc t) -> pp kc t", kc=NKC)).then_inc(syd.h, 16)
            p.on("gpsimd", f); syd.v += 16
            p.wait("scalar", syd, syd.v)
        p.wait("gpsimd", syd, syd.v)
        p.wait("gpsimd", sx1d, sx1d.v)
        blk = es.enter_context(nc.Block("p3"))
        p.emit(blk)


DBG_P4 = []
def _p4(nc, y_s, x1_s, wg_s, wu_s, wd_s, out, ident):
    TT4 = 1024
    NT4 = S // TT4
    NG = 1
    n_groups = (NIC + NG - 1) // NG
    with contextlib.ExitStack() as es:
        yfm = es.enter_context(nc.sbuf_tensor("f_yfm", [128, NKC * TT4], BF16))
        hb = es.enter_context(nc.sbuf_tensor("f_h", [128, NIC * TT4], BF16))
        wgu = [es.enter_context(nc.sbuf_tensor(f"f_wgu{i}", [128, 2 * NKC * NG * 128], BF16)) for i in range(2)]
        wdb = [es.enter_context(nc.sbuf_tensor(f"f_wd{i}", [128, INTER], BF16)) for i in range(2)]
        sgl = es.enter_context(nc.sbuf_tensor("f_sgl", [128, 512], F32))
        dstage = es.enter_context(nc.sbuf_tensor("f_dst", [128, 2 * NKC * 512], BF16))
        dT = es.enter_context(nc.sbuf_tensor("f_dT", [128, H], BF16))
        x1b = [es.enter_context(nc.sbuf_tensor("f_x1", [128, H], F32))] * 2
        
        pgu = es.enter_context(nc.psum_tensor("f_pgu", [128, 4 * 512], F32))
        pd = es.enter_context(nc.psum_tensor("f_pd", [128, 2 * 512], F32))
        ptd = es.enter_context(nc.psum_tensor("f_ptd", [128, 2048], BF16))
        ldy = Sem(es, nc, "f_ldy")
        ldw = Sem(es, nc, "f_ldw")
        ldd = Sem(es, nc, "f_ldd")
        ldx1 = Sem(es, nc, "f_ldx1")
        mmg = Sem(es, nc, "f_mmg")
        sil = Sem(es, nc, "f_sil")
        hwr = Sem(es, nc, "f_hwr")
        mmd = Sem(es, nc, "f_mmd")
        dev = Sem(es, nc, "f_dev")
        dtr = Sem(es, nc, "f_dtr")
        dte = Sem(es, nc, "f_dte")
        sou = Sem(es, nc, "f_sou")
        sod = Sem(es, nc, "f_sod")
        p = Prog()

        grp_ics = [list(range(g * NG, min((g + 1) * NG, NIC))) for g in range(n_groups)]
        grp_units = [2 * len(g) for g in grp_ics]
        cumu = [0]
        for g in grp_units:
            cumu.append(cumu[-1] + g)
        units_per_tile = cumu[-1]
        gcum = [0]
        for t4_ in range(NT4):
            for g_ in range(n_groups):
                gcum.append(gcum[-1] + grp_units[g_])

        for t4 in range(NT4):
            if t4 >= 1:
                p.wait("sync", hwr, t4 * units_per_tile)
                p.wait("vector", mmd, t4 * NKC * 2)
            def f(e, t4=t4):
                e.dma_start(out=yfm[:].rearrange("pp (kc t) -> pp kc t", kc=NKC), in_=y_s[:, t4 * TT4:(t4 + 1) * TT4].rearrange("(kc pp) t -> pp kc t", pp=128)).then_inc(ldy.h, 16)
            p.on("sync", f); ldy.v += 16
            for gi in range(n_groups):
                ics = grp_ics[gi]
                n = len(ics)
                gj = t4 * n_groups + gi
                gslot = gj % 2
                if gj >= 2:
                    p.wait("sync", hwr, gcum[gj - 1])
                def f(e, ics=ics, n=n, buf=wgu[gslot]):
                    e.dma_start(out=buf[:, 0:NKC * n * 128].rearrange("pp (kc j) -> pp kc j", kc=NKC),
                                in_=wg_s[:, :, ics[0] * 128:ics[0] * 128 + n * 128].rearrange("kc pp j -> pp kc j")).then_inc(ldw.h, 16)
                    e.dma_start(out=buf[:, NKC * NG * 128:NKC * NG * 128 + NKC * n * 128].rearrange("pp (kc j) -> pp kc j", kc=NKC),
                                in_=wu_s[:, :, ics[0] * 128:ics[0] * 128 + n * 128].rearrange("kc pp j -> pp kc j")).then_inc(ldw.h, 16)
                p.on("sync", f); ldw.v += 32
                p.wait("tensor", ldw, ldw.v)
                if gi == 0:
                    p.wait("tensor", ldy, ldy.v)
                for icl, ic in enumerate(ics):
                    for tc in range(2):
                        ui = t4 * units_per_tile + cumu[gi] + icl * 2 + tc
                        gs = ui % 2
                        if ui >= 2:
                            p.wait("tensor", sil, ui - 1)
                            p.wait("tensor", hwr, ui - 1)
                        for kc in range(NKC):
                            last = (kc == NKC - 1)
                            def f(e, icl=icl, kc=kc, tc=tc, gs=gs, gslot=gslot, last=last, n=n):
                                e.matmul(pgu[:, gs * 512:gs * 512 + 512],
                                         wgu[gslot][:, (kc * n + icl) * 128:(kc * n + icl) * 128 + 128],
                                         yfm[:, kc * TT4 + tc * 512:kc * TT4 + tc * 512 + 512],
                                         start=(kc == 0), stop=last)
                                i_ = e.matmul(pgu[:, (2 + gs) * 512:(2 + gs) * 512 + 512],
                                              wgu[gslot][:, NKC * NG * 128 + (kc * n + icl) * 128:NKC * NG * 128 + (kc * n + icl) * 128 + 128],
                                              yfm[:, kc * TT4 + tc * 512:kc * TT4 + tc * 512 + 512],
                                              start=(kc == 0), stop=last)
                                if last:
                                    i_.then_inc(mmg.h, 1)
                            p.on("tensor", f)
                        mmg.v += 1
                        p.wait("scalar", mmg, mmg.v)
                        if ui >= 1:
                            p.wait("scalar", hwr, ui)
                        def f(e, gs=gs):
                            e.activation(out=sgl[:], in_=pgu[:, gs * 512:gs * 512 + 512], func=AF.Silu).then_inc(sil.h, 1)
                        p.on("scalar", f); sil.v += 1
                        p.wait("vector", sil, sil.v)
                        def f(e, ic=ic, tc=tc, gs=gs):
                            e.tensor_tensor(out=hb[:, ic * TT4 + tc * 512:ic * TT4 + tc * 512 + 512],
                                            in0=sgl[:], in1=pgu[:, (2 + gs) * 512:(2 + gs) * 512 + 512],
                                            op=ALU.mult).then_inc(hwr.h, 1)
                        p.on("vector", f); hwr.v += 1
            for m in range(NKC):
                md = t4 * NKC + m
                if md >= 2:
                    p.wait("sync", mmd, (md - 1) * 2)
                def f(e, m=m, buf=wdb[md % 2]):
                    e.dma_start(out=buf[:], in_=wd_s[m, :, :]).then_inc(ldd.h, 16)
                p.on("sync", f); ldd.v += 16
                p.wait("tensor", ldd, ldd.v)
                if m == 0:
                    p.wait("tensor", hwr, (t4 + 1) * units_per_tile)
                for tc in range(2):
                    di = md * 2 + tc
                    if di >= 2:
                        p.wait("tensor", dev, di - 1)
                    for ic in range(NIC):
                        last = (ic == NIC - 1)
                        def f(e, ic=ic, tc=tc, slot=di % 2, buf=wdb[md % 2], last=last):
                            i_ = e.matmul(pd[:, slot * 512:slot * 512 + 512],
                                          buf[:, ic * 128:(ic + 1) * 128],
                                          hb[:, ic * TT4 + tc * 512:ic * TT4 + tc * 512 + 512],
                                          start=(ic == 0), stop=last)
                            if last:
                                i_.then_inc(mmd.h, 1)
                        p.on("tensor", f)
                    mmd.v += 1
                    p.wait("scalar", mmd, mmd.v)
                    if t4 >= 1 and di < 2:
                        p.wait("scalar", dtr, t4 * 8 * NKC)
                    def f(e, m=m, tc=tc, slot=di % 2):
                        e.copy(dstage[:, tc * NKC * 512 + m * 512:tc * NKC * 512 + m * 512 + 512],
                               pd[:, slot * 512:slot * 512 + 512]).then_inc(dev.h, 1)
                    p.on("scalar", f); dev.v += 1
            p.wait("tensor", dev, dev.v)
            for tc in range(2):
                for c in range(4):
                    k = t4 * 8 + tc * 4 + c
                    if k >= 1:
                        p.wait("sync", sou, k)
                        p.wait("sync", sod, 16 * k)
                    def f(e, t4=t4, tc=tc, c=c, buf=x1b[k % 2]):
                        e.dma_start(out=buf[:], in_=x1_s[t4 * TT4 + tc * 512 + c * 128:t4 * TT4 + tc * 512 + (c + 1) * 128, :]).then_inc(ldx1.h, 16)
                    p.on("sync", f); ldx1.v += 16
                    for m in range(NKC):
                        ti = k * NKC + m
                        if ti >= 2:
                            p.wait("tensor", dte, ti - 1)
                        def f(e, m=m, tc=tc, c=c, slot=ti % 2):
                            e.transpose(ptd[:, slot * 1024:slot * 1024 + 128],
                                        dstage[:, tc * NKC * 512 + m * 512 + c * 128:tc * NKC * 512 + m * 512 + c * 128 + 128],
                                        ident[:]).then_inc(dtr.h, 1)
                        p.on("tensor", f)
                        p.wait("scalar", dtr, ti + 1)
                        def f(e, m=m, slot=ti % 2):
                            e.copy(dT[:, m * 128:(m + 1) * 128],
                                   ptd[:, slot * 1024:slot * 1024 + 128]).then_inc(dte.h, 1)
                        p.on("scalar", f); dte.v += 1
                    dtr.v = (k + 1) * NKC
                    p.wait("vector", dte, dte.v)
                    p.wait("vector", ldx1, 16 * (k + 1))
                    if k >= 1:
                        p.wait("vector", sod, 16 * k)
                    def f(e, k=k):
                        e.tensor_tensor(out=x1b[k % 2][:], in0=x1b[k % 2][:], in1=dT[:], op=ALU.add).then_inc(sou.h, 1)
                    p.on("vector", f); sou.v += 1
                    p.wait("scalar", sou, sou.v)
                    p.wait("gpsimd", sou, sou.v)
                    def f(e, t4=t4, tc=tc, c=c, k=k):
                        e.dma_start(out=out[t4 * TT4 + tc * 512 + c * 128:t4 * TT4 + tc * 512 + (c + 1) * 128, :],
                                    in_=x1b[k % 2][:]).then_inc(sod.h, 16)
                    p.on("gpsimd", f); sod.v += 16
        p.wait("gpsimd", sod, sod.v)
        if DBG_P4:
            hdmp = nc.dram_tensor("dbg_h_last", [128, NIC * TT4], BF16, kind="ExternalOutput").ap()
            ddmp = nc.dram_tensor("dbg_dst_last", [128, 2 * NKC * 512], BF16, kind="ExternalOutput").ap()
            hst = Sem(es, nc, "f_hdmp")
            def f(e):
                e.dma_start(out=hdmp[:, :], in_=hb[:]).then_inc(hst.h, 16)
                e.dma_start(out=ddmp[:, :], in_=dstage[:]).then_inc(hst.h, 16)
                e.wait_ge(hst.h, 32)
            p.on("gpsimd", f)
        blk = es.enter_context(nc.Block("p4"))
        p.emit(blk)


from concourse.bass_utils import run_bass_kernel_spmd

_NC_CACHE = []

def kernel(L0_attn_hidden_states, L0_attn_Wq, L0_attn_Wk, L0_attn_Wv, L0_attn_Wo,
           L0_ffn_lut, L0_ffn_gate_idx, L0_ffn_up_idx, L0_ffn_down_idx):
    x = np.ascontiguousarray(np.asarray(L0_attn_hidden_states, dtype=np.float32))
    wq_a = np.ascontiguousarray(np.asarray(L0_attn_Wq, dtype=np.float32))
    wk_a = np.ascontiguousarray(np.asarray(L0_attn_Wk, dtype=np.float32))
    wv_a = np.ascontiguousarray(np.asarray(L0_attn_Wv, dtype=np.float32))
    wo_a = np.ascontiguousarray(np.asarray(L0_attn_Wo, dtype=np.float32))
    lut_a = np.asarray(L0_ffn_lut, dtype=np.float32)
    lutr = np.ascontiguousarray(np.repeat(lut_a[None, :], 128, axis=0))
    gi = np.ascontiguousarray(np.asarray(L0_ffn_gate_idx, dtype=np.int32))
    ui = np.ascontiguousarray(np.asarray(L0_ffn_up_idx, dtype=np.int32))
    di = np.ascontiguousarray(np.asarray(L0_ffn_down_idx, dtype=np.int32))

    if not _NC_CACHE:
        _NC_CACHE.append(build()[0])
    nc = _NC_CACHE[0]
    in_maps = []
    for core in range(8):
        in_maps.append({
            "x": np.ascontiguousarray(x[core]),
            "wq": wq_a, "wk": wk_a, "wv": wv_a, "wo": wo_a,
            "lutr": lutr, "gidx": gi, "uidx": ui, "didx": di,
        })
    res = run_bass_kernel_spmd(nc, in_maps, core_ids=list(range(8)))
    return np.stack([np.asarray(res.results[i]["out"], dtype=np.float32) for i in range(8)], axis=0)


# version 10
# speedup vs baseline: 1.0972x; 1.0972x over previous
import contextlib
import numpy as np
import concourse.bass as bass
import concourse.mybir as mybir

F32 = mybir.dt.float32
BF16 = mybir.dt.bfloat16
I32 = mybir.dt.int32
AF = mybir.ActivationFunctionType
ALU = mybir.AluOpType

S = 4096
H = 2048
D = 128
NH = 16
KVH = 4
WIN = 1024
INTER = 5504
NIC = INTER // 128
NKC = H // 128
TT = 512
NTILE = S // TT
SCALE = 1.0 / float(np.sqrt(D))
NEG = -1.0e9
EPS = 1e-5

ENGINES = ("sync", "vector", "scalar", "tensor", "gpsimd")


class Sem:
    def __init__(self, es, nc, name):
        self.h = es.enter_context(nc.semaphore(name))
        self.v = 0


class Prog:
    def __init__(self):
        self.ops = {e: [] for e in ENGINES}

    def on(self, eng, fn):
        self.ops[eng].append(fn)

    def wait(self, eng, sem, val):
        if val > 0:
            self.ops[eng].append(lambda e, s=sem.h, v=val: e.wait_ge(s, v))

    def emit(self, blk):
        for eng in ENGINES:
            if not self.ops[eng]:
                continue
            def body(e, lst=tuple(self.ops[eng])):
                for fn in lst:
                    fn(e)
            getattr(blk, eng)(body)


def build(phases=("p0", "p1a", "p1b", "p2", "p3", "p4"), dbg=()):
    nc = bass.Bass("TRN2", target_bir_lowering=False)

    x = nc.dram_tensor("x", [S, H], F32, kind="ExternalInput").ap()
    wq = nc.dram_tensor("wq", [H, H], F32, kind="ExternalInput").ap()
    wk = nc.dram_tensor("wk", [H, KVH * D], F32, kind="ExternalInput").ap()
    wv = nc.dram_tensor("wv", [H, KVH * D], F32, kind="ExternalInput").ap()
    wo = nc.dram_tensor("wo", [H, H], F32, kind="ExternalInput").ap()
    lutr = nc.dram_tensor("lutr", [128, 16], F32, kind="ExternalInput").ap()
    gidx = nc.dram_tensor("gidx", [INTER, H], I32, kind="ExternalInput").ap()
    uidx = nc.dram_tensor("uidx", [INTER, H], I32, kind="ExternalInput").ap()
    didx = nc.dram_tensor("didx", [H, INTER], I32, kind="ExternalInput").ap()
    out = nc.dram_tensor("out", [S, H], F32, kind="ExternalOutput").ap()

    xln_s = nc.dram_tensor("xln_s", [H, S], BF16, kind="Internal").ap()
    q_s = nc.dram_tensor("q_s", [H, S], BF16, kind="Internal").ap()
    ao_s = nc.dram_tensor("ao_s", [H, S], BF16, kind="Internal").ap()
    p4_standalone = ("p4" in phases and "p3" not in phases)
    x1_s = nc.dram_tensor("x1_s", [S, H], F32, kind="ExternalInput" if p4_standalone else "Internal").ap()
    y_s = nc.dram_tensor("y_s", [H, S], BF16, kind="ExternalInput" if p4_standalone else "Internal").ap()
    wg_s = nc.dram_tensor("wg_s", [NKC, 128, INTER], BF16, kind="Internal").ap()
    wu_s = nc.dram_tensor("wu_s", [NKC, 128, INTER], BF16, kind="Internal").ap()
    wd_s = nc.dram_tensor("wd_s", [NKC, 128, INTER], BF16, kind="Internal").ap()

    dbg_aps = {}
    with contextlib.ExitStack() as top:
        ident = top.enter_context(nc.sbuf_tensor("ident", [128, 128], BF16))
        slut = top.enter_context(nc.sbuf_tensor("slut", [128, 16], F32))
        ones_col = top.enter_context(nc.sbuf_tensor("ones_col", [128, 1], BF16))
        ones_row = top.enter_context(nc.sbuf_tensor("ones_row", [1, 128], F32))

        _setup(nc, lutr, slut, ident, ones_col, ones_row)

        gath = None
        if "p0" in phases:
            g_es = top.enter_context(contextlib.ExitStack())
            gath = Gather(nc, g_es, slut, ident, gidx, uidx, didx, wg_s, wu_s, wd_s)
        with contextlib.ExitStack() as kvscope:
            k_cache = kvscope.enter_context(nc.sbuf_tensor("k_cache", [128, KVH * S], BF16))
            v_cache = kvscope.enter_context(nc.sbuf_tensor("v_cache", [128, KVH * S], BF16))
            if "p1a" in phases:
                _p1a(nc, x, wk, wv, xln_s, k_cache, v_cache, ident)
            if "p1b" in phases:
                _p1b(nc, wq, xln_s, q_s)
            if "p2" in phases:
                _p2(nc, q_s, ao_s, k_cache, v_cache, ones_col, ones_row, gath, ident)
            if "kv" in dbg:
                for nm, t in (("k_cache", k_cache), ("v_cache", v_cache)):
                    dst = nc.dram_tensor("dbg_" + nm, [128, KVH * S], BF16, kind="ExternalOutput").ap()
                    dbg_aps[nm] = dst
                    _sbuf_dump(nc, dst, t)
        if "p3" in phases:
            _p3(nc, x, wo, ao_s, x1_s, y_s, ident, gath)
        if gath is not None:
            _p0(nc, gath)
        if "p4" in phases:
            _p4(nc, y_s, x1_s, wg_s, wu_s, wd_s, out, ident)
        else:
            _dump_stub(nc, out)

        for name, src, shp, dt in (
            ("xln_s", xln_s, [H, S], BF16), ("q_s", q_s, [H, S], BF16),
            ("ao_s", ao_s, [H, S], BF16), ("x1_s", x1_s, [S, H], F32),
            ("y_s", y_s, [H, S], BF16),
            ("wg_s", wg_s, [NKC, 128, INTER], BF16),
            ("wu_s", wu_s, [NKC, 128, INTER], BF16),
            ("wd_s", wd_s, [NKC, 128, INTER], BF16),
        ):
            if name in dbg:
                dst = nc.dram_tensor("dbg_" + name, shp, dt, kind="ExternalOutput").ap()
                dbg_aps[name] = dst
                _dram_copy(nc, dst, src, shp, dt)
    return nc, dbg_aps


def _setup(nc, lutr, slut, ident, ones_col, ones_row):
    with contextlib.ExitStack() as es:
        sidi = es.enter_context(nc.sbuf_tensor("sidi", [128, 128], I32))
        ld = Sem(es, nc, "su_ld")
        dn = Sem(es, nc, "su_dn")
        p = Prog()
        def f(e):
            e.dma_start(out=slut[:], in_=lutr[:, :]).then_inc(ld.h, 16)
        p.on("sync", f); ld.v += 16
        def f(e):
            e.iota(sidi[:], pattern=[[-1, 128]], base=0, channel_multiplier=1).then_inc(dn.h, 1)
        p.on("gpsimd", f); dn.v += 1
        p.wait("vector", dn, dn.v)
        p.wait("vector", ld, ld.v)
        def f(e):
            e.tensor_scalar(out=ident[:], in0=sidi[:], scalar1=0, scalar2=None, op0=ALU.is_equal)
            e.memset(ones_col[:], 1.0)
            e.memset(ones_row[:], 1.0)
        p.on("vector", f)
        blk = es.enter_context(nc.Block("setup"))
        p.emit(blk)


def _dump_stub(nc, out):
    with contextlib.ExitStack() as es:
        z = es.enter_context(nc.sbuf_tensor("zz", [128, H], F32))
        st = Sem(es, nc, "z_st")
        dn = Sem(es, nc, "z_dn")
        p = Prog()
        def f(e):
            e.memset(z[:], 0.0).then_inc(dn.h, 1)
        p.on("vector", f); dn.v += 1
        p.wait("gpsimd", dn, dn.v)
        def f(e):
            for i in range(S // 128):
                e.dma_start(out=out[i * 128:(i + 1) * 128, :], in_=z[:]).then_inc(st.h, 16)
            e.wait_ge(st.h, 16 * (S // 128))
        p.on("gpsimd", f); st.v += 16 * (S // 128)
        blk = es.enter_context(nc.Block("zout"))
        p.emit(blk)


_DP_N = [0]
def _sbuf_dump(nc, dst, src):
    _DP_N[0] += 1
    uid = _DP_N[0]
    with contextlib.ExitStack() as es:
        st = Sem(es, nc, f"dmp_st{uid}")
        p = Prog()
        def f(e):
            e.dma_start(out=dst[:, :], in_=src[:]).then_inc(st.h, 16)
            e.wait_ge(st.h, 16)
        p.on("gpsimd", f)
        blk = es.enter_context(nc.Block(f"dmp{uid}"))
        p.emit(blk)


_CP_N = [0]
def _dram_copy(nc, dst, src, shp, dt):
    total = int(np.prod(shp))
    FREE = 8192
    _CP_N[0] += 1
    uid = _CP_N[0]
    with contextlib.ExitStack() as es:
        buf = es.enter_context(nc.sbuf_tensor(f"cp_buf{uid}", [128, FREE], dt))
        ld = Sem(es, nc, f"cp_ld{uid}")
        st = Sem(es, nc, f"cp_st{uid}")
        p = Prog()
        if len(shp) == 2:
            fsrc = src.rearrange("a b -> (a b)")
            fdst = dst.rearrange("a b -> (a b)")
        else:
            fsrc = src.rearrange("a b c -> (a b c)")
            fdst = dst.rearrange("a b c -> (a b c)")
        chunk = 128 * FREE
        nch = (total + chunk - 1) // chunk
        def f(e):
            for i in range(nch):
                n = min(chunk, total - i * chunk)
                rows = n // FREE
                e.wait_ge(st.h, 16 * i)
                e.dma_start(out=buf[0:rows, :], in_=fsrc[i * chunk:i * chunk + n].rearrange("(pp ff) -> pp ff", ff=FREE)).then_inc(ld.h, 16)
                e.wait_ge(ld.h, 16 * (i + 1))
                e.dma_start(out=fdst[i * chunk:i * chunk + n].rearrange("(pp ff) -> pp ff", ff=FREE), in_=buf[0:rows, :]).then_inc(st.h, 16)
            e.wait_ge(st.h, 16 * nch)
        p.on("gpsimd", f)
        blk = es.enter_context(nc.Block(f"cp{uid}"))
        p.emit(blk)


class Gather:
    def __init__(self, nc, es, slut, ident, gidx, uidx, didx, wg_s, wu_s, wd_s):
        self.nc = nc
        self.slut = slut
        self.ident = ident
        units = []
        for b_ in range(NIC):
            units.append((gidx[b_ * 128:(b_ + 1) * 128, :], wg_s, b_, 0, H, "gu"))
        for b_ in range(NIC):
            units.append((uidx[b_ * 128:(b_ + 1) * 128, :], wu_s, b_, 0, H, "gu"))
        for b_ in range(NKC):
            for c0, w in ((0, 2048), (2048, 2048), (4096, 1408)):
                units.append((didx[b_ * 128:(b_ + 1) * 128, c0:c0 + w], wd_s, b_, c0, w, "d"))
        self.units = units
        self.cursor = 0
        self.ld = Sem(es, nc, "G_ld")
        self.gat = Sem(es, nc, "G_gat")
        self.trn = Sem(es, nc, "G_trn")
        self.evc = Sem(es, nc, "G_evc")
        self.stg = Sem(es, nc, "G_stg")
        self.npc_hist = []
        self.phase_first = 0
        self._t = None

    def start_phase(self, es, width=2048):
        nc = self.nc
        W = width
        self.bw = W
        self.sidx = [es.enter_context(nc.sbuf_tensor(f"G_idx{i}_{self.cursor}", [128, W], I32)) for i in range(2)]
        self.sidxb = [es.enter_context(nc.sbuf_tensor(f"G_idxb{i}_{self.cursor}", [128, W], BF16)) for i in range(2)]
        self.smask = [es.enter_context(nc.sbuf_tensor(f"G_msk{i}_{self.cursor}", [128, W // 2], BF16)) for i in range(2)]
        self.smlut = [es.enter_context(nc.sbuf_tensor(f"G_mlt{i}_{self.cursor}", [128, W // 2], BF16)) for i in range(2)]
        self.sacc = [es.enter_context(nc.sbuf_tensor(f"G_acc{i}_{self.cursor}", [128, W], BF16)) for i in range(2)]
        self.sstage = [es.enter_context(nc.sbuf_tensor(f"G_stg{i}_{self.cursor}", [128, W], BF16)) for i in range(2)]
        self.pt = es.enter_context(nc.psum_tensor(f"G_pt_{self.cursor}", [128, 2048], BF16))
        self.phase_first = self.cursor

    def remaining(self):
        return len(self.units) - self.cursor

    def emit_units(self, p, n):
        for _ in range(min(n, self.remaining())):
            self._unit_head(p)
            while self._t < 16:
                self._emit_slice(p, 16)
            self._unit_tail(p)

    def emit_slice(self, p, nt=4):
        if self.cursor >= len(self.units):
            return False
        if self._t is None:
            self._unit_head(p)
        self._emit_slice(p, nt)
        if self._t >= 16:
            self._unit_tail(p)
        return True

    def flush_phase(self, p):
        if self._t is not None:
            while self._t < 16:
                self._emit_slice(p, 16)
            self._unit_tail(p)

    def _unit_head(self, p):
        u = self.cursor
        src_ap, dst, b_, c0, fw, kind = self.units[u]
        assert fw <= self.bw
        pl = u - self.phase_first
        slot = pl % 2
        if pl >= 2:
            p.wait("sync", self.gat, u - 1)
        def f(e, src=src_ap, buf=self.sidx[slot], fw=fw, ld=self.ld):
            e.dma_start(out=buf[:, 0:fw], in_=src).then_inc(ld.h, 16)
        p.on("sync", f); self.ld.v += 16
        p.wait("vector", self.ld, self.ld.v)
        if pl >= 2:
            p.wait("vector", self.trn, sum(self.npc_hist) - self.npc_hist[-1])
        self._t = 0

    def _emit_slice(self, p, nt):
        u = self.cursor
        src_ap, dst, b_, c0, fw, kind = self.units[u]
        hw = fw // 2
        pl = u - self.phase_first
        slot = pl % 2
        t0, t1 = self._t, min(16, self._t + nt)
        slut = self.slut
        def f(e, idx=self.sidx[slot], idxb=self.sidxb[slot], acc=self.sacc[slot],
              hw=hw, msk=self.smask, mlt=self.smlut, gat=self.gat, t0=t0, t1=t1):
            for t in range(t0, t1):
                if t == 0:
                    for hh in range(2):
                        e.tensor_scalar(out=idxb[:, hh * hw:hh * hw + hw], in0=idx[:, hh * hw:hh * hw + hw],
                                        scalar1=0.0, scalar2=None, op0=ALU.add)
                for hh in range(2):
                    e.tensor_scalar(out=msk[hh][:, 0:hw], in0=idxb[:, hh * hw:hh * hw + hw],
                                    scalar1=float(t), scalar2=None, op0=ALU.is_equal)
                for hh in range(2):
                    e.tensor_scalar(out=mlt[hh][:, 0:hw], in0=msk[hh][:, 0:hw],
                                    scalar1=slut[:, t:t + 1], scalar2=None, op0=ALU.mult)
                for hh in range(2):
                    if t == 0:
                        e.tensor_copy(acc[:, hh * hw:hh * hw + hw], mlt[hh][:, 0:hw])
                    else:
                        i_ = e.tensor_tensor(out=acc[:, hh * hw:hh * hw + hw],
                                             in0=acc[:, hh * hw:hh * hw + hw],
                                             in1=mlt[hh][:, 0:hw], op=ALU.add)
                        if t == 15 and hh == 1:
                            i_.then_inc(gat.h, 1)
        p.on("vector", f)
        if t1 == 16:
            self.gat.v += 1
        self._t = t1

    def _unit_tail(self, p):
        u = self.cursor
        src_ap, dst, b_, c0, fw, kind = self.units[u]
        npc = fw // 128
        pl = u - self.phase_first
        slot = pl % 2
        prior_npc = sum(self.npc_hist)
        ident = self.ident
        p.wait("tensor", self.gat, self.gat.v)
        for pc in range(npc):
            ti = prior_npc + pc
            if ti >= 2:
                p.wait("tensor", self.evc, ti - 1)
            def f(e, acc=self.sacc[slot], pc=pc, pslot=ti % 2, pt=self.pt):
                e.transpose(pt[:, pslot * 1024:pslot * 1024 + 128],
                            acc[:, pc * 128:pc * 128 + 128], ident[:]).then_inc(self.trn.h, 1)
            p.on("tensor", f)
        self.trn.v = prior_npc + npc
        if pl >= 2:
            p.wait("scalar", self.stg, 16 * (u - 1))
        for pc in range(npc):
            ti = prior_npc + pc
            p.wait("scalar", self.trn, ti + 1)
            def f(e, stage=self.sstage[slot], pc=pc, pslot=ti % 2, pt=self.pt):
                e.copy(stage[:, pc * 128:pc * 128 + 128],
                       pt[:, pslot * 1024:pslot * 1024 + 128]).then_inc(self.evc.h, 1)
            p.on("scalar", f)
        self.evc.v = prior_npc + npc
        if kind == "gu":
            def f(e, dst=dst, b_=b_, stage=self.sstage[slot]):
                e.dma_start(out=dst[:, :, b_ * 128:(b_ + 1) * 128].rearrange("kc pp j -> pp kc j"),
                            in_=stage[:, 0:H].rearrange("pp (kc j) -> pp kc j", kc=NKC)).then_inc(self.stg.h, 16)
        else:
            def f(e, dst=dst, b_=b_, c0=c0, fw=fw, stage=self.sstage[slot]):
                e.dma_start(out=dst[b_, :, c0:c0 + fw], in_=stage[:, 0:fw]).then_inc(self.stg.h, 16)
        p.on("scalar", f); self.stg.v += 16
        self.npc_hist.append(npc)
        self.cursor += 1
        self._t = None

    def finish_wait(self, p, eng="scalar"):
        p.wait(eng, self.stg, self.stg.v)


def _p0(nc, g, n_units=None):
    if g.remaining() == 0:
        return
    n = g.remaining() if n_units is None else min(n_units, g.remaining())
    with contextlib.ExitStack() as es:
        g.start_phase(es, width=2048)
        p = Prog()
        g.emit_units(p, n)
        if g.remaining() == 0:
            g.finish_wait(p)
        blk = es.enter_context(nc.Block("p0"))
        p.emit(blk)


def _p1a(nc, x, wk, wv, xln_s, k_cache, v_cache, ident):
    with contextlib.ExitStack() as es:
        wkb = es.enter_context(nc.sbuf_tensor("wkb", [128, NKC * KVH * D], BF16))
        wvb = es.enter_context(nc.sbuf_tensor("wvb", [128, NKC * KVH * D], BF16))
        wtmp = [es.enter_context(nc.sbuf_tensor(f"wtmp{i}", [128, KVH * D], F32)) for i in range(2)]
        xbuf = [es.enter_context(nc.sbuf_tensor(f"xbuf{i}", [128, 4 * H], F32)) for i in range(2)]
        xstat = es.enter_context(nc.sbuf_tensor("xstat", [128, 24], F32))
        xaggr = es.enter_context(nc.sbuf_tensor("xaggr", [128, 2], F32))
        nmean = es.enter_context(nc.sbuf_tensor("nmean", [128, 1], F32))
        rstd = es.enter_context(nc.sbuf_tensor("rstd", [128, 1], F32))
        xln_tm = es.enter_context(nc.sbuf_tensor("xln_tm", [128, 2 * H], BF16))
        xlnfm = es.enter_context(nc.sbuf_tensor("xlnfm", [128, NKC * TT], BF16))
        vtmp = es.enter_context(nc.sbuf_tensor("vtmp", [128, TT], BF16))
        ptx = es.enter_context(nc.psum_tensor("p1_ptx", [128, 2048], BF16))
        ptv = es.enter_context(nc.psum_tensor("p1_ptv", [128, 2048], BF16))
        pkv = es.enter_context(nc.psum_tensor("p1_pkv", [128, 2 * TT], F32))
        ld = Sem(es, nc, "p1_ld")
        ldx = Sem(es, nc, "p1_ldx")
        cst = Sem(es, nc, "p1_cst")
        lnq = Sem(es, nc, "p1_lnq")
        lnr = Sem(es, nc, "p1_lnr")
        ntm = Sem(es, nc, "p1_ntm")
        trn = Sem(es, nc, "p1_trn")
        evc = Sem(es, nc, "p1_evc")
        mmk = Sem(es, nc, "p1_mmk")
        kve = Sem(es, nc, "p1_kve")
        vtr = Sem(es, nc, "p1_vtr")
        vev = Sem(es, nc, "p1_vev")
        sx = Sem(es, nc, "p1_sx")
        p = Prog()
        NCH = 4

        for w_i, (wsrc, wdst) in enumerate(((wk, wkb), (wv, wvb))):
            for kc in range(NKC):
                j = w_i * NKC + kc
                if j >= 2:
                    p.wait("sync", cst, j - 1)
                def f(e, wsrc=wsrc, kc=kc, buf=wtmp[j % 2], ld=ld):
                    e.dma_start(out=buf[:], in_=wsrc[kc * 128:(kc + 1) * 128, :]).then_inc(ld.h, 16)
                p.on("sync", f); ld.v += 16
                p.wait("vector", ld, 16 * (j + 1))
                def f(e, wdst=wdst, kc=kc, buf=wtmp[j % 2], cst=cst):
                    e.tensor_copy(wdst[:, kc * (KVH * D):(kc + 1) * (KVH * D)], buf[:]).then_inc(cst.h, 1)
                p.on("vector", f); cst.v += 1

        for tt in range(NTILE):
            if tt >= 2:
                p.wait("sync", ntm, (tt - 1) * NCH)
            def f(e, tt=tt, buf=xbuf[tt % 2], ldx=ldx):
                e.dma_start(out=buf[:].rearrange("pp (c hh) -> pp c hh", c=4), in_=x[tt * TT:(tt + 1) * TT, :].rearrange("(c pp) hh -> pp c hh", pp=128)).then_inc(ldx.h, 16)
            p.on("sync", f); ldx.v += 16
            p.wait("vector", ldx, 16 * (tt + 1))
            for c in range(NCH):
                k = tt * NCH + c
                def f(e, xc_buf=xbuf[tt % 2], c=c):
                    xc = xc_buf[:, c * H:(c + 1) * H]
                    for i in range(4):
                        e.bn_stats(xstat[:, 6 * i:6 * (i + 1)], xc[:, i * 512:(i + 1) * 512])
                    e.drain()
                    e.bn_aggr(xaggr[:], xstat[:])
                    e.drain()
                    e.tensor_scalar(out=nmean[:], in0=xaggr[:, 0:1], scalar1=-1.0, scalar2=None, op0=ALU.mult)
                    e.tensor_scalar(out=rstd[:], in0=xaggr[:, 1:2], scalar1=EPS, scalar2=None, op0=ALU.add).then_inc(lnq.h, 1)
                p.on("vector", f); lnq.v += 1
                p.wait("scalar", lnq, lnq.v)
                def f(e):
                    e.activation(out=rstd[:], in_=rstd[:], func=AF.Ln)
                    e.drain()
                    e.activation(out=rstd[:], in_=rstd[:], func=AF.Exp, scale=-0.5).then_inc(lnr.h, 1)
                p.on("scalar", f); lnr.v += 1
                p.wait("vector", lnr, lnr.v)
                if k >= 2:
                    p.wait("vector", trn, (k - 1) * NKC)
                def f(e, xc_buf=xbuf[tt % 2], c=c):
                    xc = xc_buf[:, c * H:(c + 1) * H]
                    e.tensor_scalar(out=xln_tm[:, (c % 2) * H:(c % 2) * H + H], in0=xc,
                                    scalar1=nmean[:], scalar2=rstd[:], op0=ALU.add, op1=ALU.mult).then_inc(ntm.h, 1)
                p.on("vector", f); ntm.v += 1
                p.wait("tensor", ntm, ntm.v)
                for kc in range(NKC):
                    ti = k * NKC + kc
                    if ti >= 2:
                        p.wait("tensor", evc, ti - 1)
                    def f(e, c=c, kc=kc, slot=ti % 2):
                        e.transpose(ptx[:, slot * 1024:slot * 1024 + 128],
                                    xln_tm[:, (c % 2) * H + kc * 128:(c % 2) * H + (kc + 1) * 128],
                                    ident[:]).then_inc(trn.h, 1)
                    p.on("tensor", f)
                trn.v = (k + 1) * NKC
                for kc in range(NKC):
                    ti = k * NKC + kc
                    p.wait("scalar", trn, ti + 1)
                    def f(e, c=c, kc=kc, slot=ti % 2):
                        e.copy(xlnfm[:, kc * TT + c * 128:kc * TT + c * 128 + 128],
                               ptx[:, slot * 1024:slot * 1024 + 128]).then_inc(evc.h, 1)
                    p.on("scalar", f)
                evc.v = (k + 1) * NKC
            p.wait("tensor", evc, evc.v)
            if tt == 0:
                p.wait("tensor", cst, 2 * NKC)
            for w_i, wsb in enumerate((wkb, wvb)):
                for kvh in range(KVH):
                    mi = tt * 2 * KVH + w_i * KVH + kvh
                    if mi >= 2:
                        p.wait("tensor", kve, mi - 1)
                    for kc in range(NKC):
                        last = (kc == NKC - 1)
                        def f(e, wsb=wsb, kvh=kvh, kc=kc, slot=mi % 2, last=last):
                            i_ = e.matmul(pkv[:, slot * TT:slot * TT + TT],
                                          wsb[:, kc * (KVH * D) + kvh * D:kc * (KVH * D) + (kvh + 1) * D],
                                          xlnfm[:, kc * TT:(kc + 1) * TT], start=(kc == 0), stop=last)
                            if last:
                                i_.then_inc(mmk.h, 1)
                        p.on("tensor", f)
                    mmk.v += 1
                    p.wait("scalar", mmk, mmk.v)
                    if w_i == 0:
                        def f(e, kvh=kvh, tt=tt, slot=mi % 2):
                            e.copy(k_cache[:, kvh * S + tt * TT:kvh * S + (tt + 1) * TT],
                                   pkv[:, slot * TT:slot * TT + TT]).then_inc(kve.h, 1)
                        p.on("scalar", f); kve.v += 1
                    else:
                        def f(e, slot=mi % 2):
                            e.copy(vtmp[:], pkv[:, slot * TT:slot * TT + TT]).then_inc(kve.h, 1)
                        p.on("scalar", f); kve.v += 1
                        p.wait("tensor", kve, kve.v)
                        for c in range(NCH):
                            vi = vtr.v
                            if vi >= 2:
                                p.wait("tensor", vev, vi - 1)
                            def f(e, c=c, slot=vi % 2):
                                e.transpose(ptv[:, slot * 1024:slot * 1024 + 128],
                                            vtmp[:, c * 128:(c + 1) * 128], ident[:]).then_inc(vtr.h, 1)
                            p.on("tensor", f); vtr.v += 1
                            p.wait("scalar", vtr, vtr.v)
                            def f(e, kvh=kvh, tt=tt, c=c, slot=vi % 2):
                                e.copy(v_cache[:, kvh * S + tt * TT + c * 128:kvh * S + tt * TT + (c + 1) * 128],
                                       ptv[:, slot * 1024:slot * 1024 + 128]).then_inc(vev.h, 1)
                            p.on("scalar", f); vev.v += 1
                        p.wait("scalar", vtr, vtr.v)
            p.wait("gpsimd", evc, evc.v)
            def f(e, tt=tt):
                e.dma_start(out=xln_s[:, tt * TT:(tt + 1) * TT].rearrange("(kc pp) t -> pp kc t", pp=128),
                            in_=xlnfm[:].rearrange("pp (kc t) -> pp kc t", kc=NKC)).then_inc(sx.h, 16)
            p.on("gpsimd", f); sx.v += 16
            p.wait("scalar", sx, sx.v)
            p.wait("scalar", mmk, mmk.v)
        p.wait("gpsimd", sx, sx.v)
        blk = es.enter_context(nc.Block("p1a"))
        p.emit(blk)


def _p1b(nc, wq, xln_s, q_s):
    with contextlib.ExitStack() as es:
        wqb = es.enter_context(nc.sbuf_tensor("wqb", [128, NKC * H], BF16))
        wtmp = [es.enter_context(nc.sbuf_tensor(f"qwt{i}", [128, H], F32)) for i in range(2)]
        xt = [es.enter_context(nc.sbuf_tensor(f"qxt{i}", [128, NKC * TT], BF16)) for i in range(2)]
        qstage = es.enter_context(nc.sbuf_tensor("qstage", [128, NH * TT], BF16))
        pq = es.enter_context(nc.psum_tensor("pq", [128, 2 * TT], F32))
        ld = Sem(es, nc, "q_ld")
        cst = Sem(es, nc, "q_cst")
        ldx = Sem(es, nc, "q_ldx")
        mm = Sem(es, nc, "q_mm")
        qe = Sem(es, nc, "q_qe")
        st = Sem(es, nc, "q_st")
        p = Prog()
        for kc in range(NKC):
            if kc >= 2:
                p.wait("sync", cst, kc - 1)
            def f(e, kc=kc, buf=wtmp[kc % 2], ld=ld):
                e.dma_start(out=buf[:], in_=wq[kc * 128:(kc + 1) * 128, :]).then_inc(ld.h, 16)
            p.on("sync", f); ld.v += 16
            p.wait("vector", ld, 16 * (kc + 1))
            def f(e, kc=kc, buf=wtmp[kc % 2], cst=cst):
                e.tensor_copy(wqb[:, kc * H:(kc + 1) * H], buf[:]).then_inc(cst.h, 1)
            p.on("vector", f); cst.v += 1

        for tt in range(NTILE):
            if tt >= 2:
                p.wait("sync", mm, (tt - 1) * NH)
            def f(e, tt=tt, buf=xt[tt % 2], ldx=ldx):
                e.dma_start(out=buf[:].rearrange("pp (kc t) -> pp kc t", kc=NKC), in_=xln_s[:, tt * TT:(tt + 1) * TT].rearrange("(kc pp) t -> pp kc t", pp=128)).then_inc(ldx.h, 16)
            p.on("sync", f); ldx.v += 16
            p.wait("tensor", ldx, 16 * (tt + 1))
            if tt == 0:
                p.wait("tensor", cst, NKC)
            if tt >= 1:
                p.wait("scalar", st, 16 * tt)
            for m in range(NH):
                mi = tt * NH + m
                if mi >= 2:
                    p.wait("tensor", qe, mi - 1)
                for kc in range(NKC):
                    last = (kc == NKC - 1)
                    def f(e, m=m, kc=kc, buf=xt[tt % 2], slot=mi % 2, last=last):
                        i_ = e.matmul(pq[:, slot * TT:slot * TT + TT],
                                      wqb[:, kc * H + m * 128:kc * H + (m + 1) * 128],
                                      buf[:, kc * TT:(kc + 1) * TT], start=(kc == 0), stop=last)
                        if last:
                            i_.then_inc(mm.h, 1)
                    p.on("tensor", f)
                mm.v += 1
                p.wait("scalar", mm, mm.v)
                def f(e, m=m, slot=mi % 2):
                    e.copy(qstage[:, m * TT:(m + 1) * TT], pq[:, slot * TT:slot * TT + TT]).then_inc(qe.h, 1)
                p.on("scalar", f); qe.v += 1
            p.wait("gpsimd", qe, qe.v)
            def f(e, tt=tt):
                e.dma_start(out=q_s[:, tt * TT:(tt + 1) * TT].rearrange("(kc pp) t -> pp kc t", pp=128),
                            in_=qstage[:].rearrange("pp (kc t) -> pp kc t", kc=NKC)).then_inc(st.h, 16)
            p.on("gpsimd", f); st.v += 16
        p.wait("gpsimd", st, st.v)
        blk = es.enter_context(nc.Block("p1b"))
        p.emit(blk)


def _p2(nc, q_s, ao_s, k_cache, v_cache, ones_col, ones_row, gath=None, ident=None):
    partial_D = (-384, -256, -128, 0, 640, 768, 896, 1024)
    with contextlib.ExitStack() as es:
        qt_b = [es.enter_context(nc.sbuf_tensor(f"qt{i}", [128, NH * TT], BF16)) for i in range(2)]
        probs = es.enter_context(nc.sbuf_tensor("probs", [128, 3 * TT], BF16))
        masks = es.enter_context(nc.sbuf_tensor("masks", [128, 8 * TT], BF16))
        m_io = es.enter_context(nc.sbuf_tensor("m_io", [128, TT], I32))
        m_t1 = es.enter_context(nc.sbuf_tensor("m_t1", [128, TT], F32))
        m_t2 = es.enter_context(nc.sbuf_tensor("m_t2", [128, TT], F32))
        onehot = es.enter_context(nc.sbuf_tensor("onehot", [128, NH * NH], BF16))
        oh_io = es.enter_context(nc.sbuf_tensor("oh_io", [128, NH], I32))
        ohrow = es.enter_context(nc.sbuf_tensor("ohrow", [16, NH * 128], F32))
        ohr_io = es.enter_context(nc.sbuf_tensor("ohr_io", [16, 128], I32))
        srec = es.enter_context(nc.sbuf_tensor("srec", [NH, TT], F32))
        ao_stage = es.enter_context(nc.sbuf_tensor("ao_stage", [128, NH * TT], BF16))
        pS = es.enter_context(nc.psum_tensor("pS", [128, 3 * TT], F32))
        pAV = es.enter_context(nc.psum_tensor("pAV", [128, TT], F32))
        pDen = es.enter_context(nc.psum_tensor("pDen", [128, TT], F32))
        pB = es.enter_context(nc.psum_tensor("pB", [128, TT], F32))
        ldq = Sem(es, nc, "a_ldq")
        msk = Sem(es, nc, "a_msk")
        sS = Sem(es, nc, "a_sS")
        sM = Sem(es, nc, "a_sM")
        sP = Sem(es, nc, "a_sP")
        sAV = Sem(es, nc, "a_sAV")
        sAVe = Sem(es, nc, "a_sAVe")
        sDen = Sem(es, nc, "a_sDen")
        sRec = Sem(es, nc, "a_sRec")
        sB = Sem(es, nc, "a_sB")
        sSc = Sem(es, nc, "a_sSc")
        sAOD = Sem(es, nc, "a_sAOD")
        p = Prog()
        if gath is not None:
            gath.start_phase(es, width=2048)

        D_list = sorted(partial_D)
        mask_off = {Dv: i * TT for i, Dv in enumerate(D_list)}
        def f(e):
            e.iota(m_io[:], pattern=[[1, TT]], base=128, channel_multiplier=-1).then_inc(msk.h, 1)
            e.iota(oh_io[:], pattern=[[1, NH]], base=0, channel_multiplier=0).then_inc(msk.h, 1)
            e.iota(ohr_io[:], pattern=[[0, 128]], base=0, channel_multiplier=1).then_inc(msk.h, 1)
        p.on("gpsimd", f); msk.v += 3
        p.wait("vector", msk, msk.v)
        def f(e):
            for i, Dv in enumerate(D_list):
                e.tensor_scalar(out=m_t1[:], in0=m_io[:], scalar1=float(128 - Dv), scalar2=None, op0=ALU.is_ge)
                e.tensor_scalar(out=m_t2[:], in0=m_io[:], scalar1=float(1152 - Dv), scalar2=None, op0=ALU.is_le)
                e.drain()
                e.tensor_tensor(out=m_t1[:], in0=m_t1[:], in1=m_t2[:], op=ALU.mult)
                e.drain()
                e.tensor_scalar(out=masks[:, i * TT:(i + 1) * TT], in0=m_t1[:],
                                scalar1=-1.0, scalar2=-NEG, op0=ALU.add, op1=ALU.mult)
                e.drain()
            for h in range(NH):
                e.tensor_scalar(out=onehot[:, h * NH:(h + 1) * NH], in0=oh_io[:],
                                scalar1=float(h), scalar2=None, op0=ALU.is_equal)
                e.tensor_scalar(out=ohrow[:, h * 128:(h + 1) * 128], in0=ohr_io[:],
                                scalar1=float(h), scalar2=None, op0=ALU.is_equal)
            e.drain()
            e.nop().then_inc(msk.h, 1)
        p.on("vector", f); msk.v += 1

        chunk_i = 0
        for qt in range(NTILE):
            qt0 = qt * TT
            if qt >= 2:
                p.wait("sync", sAOD, 16 * (qt - 1))
            def f(e, qt=qt, buf=qt_b[qt % 2], ldq=ldq):
                e.dma_start(out=buf[:].rearrange("pp (kc t) -> pp kc t", kc=NKC), in_=q_s[:, qt * TT:(qt + 1) * TT].rearrange("(kc pp) t -> pp kc t", pp=128)).then_inc(ldq.h, 16)
            p.on("sync", f); ldq.v += 16
            p.wait("tensor", ldq, 16 * (qt + 1))
            if qt == 0:
                p.wait("tensor", msk, msk.v)
            if qt >= 1:
                p.wait("scalar", sAOD, 16 * qt)
            k0s = list(range(max(0, qt0 - WIN), qt0 + TT - 128 + 1, 128))
            nch = len(k0s)
            for h in range(NH):
                kv = h // (NH // KVH)

                def emit_qk(ci, base_ci):
                    k0 = k0s[ci]
                    gci = base_ci + ci
                    slot = gci % 3
                    Dv = qt0 - k0
                    masked = Dv in partial_D
                    if gci >= 3:
                        p.wait("tensor", sP, gci - 2)
                    def f(e, kv=kv, k0=k0, h=h, buf=qt_b[qt % 2], slot=slot, masked=masked,
                          moff=(mask_off[Dv] if masked else 0)):
                        i_ = e.matmul(pS[:, slot * TT:slot * TT + TT],
                                      k_cache[:, kv * S + k0:kv * S + k0 + 128],
                                      buf[:, h * TT:(h + 1) * TT], start=True, stop=not masked)
                        if masked:
                            i_ = e.matmul(pS[:, slot * TT:slot * TT + TT], ident[:],
                                          masks[:, moff:moff + TT], start=False, stop=True)
                        i_.then_inc(sS.h, 1)
                    p.on("tensor", f); sS.v += 1
                    p.wait("scalar", sS, sS.v)
                    if gci >= 3:
                        p.wait("scalar", sAV, gci - 2)
                    def f(e, slot=slot):
                        e.activation(out=probs[:, slot * TT:slot * TT + TT],
                                     in_=pS[:, slot * TT:slot * TT + TT], func=AF.Exp,
                                     scale=SCALE).then_inc(sP.h, 1)
                    p.on("scalar", f); sP.v += 1

                def emit_av(ci, base_ci):
                    k0 = k0s[ci]
                    gci = base_ci + ci
                    slot = gci % 3
                    p.wait("tensor", sP, gci + 1)
                    is_first = (ci == 0)
                    is_last = (ci == nch - 1)
                    if is_first and qt * NH + h >= 1:
                        p.wait("tensor", sAVe, qt * NH + h)
                    def f(e, slot=slot, kv=kv, k0=k0, h=h,
                          is_first=is_first, is_last=is_last):
                        i_d = e.matmul(pDen[0:NH, :], onehot[:, h * NH:(h + 1) * NH],
                                       probs[:, slot * TT:slot * TT + TT],
                                       start=(h == 0 and is_first), stop=(h == NH - 1 and is_last))
                        if h == NH - 1 and is_last:
                            i_d.then_inc(sDen.h, 1)
                        e.matmul(pAV[:, :], v_cache[:, kv * S + k0:kv * S + k0 + 128],
                                 probs[:, slot * TT:slot * TT + TT],
                                 start=is_first, stop=is_last).then_inc(sAV.h, 1)
                    p.on("tensor", f); sAV.v += 1
                    if h == NH - 1 and is_last:
                        sDen.v += 1

                base_ci = chunk_i
                emit_qk(0, base_ci)
                if nch > 1:
                    emit_qk(1, base_ci)
                for ci in range(nch):
                    if ci + 2 < nch:
                        emit_qk(ci + 2, base_ci)
                    emit_av(ci, base_ci)
                chunk_i += nch
                p.wait("scalar", sAV, sAV.v)
                def f(e, h=h):
                    e.copy(ao_stage[:, h * TT:(h + 1) * TT], pAV[:, :]).then_inc(sAVe.h, 1)
                p.on("scalar", f); sAVe.v += 1
                if gath is not None:
                    gath.emit_slice(p, 2)
                    gath.emit_slice(p, 2)
            p.wait("vector", sDen, sDen.v)
            def f(e):
                e.reciprocal(srec[:], pDen[0:NH, :]).then_inc(sRec.h, 1)
            p.on("vector", f); sRec.v += 1
            p.wait("tensor", sRec, sRec.v)
            for h in range(NH):
                if sB.v >= 1:
                    p.wait("tensor", sSc, sB.v)
                def f(e, h=h):
                    e.matmul(pB[:, :], ohrow[:, h * 128:(h + 1) * 128], srec[:],
                             start=True, stop=True).then_inc(sB.h, 1)
                p.on("tensor", f); sB.v += 1
                p.wait("vector", sB, sB.v)
                p.wait("vector", sAVe, sAVe.v - (NH - 1 - h))
                def f(e, h=h):
                    e.tensor_tensor(out=ao_stage[:, h * TT:(h + 1) * TT],
                                    in0=ao_stage[:, h * TT:(h + 1) * TT],
                                    in1=pB[:, :], op=ALU.mult).then_inc(sSc.h, 1)
                p.on("vector", f); sSc.v += 1
            p.wait("gpsimd", sSc, sSc.v)
            def f(e, qt=qt):
                e.dma_start(out=ao_s[:, qt * TT:(qt + 1) * TT].rearrange("(kc pp) t -> pp kc t", pp=128),
                            in_=ao_stage[:].rearrange("pp (kc t) -> pp kc t", kc=NKC)).then_inc(sAOD.h, 16)
            p.on("gpsimd", f); sAOD.v += 16
            p.wait("vector", sAOD, sAOD.v)
        if gath is not None:
            gath.flush_phase(p)
        p.wait("gpsimd", sAOD, sAOD.v)
        blk = es.enter_context(nc.Block("p2"))
        p.emit(blk)


def _p3(nc, x, wo, ao_s, x1_s, y_s, ident, gath=None):
    with contextlib.ExitStack() as es:
        wob = es.enter_context(nc.sbuf_tensor("wob", [128, NKC * H], BF16))
        wtmp = [es.enter_context(nc.sbuf_tensor(f"owt{i}", [128, H], F32)) for i in range(2)]
        aot = [es.enter_context(nc.sbuf_tensor(f"aot{i}", [128, NKC * TT], BF16)) for i in range(2)]
        obuf = es.enter_context(nc.sbuf_tensor("obuf", [128, NH * TT], BF16))
        oT = es.enter_context(nc.sbuf_tensor("oT", [128, H], BF16))
        xc_b = [es.enter_context(nc.sbuf_tensor(f"p3x{i}", [128, H], F32)) for i in range(2)]
        x1c = [es.enter_context(nc.sbuf_tensor(f"x1c{i}", [128, H], F32)) for i in range(2)]
        ytm = es.enter_context(nc.sbuf_tensor("ytm", [128, H], BF16))
        yfm = es.enter_context(nc.sbuf_tensor("yfm", [128, NKC * TT], BF16))
        xstat = es.enter_context(nc.sbuf_tensor("p3stat", [128, 24], F32))
        xaggr = es.enter_context(nc.sbuf_tensor("p3aggr", [128, 2], F32))
        nmean = es.enter_context(nc.sbuf_tensor("p3nm", [128, 1], F32))
        rstd = es.enter_context(nc.sbuf_tensor("p3rs", [128, 1], F32))
        po = es.enter_context(nc.psum_tensor("po", [128, 2 * TT], F32))
        pto = es.enter_context(nc.psum_tensor("p3pto", [128, 2048], BF16))
        pty = es.enter_context(nc.psum_tensor("p3pty", [128, 2048], BF16))
        ld = Sem(es, nc, "o_ld")
        cst = Sem(es, nc, "o_cst")
        lda = Sem(es, nc, "o_lda")
        ldxc = Sem(es, nc, "o_ldxc")
        mm = Sem(es, nc, "o_mm")
        oev = Sem(es, nc, "o_oev")
        otr = Sem(es, nc, "o_otr")
        ote = Sem(es, nc, "o_ote")
        sx1 = Sem(es, nc, "o_sx1")
        sx1d = Sem(es, nc, "o_sx1d")
        lnq = Sem(es, nc, "o_lnq")
        lnr = Sem(es, nc, "o_lnr")
        syt = Sem(es, nc, "o_syt")
        ytr = Sem(es, nc, "o_ytr")
        yev = Sem(es, nc, "o_yev")
        syd = Sem(es, nc, "o_syd")
        p = Prog()
        NCH = 4

        for kc in range(NKC):
            if kc >= 2:
                p.wait("sync", cst, kc - 1)
            def f(e, kc=kc, buf=wtmp[kc % 2], ld=ld):
                e.dma_start(out=buf[:], in_=wo[kc * 128:(kc + 1) * 128, :]).then_inc(ld.h, 16)
            p.on("sync", f); ld.v += 16
            p.wait("vector", ld, 16 * (kc + 1))
            def f(e, kc=kc, buf=wtmp[kc % 2], cst=cst):
                e.tensor_copy(wob[:, kc * H:(kc + 1) * H], buf[:]).then_inc(cst.h, 1)
            p.on("vector", f); cst.v += 1

        for tt in range(NTILE):
            if tt >= 2:
                p.wait("sync", mm, (tt - 1) * NH)
            def f(e, tt=tt, buf=aot[tt % 2], lda=lda):
                e.dma_start(out=buf[:].rearrange("pp (kc t) -> pp kc t", kc=NKC), in_=ao_s[:, tt * TT:(tt + 1) * TT].rearrange("(kc pp) t -> pp kc t", pp=128)).then_inc(lda.h, 16)
            p.on("sync", f); lda.v += 16
            for c in range(NCH):
                k = tt * NCH + c
                if k >= 2:
                    p.wait("sync", sx1, k - 1)
                def f(e, tt=tt, c=c, buf=xc_b[k % 2], ldxc=ldxc):
                    e.dma_start(out=buf[:], in_=x[tt * TT + c * 128:tt * TT + (c + 1) * 128, :]).then_inc(ldxc.h, 16)
                p.on("sync", f); ldxc.v += 16
            p.wait("tensor", lda, 16 * (tt + 1))
            if tt == 0:
                p.wait("tensor", cst, NKC)
            if tt >= 1:
                p.wait("scalar", otr, tt * NCH * NKC)
            for m in range(NH):
                mi = tt * NH + m
                if mi >= 2:
                    p.wait("tensor", oev, mi - 1)
                for kc in range(NKC):
                    last = (kc == NKC - 1)
                    def f(e, m=m, kc=kc, buf=aot[tt % 2], slot=mi % 2, last=last):
                        i_ = e.matmul(po[:, slot * TT:slot * TT + TT],
                                      wob[:, kc * H + m * 128:kc * H + (m + 1) * 128],
                                      buf[:, kc * TT:(kc + 1) * TT], start=(kc == 0), stop=last)
                        if last:
                            i_.then_inc(mm.h, 1)
                    p.on("tensor", f)
                mm.v += 1
                p.wait("scalar", mm, mm.v)
                def f(e, m=m, slot=mi % 2):
                    e.copy(obuf[:, m * TT:(m + 1) * TT], po[:, slot * TT:slot * TT + TT]).then_inc(oev.h, 1)
                p.on("scalar", f); oev.v += 1
            p.wait("tensor", oev, oev.v)
            for c in range(NCH):
                k = tt * NCH + c
                for mch in range(NKC):
                    ti = k * NKC + mch
                    if ti >= 2:
                        p.wait("tensor", ote, ti - 1)
                    def f(e, c=c, mch=mch, slot=ti % 2):
                        e.transpose(pto[:, slot * 1024:slot * 1024 + 128],
                                    obuf[:, mch * TT + c * 128:mch * TT + c * 128 + 128],
                                    ident[:]).then_inc(otr.h, 1)
                    p.on("tensor", f)
                    p.wait("scalar", otr, ti + 1)
                    def f(e, mch=mch, slot=ti % 2):
                        e.copy(oT[:, mch * 128:(mch + 1) * 128],
                               pto[:, slot * 1024:slot * 1024 + 128]).then_inc(ote.h, 1)
                    p.on("scalar", f); ote.v += 1
                otr.v = (k + 1) * NKC
                p.wait("vector", ote, ote.v)
                p.wait("vector", ldxc, 16 * (k + 1))
                if k >= 2:
                    p.wait("vector", sx1d, 16 * (k - 1))
                def f(e, k=k):
                    e.tensor_tensor(out=x1c[k % 2][:], in0=xc_b[k % 2][:], in1=oT[:], op=ALU.add).then_inc(sx1.h, 1)
                    e.drain()
                    for i in range(4):
                        e.bn_stats(xstat[:, 6 * i:6 * (i + 1)], x1c[k % 2][:, i * 512:(i + 1) * 512])
                    e.drain()
                    e.bn_aggr(xaggr[:], xstat[:])
                    e.drain()
                    e.tensor_scalar(out=nmean[:], in0=xaggr[:, 0:1], scalar1=-1.0, scalar2=None, op0=ALU.mult)
                    e.tensor_scalar(out=rstd[:], in0=xaggr[:, 1:2], scalar1=EPS, scalar2=None, op0=ALU.add).then_inc(lnq.h, 1)
                p.on("vector", f); sx1.v += 1; lnq.v += 1
                p.wait("scalar", sx1, sx1.v)
                p.wait("scalar", lnq, lnq.v)
                def f(e):
                    e.activation(out=rstd[:], in_=rstd[:], func=AF.Ln)
                    e.drain()
                    e.activation(out=rstd[:], in_=rstd[:], func=AF.Exp, scale=-0.5).then_inc(lnr.h, 1)
                p.on("scalar", f); lnr.v += 1
                p.wait("vector", lnr, lnr.v)
                if k >= 1:
                    p.wait("vector", ytr, k * NKC)
                def f(e, k=k):
                    e.tensor_scalar(out=ytm[:], in0=x1c[k % 2][:], scalar1=nmean[:], scalar2=rstd[:],
                                    op0=ALU.add, op1=ALU.mult).then_inc(syt.h, 1)
                p.on("vector", f); syt.v += 1
                p.wait("gpsimd", sx1, sx1.v)
                def f(e, tt=tt, c=c, k=k):
                    e.dma_start(out=x1_s[tt * TT + c * 128:tt * TT + (c + 1) * 128, :], in_=x1c[k % 2][:]).then_inc(sx1d.h, 16)
                p.on("gpsimd", f); sx1d.v += 16
                p.wait("tensor", syt, syt.v)
                for kc in range(NKC):
                    ti = k * NKC + kc
                    if ti >= 2:
                        p.wait("tensor", yev, ti - 1)
                    def f(e, kc=kc, slot=ti % 2):
                        e.transpose(pty[:, slot * 1024:slot * 1024 + 128],
                                    ytm[:, kc * 128:(kc + 1) * 128], ident[:]).then_inc(ytr.h, 1)
                    p.on("tensor", f)
                    p.wait("scalar", ytr, ti + 1)
                    def f(e, kc=kc, c=c, slot=ti % 2):
                        e.copy(yfm[:, kc * TT + c * 128:kc * TT + c * 128 + 128],
                               pty[:, slot * 1024:slot * 1024 + 128]).then_inc(yev.h, 1)
                    p.on("scalar", f); yev.v += 1
                ytr.v = (k + 1) * NKC
            p.wait("gpsimd", yev, yev.v)
            def f(e, tt=tt):
                e.dma_start(out=y_s[:, tt * TT:(tt + 1) * TT].rearrange("(kc pp) t -> pp kc t", pp=128),
                            in_=yfm[:].rearrange("pp (kc t) -> pp kc t", kc=NKC)).then_inc(syd.h, 16)
            p.on("gpsimd", f); syd.v += 16
            p.wait("scalar", syd, syd.v)
        p.wait("gpsimd", syd, syd.v)
        p.wait("gpsimd", sx1d, sx1d.v)
        blk = es.enter_context(nc.Block("p3"))
        p.emit(blk)


DBG_P4 = []
def _p4(nc, y_s, x1_s, wg_s, wu_s, wd_s, out, ident):
    TT4 = 1024
    NT4 = S // TT4
    NG = 1
    n_groups = (NIC + NG - 1) // NG
    with contextlib.ExitStack() as es:
        yfm = es.enter_context(nc.sbuf_tensor("f_yfm", [128, NKC * TT4], BF16))
        hb = es.enter_context(nc.sbuf_tensor("f_h", [128, NIC * TT4], BF16))
        wgu = [es.enter_context(nc.sbuf_tensor(f"f_wgu{i}", [128, 2 * NKC * NG * 128], BF16)) for i in range(2)]
        wdb = [es.enter_context(nc.sbuf_tensor(f"f_wd{i}", [128, INTER], BF16)) for i in range(2)]
        sgl = es.enter_context(nc.sbuf_tensor("f_sgl", [128, 512], F32))
        dstage = es.enter_context(nc.sbuf_tensor("f_dst", [128, 2 * NKC * 512], BF16))
        dT = es.enter_context(nc.sbuf_tensor("f_dT", [128, H], BF16))
        x1b = [es.enter_context(nc.sbuf_tensor("f_x1", [128, H], F32))] * 2
        
        pgu = es.enter_context(nc.psum_tensor("f_pgu", [128, 4 * 512], F32))
        pd = es.enter_context(nc.psum_tensor("f_pd", [128, 2 * 512], F32))
        ptd = es.enter_context(nc.psum_tensor("f_ptd", [128, 2048], BF16))
        ldy = Sem(es, nc, "f_ldy")
        ldw = Sem(es, nc, "f_ldw")
        ldd = Sem(es, nc, "f_ldd")
        ldx1 = Sem(es, nc, "f_ldx1")
        mmg = Sem(es, nc, "f_mmg")
        sil = Sem(es, nc, "f_sil")
        hwr = Sem(es, nc, "f_hwr")
        mmd = Sem(es, nc, "f_mmd")
        dev = Sem(es, nc, "f_dev")
        dtr = Sem(es, nc, "f_dtr")
        dte = Sem(es, nc, "f_dte")
        sou = Sem(es, nc, "f_sou")
        sod = Sem(es, nc, "f_sod")
        p = Prog()

        grp_ics = [list(range(g * NG, min((g + 1) * NG, NIC))) for g in range(n_groups)]
        grp_units = [2 * len(g) for g in grp_ics]
        cumu = [0]
        for g in grp_units:
            cumu.append(cumu[-1] + g)
        units_per_tile = cumu[-1]
        gcum = [0]
        for t4_ in range(NT4):
            for g_ in range(n_groups):
                gcum.append(gcum[-1] + grp_units[g_])

        for t4 in range(NT4):
            if t4 >= 1:
                p.wait("sync", hwr, t4 * units_per_tile)
                p.wait("vector", mmd, t4 * NKC * 2)
            def f(e, t4=t4):
                e.dma_start(out=yfm[:].rearrange("pp (kc t) -> pp kc t", kc=NKC), in_=y_s[:, t4 * TT4:(t4 + 1) * TT4].rearrange("(kc pp) t -> pp kc t", pp=128)).then_inc(ldy.h, 16)
            p.on("sync", f); ldy.v += 16
            for gi in range(n_groups):
                ics = grp_ics[gi]
                n = len(ics)
                gj = t4 * n_groups + gi
                gslot = gj % 2
                if gj >= 2:
                    p.wait("sync", hwr, gcum[gj - 1])
                def f(e, ics=ics, n=n, buf=wgu[gslot]):
                    e.dma_start(out=buf[:, 0:NKC * n * 128].rearrange("pp (kc j) -> pp kc j", kc=NKC),
                                in_=wg_s[:, :, ics[0] * 128:ics[0] * 128 + n * 128].rearrange("kc pp j -> pp kc j")).then_inc(ldw.h, 16)
                    e.dma_start(out=buf[:, NKC * NG * 128:NKC * NG * 128 + NKC * n * 128].rearrange("pp (kc j) -> pp kc j", kc=NKC),
                                in_=wu_s[:, :, ics[0] * 128:ics[0] * 128 + n * 128].rearrange("kc pp j -> pp kc j")).then_inc(ldw.h, 16)
                p.on("sync", f); ldw.v += 32
                p.wait("tensor", ldw, ldw.v)
                if gi == 0:
                    p.wait("tensor", ldy, ldy.v)
                for icl, ic in enumerate(ics):
                    for tc in range(2):
                        ui = t4 * units_per_tile + cumu[gi] + icl * 2 + tc
                        gs = ui % 2
                        if ui >= 2:
                            p.wait("tensor", sil, ui - 1)
                            p.wait("tensor", hwr, ui - 1)
                        for kc in range(NKC):
                            last = (kc == NKC - 1)
                            def f(e, icl=icl, kc=kc, tc=tc, gs=gs, gslot=gslot, last=last, n=n):
                                e.matmul(pgu[:, gs * 512:gs * 512 + 512],
                                         wgu[gslot][:, (kc * n + icl) * 128:(kc * n + icl) * 128 + 128],
                                         yfm[:, kc * TT4 + tc * 512:kc * TT4 + tc * 512 + 512],
                                         start=(kc == 0), stop=last)
                                i_ = e.matmul(pgu[:, (2 + gs) * 512:(2 + gs) * 512 + 512],
                                              wgu[gslot][:, NKC * NG * 128 + (kc * n + icl) * 128:NKC * NG * 128 + (kc * n + icl) * 128 + 128],
                                              yfm[:, kc * TT4 + tc * 512:kc * TT4 + tc * 512 + 512],
                                              start=(kc == 0), stop=last)
                                if last:
                                    i_.then_inc(mmg.h, 1)
                            p.on("tensor", f)
                        mmg.v += 1
                        p.wait("scalar", mmg, mmg.v)
                        if ui >= 1:
                            p.wait("scalar", hwr, ui)
                        def f(e, gs=gs):
                            e.activation(out=sgl[:], in_=pgu[:, gs * 512:gs * 512 + 512], func=AF.Silu).then_inc(sil.h, 1)
                        p.on("scalar", f); sil.v += 1
                        p.wait("vector", sil, sil.v)
                        def f(e, ic=ic, tc=tc, gs=gs):
                            e.tensor_tensor(out=hb[:, ic * TT4 + tc * 512:ic * TT4 + tc * 512 + 512],
                                            in0=sgl[:], in1=pgu[:, (2 + gs) * 512:(2 + gs) * 512 + 512],
                                            op=ALU.mult).then_inc(hwr.h, 1)
                        p.on("vector", f); hwr.v += 1
            for m in range(NKC):
                md = t4 * NKC + m
                if md >= 2:
                    p.wait("sync", mmd, (md - 1) * 2)
                def f(e, m=m, buf=wdb[md % 2]):
                    e.dma_start(out=buf[:], in_=wd_s[m, :, :]).then_inc(ldd.h, 16)
                p.on("sync", f); ldd.v += 16
                p.wait("tensor", ldd, ldd.v)
                if m == 0:
                    p.wait("tensor", hwr, (t4 + 1) * units_per_tile)
                for tc in range(2):
                    di = md * 2 + tc
                    if di >= 2:
                        p.wait("tensor", dev, di - 1)
                    for ic in range(NIC):
                        last = (ic == NIC - 1)
                        def f(e, ic=ic, tc=tc, slot=di % 2, buf=wdb[md % 2], last=last):
                            i_ = e.matmul(pd[:, slot * 512:slot * 512 + 512],
                                          buf[:, ic * 128:(ic + 1) * 128],
                                          hb[:, ic * TT4 + tc * 512:ic * TT4 + tc * 512 + 512],
                                          start=(ic == 0), stop=last)
                            if last:
                                i_.then_inc(mmd.h, 1)
                        p.on("tensor", f)
                    mmd.v += 1
                    p.wait("scalar", mmd, mmd.v)
                    if t4 >= 1 and di < 2:
                        p.wait("scalar", dtr, t4 * 8 * NKC)
                    def f(e, m=m, tc=tc, slot=di % 2):
                        e.copy(dstage[:, tc * NKC * 512 + m * 512:tc * NKC * 512 + m * 512 + 512],
                               pd[:, slot * 512:slot * 512 + 512]).then_inc(dev.h, 1)
                    p.on("scalar", f); dev.v += 1
            p.wait("tensor", dev, dev.v)
            for tc in range(2):
                for c in range(4):
                    k = t4 * 8 + tc * 4 + c
                    if k >= 1:
                        p.wait("sync", sou, k)
                        p.wait("sync", sod, 16 * k)
                    def f(e, t4=t4, tc=tc, c=c, buf=x1b[k % 2]):
                        e.dma_start(out=buf[:], in_=x1_s[t4 * TT4 + tc * 512 + c * 128:t4 * TT4 + tc * 512 + (c + 1) * 128, :]).then_inc(ldx1.h, 16)
                    p.on("sync", f); ldx1.v += 16
                    for m in range(NKC):
                        ti = k * NKC + m
                        if ti >= 2:
                            p.wait("tensor", dte, ti - 1)
                        def f(e, m=m, tc=tc, c=c, slot=ti % 2):
                            e.transpose(ptd[:, slot * 1024:slot * 1024 + 128],
                                        dstage[:, tc * NKC * 512 + m * 512 + c * 128:tc * NKC * 512 + m * 512 + c * 128 + 128],
                                        ident[:]).then_inc(dtr.h, 1)
                        p.on("tensor", f)
                        p.wait("scalar", dtr, ti + 1)
                        def f(e, m=m, slot=ti % 2):
                            e.copy(dT[:, m * 128:(m + 1) * 128],
                                   ptd[:, slot * 1024:slot * 1024 + 128]).then_inc(dte.h, 1)
                        p.on("scalar", f); dte.v += 1
                    dtr.v = (k + 1) * NKC
                    p.wait("vector", dte, dte.v)
                    p.wait("vector", ldx1, 16 * (k + 1))
                    if k >= 1:
                        p.wait("vector", sod, 16 * k)
                    def f(e, k=k):
                        e.tensor_tensor(out=x1b[k % 2][:], in0=x1b[k % 2][:], in1=dT[:], op=ALU.add).then_inc(sou.h, 1)
                    p.on("vector", f); sou.v += 1
                    p.wait("scalar", sou, sou.v)
                    p.wait("gpsimd", sou, sou.v)
                    def f(e, t4=t4, tc=tc, c=c, k=k):
                        e.dma_start(out=out[t4 * TT4 + tc * 512 + c * 128:t4 * TT4 + tc * 512 + (c + 1) * 128, :],
                                    in_=x1b[k % 2][:]).then_inc(sod.h, 16)
                    p.on("gpsimd", f); sod.v += 16
        p.wait("gpsimd", sod, sod.v)
        if DBG_P4:
            hdmp = nc.dram_tensor("dbg_h_last", [128, NIC * TT4], BF16, kind="ExternalOutput").ap()
            ddmp = nc.dram_tensor("dbg_dst_last", [128, 2 * NKC * 512], BF16, kind="ExternalOutput").ap()
            hst = Sem(es, nc, "f_hdmp")
            def f(e):
                e.dma_start(out=hdmp[:, :], in_=hb[:]).then_inc(hst.h, 16)
                e.dma_start(out=ddmp[:, :], in_=dstage[:]).then_inc(hst.h, 16)
                e.wait_ge(hst.h, 32)
            p.on("gpsimd", f)
        blk = es.enter_context(nc.Block("p4"))
        p.emit(blk)


from concourse.bass_utils import run_bass_kernel_spmd

_NC_CACHE = []

def kernel(L0_attn_hidden_states, L0_attn_Wq, L0_attn_Wk, L0_attn_Wv, L0_attn_Wo,
           L0_ffn_lut, L0_ffn_gate_idx, L0_ffn_up_idx, L0_ffn_down_idx):
    x = np.ascontiguousarray(np.asarray(L0_attn_hidden_states, dtype=np.float32))
    wq_a = np.ascontiguousarray(np.asarray(L0_attn_Wq, dtype=np.float32))
    wk_a = np.ascontiguousarray(np.asarray(L0_attn_Wk, dtype=np.float32))
    wv_a = np.ascontiguousarray(np.asarray(L0_attn_Wv, dtype=np.float32))
    wo_a = np.ascontiguousarray(np.asarray(L0_attn_Wo, dtype=np.float32))
    lut_a = np.asarray(L0_ffn_lut, dtype=np.float32)
    lutr = np.ascontiguousarray(np.repeat(lut_a[None, :], 128, axis=0))
    gi = np.ascontiguousarray(np.asarray(L0_ffn_gate_idx, dtype=np.int32))
    ui = np.ascontiguousarray(np.asarray(L0_ffn_up_idx, dtype=np.int32))
    di = np.ascontiguousarray(np.asarray(L0_ffn_down_idx, dtype=np.int32))

    if not _NC_CACHE:
        _NC_CACHE.append(build()[0])
    nc = _NC_CACHE[0]
    in_maps = []
    for core in range(8):
        in_maps.append({
            "x": np.ascontiguousarray(x[core]),
            "wq": wq_a, "wk": wk_a, "wv": wv_a, "wo": wo_a,
            "lutr": lutr, "gidx": gi, "uidx": ui, "didx": di,
        })
    res = run_bass_kernel_spmd(nc, in_maps, core_ids=list(range(8)))
    return np.stack([np.asarray(res.results[i]["out"], dtype=np.float32) for i in range(8)], axis=0)


# version 11
# speedup vs baseline: 1.1297x; 1.0296x over previous
import contextlib
import numpy as np
import concourse.bass as bass
import concourse.mybir as mybir

F32 = mybir.dt.float32
BF16 = mybir.dt.bfloat16
I32 = mybir.dt.int32
AF = mybir.ActivationFunctionType
ALU = mybir.AluOpType

S = 4096
H = 2048
D = 128
NH = 16
KVH = 4
WIN = 1024
INTER = 5504
NIC = INTER // 128
NKC = H // 128
TT = 512
NTILE = S // TT
SCALE = 1.0 / float(np.sqrt(D))
NEG = -1.0e9
EPS = 1e-5

ENGINES = ("sync", "vector", "scalar", "tensor", "gpsimd")


class Sem:
    def __init__(self, es, nc, name):
        self.h = es.enter_context(nc.semaphore(name))
        self.v = 0


class Prog:
    def __init__(self):
        self.ops = {e: [] for e in ENGINES}

    def on(self, eng, fn):
        self.ops[eng].append(fn)

    def wait(self, eng, sem, val):
        if val > 0:
            self.ops[eng].append(lambda e, s=sem.h, v=val: e.wait_ge(s, v))

    def emit(self, blk):
        for eng in ENGINES:
            if not self.ops[eng]:
                continue
            def body(e, lst=tuple(self.ops[eng])):
                for fn in lst:
                    fn(e)
            getattr(blk, eng)(body)


def build(phases=("p0", "p1a", "p1b", "p2", "p3", "p4"), dbg=()):
    nc = bass.Bass("TRN2", target_bir_lowering=False)

    x = nc.dram_tensor("x", [S, H], F32, kind="ExternalInput").ap()
    wq = nc.dram_tensor("wq", [H, H], F32, kind="ExternalInput").ap()
    wk = nc.dram_tensor("wk", [H, KVH * D], F32, kind="ExternalInput").ap()
    wv = nc.dram_tensor("wv", [H, KVH * D], F32, kind="ExternalInput").ap()
    wo = nc.dram_tensor("wo", [H, H], F32, kind="ExternalInput").ap()
    lutr = nc.dram_tensor("lutr", [128, 16], F32, kind="ExternalInput").ap()
    gidx = nc.dram_tensor("gidx", [INTER, H], I32, kind="ExternalInput").ap()
    uidx = nc.dram_tensor("uidx", [INTER, H], I32, kind="ExternalInput").ap()
    didx = nc.dram_tensor("didx", [H, INTER], I32, kind="ExternalInput").ap()
    out = nc.dram_tensor("out", [S, H], F32, kind="ExternalOutput").ap()

    xln_s = nc.dram_tensor("xln_s", [H, S], BF16, kind="Internal").ap()
    q_s = nc.dram_tensor("q_s", [H, S], BF16, kind="Internal").ap()
    ao_s = nc.dram_tensor("ao_s", [H, S], BF16, kind="Internal").ap()
    p4_standalone = ("p4" in phases and "p3" not in phases)
    x1_s = nc.dram_tensor("x1_s", [S, H], F32, kind="ExternalInput" if p4_standalone else "Internal").ap()
    y_s = nc.dram_tensor("y_s", [H, S], BF16, kind="ExternalInput" if p4_standalone else "Internal").ap()
    wg_s = nc.dram_tensor("wg_s", [NKC, 128, INTER], BF16, kind="Internal").ap()
    wu_s = nc.dram_tensor("wu_s", [NKC, 128, INTER], BF16, kind="Internal").ap()
    wd_s = nc.dram_tensor("wd_s", [NKC, 128, INTER], BF16, kind="Internal").ap()

    dbg_aps = {}
    with contextlib.ExitStack() as top:
        ident = top.enter_context(nc.sbuf_tensor("ident", [128, 128], BF16))
        slut = top.enter_context(nc.sbuf_tensor("slut", [128, 16], F32))
        ones_col = top.enter_context(nc.sbuf_tensor("ones_col", [128, 1], BF16))
        ones_row = top.enter_context(nc.sbuf_tensor("ones_row", [1, 128], F32))

        _setup(nc, lutr, slut, ident, ones_col, ones_row)

        gath = None
        if "p0" in phases:
            g_es = top.enter_context(contextlib.ExitStack())
            gath = Gather(nc, g_es, slut, ident, gidx, uidx, didx, wg_s, wu_s, wd_s)
        with contextlib.ExitStack() as kvscope:
            k_cache = kvscope.enter_context(nc.sbuf_tensor("k_cache", [128, KVH * S], BF16))
            v_cache = kvscope.enter_context(nc.sbuf_tensor("v_cache", [128, KVH * S], BF16))
            if "p1a" in phases:
                _p1a(nc, x, wk, wv, xln_s, k_cache, v_cache, ident)
            if "p1b" in phases:
                _p1b(nc, wq, xln_s, q_s)
            if "p2" in phases:
                _p2(nc, q_s, ao_s, k_cache, v_cache, ones_col, ones_row, gath, ident)
            if "kv" in dbg:
                for nm, t in (("k_cache", k_cache), ("v_cache", v_cache)):
                    dst = nc.dram_tensor("dbg_" + nm, [128, KVH * S], BF16, kind="ExternalOutput").ap()
                    dbg_aps[nm] = dst
                    _sbuf_dump(nc, dst, t)
        if "p3" in phases:
            _p3(nc, x, wo, ao_s, x1_s, y_s, ident, gath)
        if gath is not None:
            _p0(nc, gath)
        if "p4" in phases:
            _p4(nc, y_s, x1_s, wg_s, wu_s, wd_s, out, ident)
        else:
            _dump_stub(nc, out)

        for name, src, shp, dt in (
            ("xln_s", xln_s, [H, S], BF16), ("q_s", q_s, [H, S], BF16),
            ("ao_s", ao_s, [H, S], BF16), ("x1_s", x1_s, [S, H], F32),
            ("y_s", y_s, [H, S], BF16),
            ("wg_s", wg_s, [NKC, 128, INTER], BF16),
            ("wu_s", wu_s, [NKC, 128, INTER], BF16),
            ("wd_s", wd_s, [NKC, 128, INTER], BF16),
        ):
            if name in dbg:
                dst = nc.dram_tensor("dbg_" + name, shp, dt, kind="ExternalOutput").ap()
                dbg_aps[name] = dst
                _dram_copy(nc, dst, src, shp, dt)
    return nc, dbg_aps


def _setup(nc, lutr, slut, ident, ones_col, ones_row):
    with contextlib.ExitStack() as es:
        sidi = es.enter_context(nc.sbuf_tensor("sidi", [128, 128], I32))
        ld = Sem(es, nc, "su_ld")
        dn = Sem(es, nc, "su_dn")
        p = Prog()
        def f(e):
            e.dma_start(out=slut[:], in_=lutr[:, :]).then_inc(ld.h, 16)
        p.on("sync", f); ld.v += 16
        def f(e):
            e.iota(sidi[:], pattern=[[-1, 128]], base=0, channel_multiplier=1).then_inc(dn.h, 1)
        p.on("gpsimd", f); dn.v += 1
        p.wait("vector", dn, dn.v)
        p.wait("vector", ld, ld.v)
        def f(e):
            e.tensor_scalar(out=ident[:], in0=sidi[:], scalar1=0, scalar2=None, op0=ALU.is_equal)
            e.memset(ones_col[:], 1.0)
            e.memset(ones_row[:], 1.0)
        p.on("vector", f)
        blk = es.enter_context(nc.Block("setup"))
        p.emit(blk)


def _dump_stub(nc, out):
    with contextlib.ExitStack() as es:
        z = es.enter_context(nc.sbuf_tensor("zz", [128, H], F32))
        st = Sem(es, nc, "z_st")
        dn = Sem(es, nc, "z_dn")
        p = Prog()
        def f(e):
            e.memset(z[:], 0.0).then_inc(dn.h, 1)
        p.on("vector", f); dn.v += 1
        p.wait("gpsimd", dn, dn.v)
        def f(e):
            for i in range(S // 128):
                e.dma_start(out=out[i * 128:(i + 1) * 128, :], in_=z[:]).then_inc(st.h, 16)
            e.wait_ge(st.h, 16 * (S // 128))
        p.on("gpsimd", f); st.v += 16 * (S // 128)
        blk = es.enter_context(nc.Block("zout"))
        p.emit(blk)


_DP_N = [0]
def _sbuf_dump(nc, dst, src):
    _DP_N[0] += 1
    uid = _DP_N[0]
    with contextlib.ExitStack() as es:
        st = Sem(es, nc, f"dmp_st{uid}")
        p = Prog()
        def f(e):
            e.dma_start(out=dst[:, :], in_=src[:]).then_inc(st.h, 16)
            e.wait_ge(st.h, 16)
        p.on("gpsimd", f)
        blk = es.enter_context(nc.Block(f"dmp{uid}"))
        p.emit(blk)


_CP_N = [0]
def _dram_copy(nc, dst, src, shp, dt):
    total = int(np.prod(shp))
    FREE = 8192
    _CP_N[0] += 1
    uid = _CP_N[0]
    with contextlib.ExitStack() as es:
        buf = es.enter_context(nc.sbuf_tensor(f"cp_buf{uid}", [128, FREE], dt))
        ld = Sem(es, nc, f"cp_ld{uid}")
        st = Sem(es, nc, f"cp_st{uid}")
        p = Prog()
        if len(shp) == 2:
            fsrc = src.rearrange("a b -> (a b)")
            fdst = dst.rearrange("a b -> (a b)")
        else:
            fsrc = src.rearrange("a b c -> (a b c)")
            fdst = dst.rearrange("a b c -> (a b c)")
        chunk = 128 * FREE
        nch = (total + chunk - 1) // chunk
        def f(e):
            for i in range(nch):
                n = min(chunk, total - i * chunk)
                rows = n // FREE
                e.wait_ge(st.h, 16 * i)
                e.dma_start(out=buf[0:rows, :], in_=fsrc[i * chunk:i * chunk + n].rearrange("(pp ff) -> pp ff", ff=FREE)).then_inc(ld.h, 16)
                e.wait_ge(ld.h, 16 * (i + 1))
                e.dma_start(out=fdst[i * chunk:i * chunk + n].rearrange("(pp ff) -> pp ff", ff=FREE), in_=buf[0:rows, :]).then_inc(st.h, 16)
            e.wait_ge(st.h, 16 * nch)
        p.on("gpsimd", f)
        blk = es.enter_context(nc.Block(f"cp{uid}"))
        p.emit(blk)


class Gather:
    def __init__(self, nc, es, slut, ident, gidx, uidx, didx, wg_s, wu_s, wd_s):
        self.nc = nc
        self.slut = slut
        self.ident = ident
        units = []
        for b_ in range(NIC):
            units.append((gidx[b_ * 128:(b_ + 1) * 128, :], wg_s, b_, 0, H, "gu"))
        for b_ in range(NIC):
            units.append((uidx[b_ * 128:(b_ + 1) * 128, :], wu_s, b_, 0, H, "gu"))
        for b_ in range(NKC):
            for c0, w in ((0, 2048), (2048, 2048), (4096, 1408)):
                units.append((didx[b_ * 128:(b_ + 1) * 128, c0:c0 + w], wd_s, b_, c0, w, "d"))
        self.units = units
        self.cursor = 0
        self.ld = Sem(es, nc, "G_ld")
        self.gat = Sem(es, nc, "G_gat")
        self.trn = Sem(es, nc, "G_trn")
        self.evc = Sem(es, nc, "G_evc")
        self.stg = Sem(es, nc, "G_stg")
        self.npc_hist = []
        self.phase_first = 0
        self._t = None

    def start_phase(self, es, width=2048):
        nc = self.nc
        W = width
        self.bw = W
        self.sidx = [es.enter_context(nc.sbuf_tensor(f"G_idx{i}_{self.cursor}", [128, W], I32)) for i in range(2)]
        self.sidxb = [es.enter_context(nc.sbuf_tensor(f"G_idxb{i}_{self.cursor}", [128, W], BF16)) for i in range(2)]
        self.smask = [es.enter_context(nc.sbuf_tensor(f"G_msk{i}_{self.cursor}", [128, W // 2], BF16)) for i in range(2)]
        self.smlut = [es.enter_context(nc.sbuf_tensor(f"G_mlt{i}_{self.cursor}", [128, W // 2], BF16)) for i in range(2)]
        self.sacc = [es.enter_context(nc.sbuf_tensor(f"G_acc{i}_{self.cursor}", [128, W], BF16)) for i in range(2)]
        self.sstage = [es.enter_context(nc.sbuf_tensor(f"G_stg{i}_{self.cursor}", [128, W], BF16)) for i in range(2)]
        self.pt = es.enter_context(nc.psum_tensor(f"G_pt_{self.cursor}", [128, 2048], BF16))
        self.phase_first = self.cursor

    def remaining(self):
        return len(self.units) - self.cursor

    def emit_units(self, p, n):
        for _ in range(min(n, self.remaining())):
            self._unit_head(p)
            while self._t < 16:
                self._emit_slice(p, 16)
            self._unit_tail(p)

    def emit_slice(self, p, nt=4):
        if self.cursor >= len(self.units):
            return False
        if self._t is None:
            self._unit_head(p)
        self._emit_slice(p, nt)
        if self._t >= 16:
            self._unit_tail(p)
        return True

    def flush_phase(self, p):
        if self._t is not None:
            while self._t < 16:
                self._emit_slice(p, 16)
            self._unit_tail(p)

    def _unit_head(self, p):
        u = self.cursor
        src_ap, dst, b_, c0, fw, kind = self.units[u]
        assert fw <= self.bw
        pl = u - self.phase_first
        slot = pl % 2
        if pl >= 2:
            p.wait("sync", self.gat, u - 1)
        def f(e, src=src_ap, buf=self.sidx[slot], fw=fw, ld=self.ld):
            e.dma_start(out=buf[:, 0:fw], in_=src).then_inc(ld.h, 16)
        p.on("sync", f); self.ld.v += 16
        p.wait("vector", self.ld, self.ld.v)
        if pl >= 2:
            p.wait("vector", self.trn, sum(self.npc_hist) - self.npc_hist[-1])
        self._t = 0

    def _emit_slice(self, p, nt):
        u = self.cursor
        src_ap, dst, b_, c0, fw, kind = self.units[u]
        hw = fw // 2
        pl = u - self.phase_first
        slot = pl % 2
        t0, t1 = self._t, min(16, self._t + nt)
        slut = self.slut
        def f(e, idx=self.sidx[slot], idxb=self.sidxb[slot], acc=self.sacc[slot],
              hw=hw, msk=self.smask, mlt=self.smlut, gat=self.gat, t0=t0, t1=t1):
            for t in range(t0, t1):
                if t == 0:
                    for hh in range(2):
                        e.tensor_scalar(out=idxb[:, hh * hw:hh * hw + hw], in0=idx[:, hh * hw:hh * hw + hw],
                                        scalar1=0.0, scalar2=None, op0=ALU.add)
                for hh in range(2):
                    e.tensor_scalar(out=msk[hh][:, 0:hw], in0=idxb[:, hh * hw:hh * hw + hw],
                                    scalar1=float(t), scalar2=None, op0=ALU.is_equal)
                for hh in range(2):
                    e.tensor_scalar(out=mlt[hh][:, 0:hw], in0=msk[hh][:, 0:hw],
                                    scalar1=slut[:, t:t + 1], scalar2=None, op0=ALU.mult)
                for hh in range(2):
                    if t == 0:
                        e.tensor_copy(acc[:, hh * hw:hh * hw + hw], mlt[hh][:, 0:hw])
                    else:
                        i_ = e.tensor_tensor(out=acc[:, hh * hw:hh * hw + hw],
                                             in0=acc[:, hh * hw:hh * hw + hw],
                                             in1=mlt[hh][:, 0:hw], op=ALU.add)
                        if t == 15 and hh == 1:
                            i_.then_inc(gat.h, 1)
        p.on("vector", f)
        if t1 == 16:
            self.gat.v += 1
        self._t = t1

    def _unit_tail(self, p):
        u = self.cursor
        src_ap, dst, b_, c0, fw, kind = self.units[u]
        npc = fw // 128
        pl = u - self.phase_first
        slot = pl % 2
        prior_npc = sum(self.npc_hist)
        ident = self.ident
        p.wait("tensor", self.gat, self.gat.v)
        for pc in range(npc):
            ti = prior_npc + pc
            if ti >= 2:
                p.wait("tensor", self.evc, ti - 1)
            def f(e, acc=self.sacc[slot], pc=pc, pslot=ti % 2, pt=self.pt):
                e.transpose(pt[:, pslot * 1024:pslot * 1024 + 128],
                            acc[:, pc * 128:pc * 128 + 128], ident[:]).then_inc(self.trn.h, 1)
            p.on("tensor", f)
        self.trn.v = prior_npc + npc
        if pl >= 2:
            p.wait("scalar", self.stg, 16 * (u - 1))
        for pc in range(npc):
            ti = prior_npc + pc
            p.wait("scalar", self.trn, ti + 1)
            def f(e, stage=self.sstage[slot], pc=pc, pslot=ti % 2, pt=self.pt):
                e.copy(stage[:, pc * 128:pc * 128 + 128],
                       pt[:, pslot * 1024:pslot * 1024 + 128]).then_inc(self.evc.h, 1)
            p.on("scalar", f)
        self.evc.v = prior_npc + npc
        if kind == "gu":
            def f(e, dst=dst, b_=b_, stage=self.sstage[slot]):
                e.dma_start(out=dst[:, :, b_ * 128:(b_ + 1) * 128].rearrange("kc pp j -> pp kc j"),
                            in_=stage[:, 0:H].rearrange("pp (kc j) -> pp kc j", kc=NKC)).then_inc(self.stg.h, 16)
        else:
            def f(e, dst=dst, b_=b_, c0=c0, fw=fw, stage=self.sstage[slot]):
                e.dma_start(out=dst[b_, :, c0:c0 + fw], in_=stage[:, 0:fw]).then_inc(self.stg.h, 16)
        p.on("scalar", f); self.stg.v += 16
        self.npc_hist.append(npc)
        self.cursor += 1
        self._t = None

    def finish_wait(self, p, eng="scalar"):
        p.wait(eng, self.stg, self.stg.v)


def _p0(nc, g, n_units=None):
    if g.remaining() == 0:
        return
    n = g.remaining() if n_units is None else min(n_units, g.remaining())
    with contextlib.ExitStack() as es:
        g.start_phase(es, width=2048)
        p = Prog()
        g.emit_units(p, n)
        if g.remaining() == 0:
            g.finish_wait(p)
        blk = es.enter_context(nc.Block("p0"))
        p.emit(blk)


def _p1a(nc, x, wk, wv, xln_s, k_cache, v_cache, ident):
    with contextlib.ExitStack() as es:
        wkb = es.enter_context(nc.sbuf_tensor("wkb", [128, NKC * KVH * D], BF16))
        wvb = es.enter_context(nc.sbuf_tensor("wvb", [128, NKC * KVH * D], BF16))
        wtmp = [es.enter_context(nc.sbuf_tensor(f"wtmp{i}", [128, KVH * D], F32)) for i in range(2)]
        xbuf = [es.enter_context(nc.sbuf_tensor(f"xbuf{i}", [128, 4 * H], F32)) for i in range(2)]
        xstat = es.enter_context(nc.sbuf_tensor("xstat", [128, 24], F32))
        xaggr = es.enter_context(nc.sbuf_tensor("xaggr", [128, 2], F32))
        nmean = es.enter_context(nc.sbuf_tensor("nmean", [128, 1], F32))
        rstd = es.enter_context(nc.sbuf_tensor("rstd", [128, 1], F32))
        xln_tm = es.enter_context(nc.sbuf_tensor("xln_tm", [128, 2 * H], BF16))
        xlnfm = es.enter_context(nc.sbuf_tensor("xlnfm", [128, NKC * TT], BF16))
        vtmp = es.enter_context(nc.sbuf_tensor("vtmp", [128, TT], BF16))
        ptx = es.enter_context(nc.psum_tensor("p1_ptx", [128, 2048], BF16))
        ptv = es.enter_context(nc.psum_tensor("p1_ptv", [128, 2048], BF16))
        pkv = es.enter_context(nc.psum_tensor("p1_pkv", [128, 2 * TT], F32))
        ld = Sem(es, nc, "p1_ld")
        ldx = Sem(es, nc, "p1_ldx")
        cst = Sem(es, nc, "p1_cst")
        lnq = Sem(es, nc, "p1_lnq")
        lnr = Sem(es, nc, "p1_lnr")
        ntm = Sem(es, nc, "p1_ntm")
        trn = Sem(es, nc, "p1_trn")
        evc = Sem(es, nc, "p1_evc")
        mmk = Sem(es, nc, "p1_mmk")
        kve = Sem(es, nc, "p1_kve")
        vtr = Sem(es, nc, "p1_vtr")
        vev = Sem(es, nc, "p1_vev")
        sx = Sem(es, nc, "p1_sx")
        p = Prog()
        NCH = 4

        for w_i, (wsrc, wdst) in enumerate(((wk, wkb), (wv, wvb))):
            for kc in range(NKC):
                j = w_i * NKC + kc
                if j >= 2:
                    p.wait("sync", cst, j - 1)
                def f(e, wsrc=wsrc, kc=kc, buf=wtmp[j % 2], ld=ld):
                    e.dma_start(out=buf[:], in_=wsrc[kc * 128:(kc + 1) * 128, :]).then_inc(ld.h, 16)
                p.on("sync", f); ld.v += 16
                p.wait("vector", ld, 16 * (j + 1))
                def f(e, wdst=wdst, kc=kc, buf=wtmp[j % 2], cst=cst):
                    e.tensor_copy(wdst[:, kc * (KVH * D):(kc + 1) * (KVH * D)], buf[:]).then_inc(cst.h, 1)
                p.on("vector", f); cst.v += 1

        for tt in range(NTILE):
            if tt >= 2:
                p.wait("sync", ntm, (tt - 1) * NCH)
            def f(e, tt=tt, buf=xbuf[tt % 2], ldx=ldx):
                e.dma_start(out=buf[:].rearrange("pp (c hh) -> pp c hh", c=4), in_=x[tt * TT:(tt + 1) * TT, :].rearrange("(c pp) hh -> pp c hh", pp=128)).then_inc(ldx.h, 16)
            p.on("sync", f); ldx.v += 16
            p.wait("vector", ldx, 16 * (tt + 1))
            for c in range(NCH):
                k = tt * NCH + c
                def f(e, xc_buf=xbuf[tt % 2], c=c):
                    xc = xc_buf[:, c * H:(c + 1) * H]
                    for i in range(4):
                        e.bn_stats(xstat[:, 6 * i:6 * (i + 1)], xc[:, i * 512:(i + 1) * 512])
                    e.drain()
                    e.bn_aggr(xaggr[:], xstat[:])
                    e.drain()
                    e.tensor_scalar(out=nmean[:], in0=xaggr[:, 0:1], scalar1=-1.0, scalar2=None, op0=ALU.mult)
                    e.tensor_scalar(out=rstd[:], in0=xaggr[:, 1:2], scalar1=EPS, scalar2=None, op0=ALU.add).then_inc(lnq.h, 1)
                p.on("vector", f); lnq.v += 1
                p.wait("scalar", lnq, lnq.v)
                def f(e):
                    e.activation(out=rstd[:], in_=rstd[:], func=AF.Ln)
                    e.drain()
                    e.activation(out=rstd[:], in_=rstd[:], func=AF.Exp, scale=-0.5).then_inc(lnr.h, 1)
                p.on("scalar", f); lnr.v += 1
                p.wait("vector", lnr, lnr.v)
                if k >= 2:
                    p.wait("vector", trn, (k - 1) * NKC)
                def f(e, xc_buf=xbuf[tt % 2], c=c):
                    xc = xc_buf[:, c * H:(c + 1) * H]
                    e.tensor_scalar(out=xln_tm[:, (c % 2) * H:(c % 2) * H + H], in0=xc,
                                    scalar1=nmean[:], scalar2=rstd[:], op0=ALU.add, op1=ALU.mult).then_inc(ntm.h, 1)
                p.on("vector", f); ntm.v += 1
                p.wait("tensor", ntm, ntm.v)
                for kc in range(NKC):
                    ti = k * NKC + kc
                    if ti >= 2:
                        p.wait("tensor", evc, ti - 1)
                    def f(e, c=c, kc=kc, slot=ti % 2):
                        e.transpose(ptx[:, slot * 1024:slot * 1024 + 128],
                                    xln_tm[:, (c % 2) * H + kc * 128:(c % 2) * H + (kc + 1) * 128],
                                    ident[:]).then_inc(trn.h, 1)
                    p.on("tensor", f)
                trn.v = (k + 1) * NKC
                for kc in range(NKC):
                    ti = k * NKC + kc
                    p.wait("scalar", trn, ti + 1)
                    def f(e, c=c, kc=kc, slot=ti % 2):
                        e.copy(xlnfm[:, kc * TT + c * 128:kc * TT + c * 128 + 128],
                               ptx[:, slot * 1024:slot * 1024 + 128]).then_inc(evc.h, 1)
                    p.on("scalar", f)
                evc.v = (k + 1) * NKC
            p.wait("tensor", evc, evc.v)
            if tt == 0:
                p.wait("tensor", cst, 2 * NKC)
            for w_i, wsb in enumerate((wkb, wvb)):
                for kvh in range(KVH):
                    mi = tt * 2 * KVH + w_i * KVH + kvh
                    if mi >= 2:
                        p.wait("tensor", kve, mi - 1)
                    for kc in range(NKC):
                        last = (kc == NKC - 1)
                        def f(e, wsb=wsb, kvh=kvh, kc=kc, slot=mi % 2, last=last):
                            i_ = e.matmul(pkv[:, slot * TT:slot * TT + TT],
                                          wsb[:, kc * (KVH * D) + kvh * D:kc * (KVH * D) + (kvh + 1) * D],
                                          xlnfm[:, kc * TT:(kc + 1) * TT], start=(kc == 0), stop=last)
                            if last:
                                i_.then_inc(mmk.h, 1)
                        p.on("tensor", f)
                    mmk.v += 1
                    p.wait("scalar", mmk, mmk.v)
                    if w_i == 0:
                        def f(e, kvh=kvh, tt=tt, slot=mi % 2):
                            e.copy(k_cache[:, kvh * S + tt * TT:kvh * S + (tt + 1) * TT],
                                   pkv[:, slot * TT:slot * TT + TT]).then_inc(kve.h, 1)
                        p.on("scalar", f); kve.v += 1
                    else:
                        def f(e, slot=mi % 2):
                            e.copy(vtmp[:], pkv[:, slot * TT:slot * TT + TT]).then_inc(kve.h, 1)
                        p.on("scalar", f); kve.v += 1
                        p.wait("tensor", kve, kve.v)
                        for c in range(NCH):
                            vi = vtr.v
                            if vi >= 2:
                                p.wait("tensor", vev, vi - 1)
                            def f(e, c=c, slot=vi % 2):
                                e.transpose(ptv[:, slot * 1024:slot * 1024 + 128],
                                            vtmp[:, c * 128:(c + 1) * 128], ident[:]).then_inc(vtr.h, 1)
                            p.on("tensor", f); vtr.v += 1
                            p.wait("scalar", vtr, vtr.v)
                            def f(e, kvh=kvh, tt=tt, c=c, slot=vi % 2):
                                e.copy(v_cache[:, kvh * S + tt * TT + c * 128:kvh * S + tt * TT + (c + 1) * 128],
                                       ptv[:, slot * 1024:slot * 1024 + 128]).then_inc(vev.h, 1)
                            p.on("scalar", f); vev.v += 1
                        p.wait("scalar", vtr, vtr.v)
            p.wait("gpsimd", evc, evc.v)
            def f(e, tt=tt):
                e.dma_start(out=xln_s[:, tt * TT:(tt + 1) * TT].rearrange("(kc pp) t -> pp kc t", pp=128),
                            in_=xlnfm[:].rearrange("pp (kc t) -> pp kc t", kc=NKC)).then_inc(sx.h, 16)
            p.on("gpsimd", f); sx.v += 16
            p.wait("scalar", sx, sx.v)
            p.wait("scalar", mmk, mmk.v)
        p.wait("gpsimd", sx, sx.v)
        blk = es.enter_context(nc.Block("p1a"))
        p.emit(blk)


def _p1b(nc, wq, xln_s, q_s):
    with contextlib.ExitStack() as es:
        wqb = es.enter_context(nc.sbuf_tensor("wqb", [128, NKC * H], BF16))
        wtmp = [es.enter_context(nc.sbuf_tensor(f"qwt{i}", [128, H], F32)) for i in range(2)]
        xt = [es.enter_context(nc.sbuf_tensor(f"qxt{i}", [128, NKC * TT], BF16)) for i in range(2)]
        qstage = es.enter_context(nc.sbuf_tensor("qstage", [128, NH * TT], BF16))
        pq = es.enter_context(nc.psum_tensor("pq", [128, 2 * TT], F32))
        ld = Sem(es, nc, "q_ld")
        cst = Sem(es, nc, "q_cst")
        ldx = Sem(es, nc, "q_ldx")
        mm = Sem(es, nc, "q_mm")
        qe = Sem(es, nc, "q_qe")
        st = Sem(es, nc, "q_st")
        p = Prog()
        for kc in range(NKC):
            if kc >= 2:
                p.wait("sync", cst, kc - 1)
            def f(e, kc=kc, buf=wtmp[kc % 2], ld=ld):
                e.dma_start(out=buf[:], in_=wq[kc * 128:(kc + 1) * 128, :]).then_inc(ld.h, 16)
            p.on("sync", f); ld.v += 16
            p.wait("vector", ld, 16 * (kc + 1))
            def f(e, kc=kc, buf=wtmp[kc % 2], cst=cst):
                e.tensor_copy(wqb[:, kc * H:(kc + 1) * H], buf[:]).then_inc(cst.h, 1)
            p.on("vector", f); cst.v += 1

        for tt in range(NTILE):
            if tt >= 2:
                p.wait("sync", mm, (tt - 1) * NH)
            def f(e, tt=tt, buf=xt[tt % 2], ldx=ldx):
                e.dma_start(out=buf[:].rearrange("pp (kc t) -> pp kc t", kc=NKC), in_=xln_s[:, tt * TT:(tt + 1) * TT].rearrange("(kc pp) t -> pp kc t", pp=128)).then_inc(ldx.h, 16)
            p.on("sync", f); ldx.v += 16
            p.wait("tensor", ldx, 16 * (tt + 1))
            if tt == 0:
                p.wait("tensor", cst, NKC)
            if tt >= 1:
                p.wait("scalar", st, 16 * tt)
            for m in range(NH):
                mi = tt * NH + m
                if mi >= 2:
                    p.wait("tensor", qe, mi - 1)
                for kc in range(NKC):
                    last = (kc == NKC - 1)
                    def f(e, m=m, kc=kc, buf=xt[tt % 2], slot=mi % 2, last=last):
                        i_ = e.matmul(pq[:, slot * TT:slot * TT + TT],
                                      wqb[:, kc * H + m * 128:kc * H + (m + 1) * 128],
                                      buf[:, kc * TT:(kc + 1) * TT], start=(kc == 0), stop=last)
                        if last:
                            i_.then_inc(mm.h, 1)
                    p.on("tensor", f)
                mm.v += 1
                p.wait("scalar", mm, mm.v)
                def f(e, m=m, slot=mi % 2):
                    e.copy(qstage[:, m * TT:(m + 1) * TT], pq[:, slot * TT:slot * TT + TT]).then_inc(qe.h, 1)
                p.on("scalar", f); qe.v += 1
            p.wait("gpsimd", qe, qe.v)
            def f(e, tt=tt):
                e.dma_start(out=q_s[:, tt * TT:(tt + 1) * TT].rearrange("(kc pp) t -> pp kc t", pp=128),
                            in_=qstage[:].rearrange("pp (kc t) -> pp kc t", kc=NKC)).then_inc(st.h, 16)
            p.on("gpsimd", f); st.v += 16
        p.wait("gpsimd", st, st.v)
        blk = es.enter_context(nc.Block("p1b"))
        p.emit(blk)


def _p2(nc, q_s, ao_s, k_cache, v_cache, ones_col, ones_row, gath=None, ident=None):
    partial_D = (-384, -256, -128, 0, 640, 768, 896, 1024)
    with contextlib.ExitStack() as es:
        qt_b = [es.enter_context(nc.sbuf_tensor(f"qt{i}", [128, NH * TT], BF16)) for i in range(2)]
        probs = es.enter_context(nc.sbuf_tensor("probs", [128, 3 * TT], BF16))
        masks = es.enter_context(nc.sbuf_tensor("masks", [128, 8 * TT], BF16))
        m_io = es.enter_context(nc.sbuf_tensor("m_io", [128, TT], I32))
        m_t1 = es.enter_context(nc.sbuf_tensor("m_t1", [128, TT], F32))
        m_t2 = es.enter_context(nc.sbuf_tensor("m_t2", [128, TT], F32))
        onehot = es.enter_context(nc.sbuf_tensor("onehot", [128, NH * NH], BF16))
        oh_io = es.enter_context(nc.sbuf_tensor("oh_io", [128, NH], I32))
        ohrow = es.enter_context(nc.sbuf_tensor("ohrow", [16, NH * 128], F32))
        ohr_io = es.enter_context(nc.sbuf_tensor("ohr_io", [16, 128], I32))
        srec = es.enter_context(nc.sbuf_tensor("srec", [NH, TT], F32))
        ao_stage = es.enter_context(nc.sbuf_tensor("ao_stage", [128, NH * TT], BF16))
        pS = es.enter_context(nc.psum_tensor("pS", [128, 3 * TT], F32))
        pAV = es.enter_context(nc.psum_tensor("pAV", [128, TT], F32))
        pDen = es.enter_context(nc.psum_tensor("pDen", [128, TT], F32))
        pB = es.enter_context(nc.psum_tensor("pB", [128, TT], F32))
        ldq = Sem(es, nc, "a_ldq")
        msk = Sem(es, nc, "a_msk")
        sS = Sem(es, nc, "a_sS")
        sM = Sem(es, nc, "a_sM")
        sP = Sem(es, nc, "a_sP")
        sAV = Sem(es, nc, "a_sAV")
        sAVe = Sem(es, nc, "a_sAVe")
        sDen = Sem(es, nc, "a_sDen")
        sRec = Sem(es, nc, "a_sRec")
        sB = Sem(es, nc, "a_sB")
        sSc = Sem(es, nc, "a_sSc")
        sAOD = Sem(es, nc, "a_sAOD")
        p = Prog()
        if gath is not None:
            gath.start_phase(es, width=2048)

        D_list = sorted(partial_D)
        mask_off = {Dv: i * TT for i, Dv in enumerate(D_list)}
        def f(e):
            e.iota(m_io[:], pattern=[[1, TT]], base=128, channel_multiplier=-1).then_inc(msk.h, 1)
            e.iota(oh_io[:], pattern=[[1, NH]], base=0, channel_multiplier=0).then_inc(msk.h, 1)
            e.iota(ohr_io[:], pattern=[[0, 128]], base=0, channel_multiplier=1).then_inc(msk.h, 1)
        p.on("gpsimd", f); msk.v += 3
        p.wait("vector", msk, msk.v)
        def f(e):
            for i, Dv in enumerate(D_list):
                e.tensor_scalar(out=m_t1[:], in0=m_io[:], scalar1=float(128 - Dv), scalar2=None, op0=ALU.is_ge)
                e.tensor_scalar(out=m_t2[:], in0=m_io[:], scalar1=float(1152 - Dv), scalar2=None, op0=ALU.is_le)
                e.drain()
                e.tensor_tensor(out=m_t1[:], in0=m_t1[:], in1=m_t2[:], op=ALU.mult)
                e.drain()
                e.tensor_scalar(out=masks[:, i * TT:(i + 1) * TT], in0=m_t1[:],
                                scalar1=-1.0, scalar2=-NEG, op0=ALU.add, op1=ALU.mult)
                e.drain()
            for h in range(NH):
                e.tensor_scalar(out=onehot[:, h * NH:(h + 1) * NH], in0=oh_io[:],
                                scalar1=float(h), scalar2=None, op0=ALU.is_equal)
                e.tensor_scalar(out=ohrow[:, h * 128:(h + 1) * 128], in0=ohr_io[:],
                                scalar1=float(h), scalar2=None, op0=ALU.is_equal)
            e.drain()
            e.nop().then_inc(msk.h, 1)
        p.on("vector", f); msk.v += 1

        chunk_i = 0
        for qt in range(NTILE):
            qt0 = qt * TT
            if qt >= 2:
                p.wait("sync", sAOD, 16 * (qt - 1))
            def f(e, qt=qt, buf=qt_b[qt % 2], ldq=ldq):
                e.dma_start(out=buf[:].rearrange("pp (kc t) -> pp kc t", kc=NKC), in_=q_s[:, qt * TT:(qt + 1) * TT].rearrange("(kc pp) t -> pp kc t", pp=128)).then_inc(ldq.h, 16)
            p.on("sync", f); ldq.v += 16
            p.wait("tensor", ldq, 16 * (qt + 1))
            if qt == 0:
                p.wait("tensor", msk, msk.v)
            if qt >= 1:
                p.wait("scalar", sAOD, 16 * qt)
            k0s = list(range(max(0, qt0 - WIN), qt0 + TT - 128 + 1, 128))
            nch = len(k0s)
            for h in range(NH):
                kv = h // (NH // KVH)

                def emit_qk(ci, base_ci):
                    k0 = k0s[ci]
                    gci = base_ci + ci
                    slot = gci % 3
                    Dv = qt0 - k0
                    masked = Dv in partial_D
                    if gci >= 3:
                        p.wait("tensor", sP, gci - 2)
                    def f(e, kv=kv, k0=k0, h=h, buf=qt_b[qt % 2], slot=slot, masked=masked,
                          moff=(mask_off[Dv] if masked else 0)):
                        i_ = e.matmul(pS[:, slot * TT:slot * TT + TT],
                                      k_cache[:, kv * S + k0:kv * S + k0 + 128],
                                      buf[:, h * TT:(h + 1) * TT], start=True, stop=not masked)
                        if masked:
                            i_ = e.matmul(pS[:, slot * TT:slot * TT + TT], ident[:],
                                          masks[:, moff:moff + TT], start=False, stop=True)
                        i_.then_inc(sS.h, 1)
                    p.on("tensor", f); sS.v += 1
                    p.wait("scalar", sS, sS.v)
                    if gci >= 3:
                        p.wait("scalar", sAV, gci - 2)
                    def f(e, slot=slot):
                        e.activation(out=probs[:, slot * TT:slot * TT + TT],
                                     in_=pS[:, slot * TT:slot * TT + TT], func=AF.Exp,
                                     scale=SCALE).then_inc(sP.h, 1)
                    p.on("scalar", f); sP.v += 1

                def emit_av(ci, base_ci):
                    k0 = k0s[ci]
                    gci = base_ci + ci
                    slot = gci % 3
                    p.wait("tensor", sP, gci + 1)
                    is_first = (ci == 0)
                    is_last = (ci == nch - 1)
                    if is_first and qt * NH + h >= 1:
                        p.wait("tensor", sAVe, qt * NH + h)
                    def f(e, slot=slot, kv=kv, k0=k0, h=h,
                          is_first=is_first, is_last=is_last):
                        i_d = e.matmul(pDen[0:NH, :], onehot[:, h * NH:(h + 1) * NH],
                                       probs[:, slot * TT:slot * TT + TT],
                                       start=(h == 0 and is_first), stop=(h == NH - 1 and is_last))
                        if h == NH - 1 and is_last:
                            i_d.then_inc(sDen.h, 1)
                        e.matmul(pAV[:, :], v_cache[:, kv * S + k0:kv * S + k0 + 128],
                                 probs[:, slot * TT:slot * TT + TT],
                                 start=is_first, stop=is_last).then_inc(sAV.h, 1)
                    p.on("tensor", f); sAV.v += 1
                    if h == NH - 1 and is_last:
                        sDen.v += 1

                base_ci = chunk_i
                emit_qk(0, base_ci)
                if nch > 1:
                    emit_qk(1, base_ci)
                for ci in range(nch):
                    if ci + 2 < nch:
                        emit_qk(ci + 2, base_ci)
                    emit_av(ci, base_ci)
                chunk_i += nch
                p.wait("scalar", sAV, sAV.v)
                def f(e, h=h):
                    e.copy(ao_stage[:, h * TT:(h + 1) * TT], pAV[:, :]).then_inc(sAVe.h, 1)
                p.on("scalar", f); sAVe.v += 1
                if gath is not None:
                    for _ in range(4):
                        gath.emit_slice(p, 2)
            p.wait("vector", sDen, sDen.v)
            def f(e):
                e.reciprocal(srec[:], pDen[0:NH, :]).then_inc(sRec.h, 1)
            p.on("vector", f); sRec.v += 1
            p.wait("tensor", sRec, sRec.v)
            for h in range(NH):
                if sB.v >= 1:
                    p.wait("tensor", sSc, sB.v)
                def f(e, h=h):
                    e.matmul(pB[:, :], ohrow[:, h * 128:(h + 1) * 128], srec[:],
                             start=True, stop=True).then_inc(sB.h, 1)
                p.on("tensor", f); sB.v += 1
                p.wait("vector", sB, sB.v)
                p.wait("vector", sAVe, sAVe.v - (NH - 1 - h))
                def f(e, h=h):
                    e.tensor_tensor(out=ao_stage[:, h * TT:(h + 1) * TT],
                                    in0=ao_stage[:, h * TT:(h + 1) * TT],
                                    in1=pB[:, :], op=ALU.mult).then_inc(sSc.h, 1)
                p.on("vector", f); sSc.v += 1
            p.wait("gpsimd", sSc, sSc.v)
            def f(e, qt=qt):
                e.dma_start(out=ao_s[:, qt * TT:(qt + 1) * TT].rearrange("(kc pp) t -> pp kc t", pp=128),
                            in_=ao_stage[:].rearrange("pp (kc t) -> pp kc t", kc=NKC)).then_inc(sAOD.h, 16)
            p.on("gpsimd", f); sAOD.v += 16
            p.wait("vector", sAOD, sAOD.v)
        if gath is not None:
            gath.flush_phase(p)
        p.wait("gpsimd", sAOD, sAOD.v)
        blk = es.enter_context(nc.Block("p2"))
        p.emit(blk)


def _p3(nc, x, wo, ao_s, x1_s, y_s, ident, gath=None):
    with contextlib.ExitStack() as es:
        wob = es.enter_context(nc.sbuf_tensor("wob", [128, NKC * H], BF16))
        wtmp = [es.enter_context(nc.sbuf_tensor(f"owt{i}", [128, H], F32)) for i in range(2)]
        aot = [es.enter_context(nc.sbuf_tensor(f"aot{i}", [128, NKC * TT], BF16)) for i in range(2)]
        obuf = es.enter_context(nc.sbuf_tensor("obuf", [128, NH * TT], BF16))
        oT = es.enter_context(nc.sbuf_tensor("oT", [128, H], BF16))
        xc_b = [es.enter_context(nc.sbuf_tensor(f"p3x{i}", [128, H], F32)) for i in range(2)]
        x1c = [es.enter_context(nc.sbuf_tensor(f"x1c{i}", [128, H], F32)) for i in range(2)]
        ytm = es.enter_context(nc.sbuf_tensor("ytm", [128, H], BF16))
        yfm = es.enter_context(nc.sbuf_tensor("yfm", [128, NKC * TT], BF16))
        xstat = es.enter_context(nc.sbuf_tensor("p3stat", [128, 24], F32))
        xaggr = es.enter_context(nc.sbuf_tensor("p3aggr", [128, 2], F32))
        nmean = es.enter_context(nc.sbuf_tensor("p3nm", [128, 1], F32))
        rstd = es.enter_context(nc.sbuf_tensor("p3rs", [128, 1], F32))
        po = es.enter_context(nc.psum_tensor("po", [128, 2 * TT], F32))
        pto = es.enter_context(nc.psum_tensor("p3pto", [128, 2048], BF16))
        pty = es.enter_context(nc.psum_tensor("p3pty", [128, 2048], BF16))
        ld = Sem(es, nc, "o_ld")
        cst = Sem(es, nc, "o_cst")
        lda = Sem(es, nc, "o_lda")
        ldxc = Sem(es, nc, "o_ldxc")
        mm = Sem(es, nc, "o_mm")
        oev = Sem(es, nc, "o_oev")
        otr = Sem(es, nc, "o_otr")
        ote = Sem(es, nc, "o_ote")
        sx1 = Sem(es, nc, "o_sx1")
        sx1d = Sem(es, nc, "o_sx1d")
        lnq = Sem(es, nc, "o_lnq")
        lnr = Sem(es, nc, "o_lnr")
        syt = Sem(es, nc, "o_syt")
        ytr = Sem(es, nc, "o_ytr")
        yev = Sem(es, nc, "o_yev")
        syd = Sem(es, nc, "o_syd")
        p = Prog()
        NCH = 4

        for kc in range(NKC):
            if kc >= 2:
                p.wait("sync", cst, kc - 1)
            def f(e, kc=kc, buf=wtmp[kc % 2], ld=ld):
                e.dma_start(out=buf[:], in_=wo[kc * 128:(kc + 1) * 128, :]).then_inc(ld.h, 16)
            p.on("sync", f); ld.v += 16
            p.wait("vector", ld, 16 * (kc + 1))
            def f(e, kc=kc, buf=wtmp[kc % 2], cst=cst):
                e.tensor_copy(wob[:, kc * H:(kc + 1) * H], buf[:]).then_inc(cst.h, 1)
            p.on("vector", f); cst.v += 1

        for tt in range(NTILE):
            if tt >= 2:
                p.wait("sync", mm, (tt - 1) * NH)
            def f(e, tt=tt, buf=aot[tt % 2], lda=lda):
                e.dma_start(out=buf[:].rearrange("pp (kc t) -> pp kc t", kc=NKC), in_=ao_s[:, tt * TT:(tt + 1) * TT].rearrange("(kc pp) t -> pp kc t", pp=128)).then_inc(lda.h, 16)
            p.on("sync", f); lda.v += 16
            for c in range(NCH):
                k = tt * NCH + c
                if k >= 2:
                    p.wait("sync", sx1, k - 1)
                def f(e, tt=tt, c=c, buf=xc_b[k % 2], ldxc=ldxc):
                    e.dma_start(out=buf[:], in_=x[tt * TT + c * 128:tt * TT + (c + 1) * 128, :]).then_inc(ldxc.h, 16)
                p.on("sync", f); ldxc.v += 16
            p.wait("tensor", lda, 16 * (tt + 1))
            if tt == 0:
                p.wait("tensor", cst, NKC)
            if tt >= 1:
                p.wait("scalar", otr, tt * NCH * NKC)
            for m in range(NH):
                mi = tt * NH + m
                if mi >= 2:
                    p.wait("tensor", oev, mi - 1)
                for kc in range(NKC):
                    last = (kc == NKC - 1)
                    def f(e, m=m, kc=kc, buf=aot[tt % 2], slot=mi % 2, last=last):
                        i_ = e.matmul(po[:, slot * TT:slot * TT + TT],
                                      wob[:, kc * H + m * 128:kc * H + (m + 1) * 128],
                                      buf[:, kc * TT:(kc + 1) * TT], start=(kc == 0), stop=last)
                        if last:
                            i_.then_inc(mm.h, 1)
                    p.on("tensor", f)
                mm.v += 1
                p.wait("scalar", mm, mm.v)
                def f(e, m=m, slot=mi % 2):
                    e.copy(obuf[:, m * TT:(m + 1) * TT], po[:, slot * TT:slot * TT + TT]).then_inc(oev.h, 1)
                p.on("scalar", f); oev.v += 1
            p.wait("tensor", oev, oev.v)
            for c in range(NCH):
                k = tt * NCH + c
                for mch in range(NKC):
                    ti = k * NKC + mch
                    if ti >= 2:
                        p.wait("tensor", ote, ti - 1)
                    def f(e, c=c, mch=mch, slot=ti % 2):
                        e.transpose(pto[:, slot * 1024:slot * 1024 + 128],
                                    obuf[:, mch * TT + c * 128:mch * TT + c * 128 + 128],
                                    ident[:]).then_inc(otr.h, 1)
                    p.on("tensor", f)
                    p.wait("scalar", otr, ti + 1)
                    def f(e, mch=mch, slot=ti % 2):
                        e.copy(oT[:, mch * 128:(mch + 1) * 128],
                               pto[:, slot * 1024:slot * 1024 + 128]).then_inc(ote.h, 1)
                    p.on("scalar", f); ote.v += 1
                otr.v = (k + 1) * NKC
                p.wait("vector", ote, ote.v)
                p.wait("vector", ldxc, 16 * (k + 1))
                if k >= 2:
                    p.wait("vector", sx1d, 16 * (k - 1))
                def f(e, k=k):
                    e.tensor_tensor(out=x1c[k % 2][:], in0=xc_b[k % 2][:], in1=oT[:], op=ALU.add).then_inc(sx1.h, 1)
                    e.drain()
                    for i in range(4):
                        e.bn_stats(xstat[:, 6 * i:6 * (i + 1)], x1c[k % 2][:, i * 512:(i + 1) * 512])
                    e.drain()
                    e.bn_aggr(xaggr[:], xstat[:])
                    e.drain()
                    e.tensor_scalar(out=nmean[:], in0=xaggr[:, 0:1], scalar1=-1.0, scalar2=None, op0=ALU.mult)
                    e.tensor_scalar(out=rstd[:], in0=xaggr[:, 1:2], scalar1=EPS, scalar2=None, op0=ALU.add).then_inc(lnq.h, 1)
                p.on("vector", f); sx1.v += 1; lnq.v += 1
                p.wait("scalar", sx1, sx1.v)
                p.wait("scalar", lnq, lnq.v)
                def f(e):
                    e.activation(out=rstd[:], in_=rstd[:], func=AF.Ln)
                    e.drain()
                    e.activation(out=rstd[:], in_=rstd[:], func=AF.Exp, scale=-0.5).then_inc(lnr.h, 1)
                p.on("scalar", f); lnr.v += 1
                p.wait("vector", lnr, lnr.v)
                if k >= 1:
                    p.wait("vector", ytr, k * NKC)
                def f(e, k=k):
                    e.tensor_scalar(out=ytm[:], in0=x1c[k % 2][:], scalar1=nmean[:], scalar2=rstd[:],
                                    op0=ALU.add, op1=ALU.mult).then_inc(syt.h, 1)
                p.on("vector", f); syt.v += 1
                p.wait("gpsimd", sx1, sx1.v)
                def f(e, tt=tt, c=c, k=k):
                    e.dma_start(out=x1_s[tt * TT + c * 128:tt * TT + (c + 1) * 128, :], in_=x1c[k % 2][:]).then_inc(sx1d.h, 16)
                p.on("gpsimd", f); sx1d.v += 16
                p.wait("tensor", syt, syt.v)
                for kc in range(NKC):
                    ti = k * NKC + kc
                    if ti >= 2:
                        p.wait("tensor", yev, ti - 1)
                    def f(e, kc=kc, slot=ti % 2):
                        e.transpose(pty[:, slot * 1024:slot * 1024 + 128],
                                    ytm[:, kc * 128:(kc + 1) * 128], ident[:]).then_inc(ytr.h, 1)
                    p.on("tensor", f)
                    p.wait("scalar", ytr, ti + 1)
                    def f(e, kc=kc, c=c, slot=ti % 2):
                        e.copy(yfm[:, kc * TT + c * 128:kc * TT + c * 128 + 128],
                               pty[:, slot * 1024:slot * 1024 + 128]).then_inc(yev.h, 1)
                    p.on("scalar", f); yev.v += 1
                ytr.v = (k + 1) * NKC
            p.wait("gpsimd", yev, yev.v)
            def f(e, tt=tt):
                e.dma_start(out=y_s[:, tt * TT:(tt + 1) * TT].rearrange("(kc pp) t -> pp kc t", pp=128),
                            in_=yfm[:].rearrange("pp (kc t) -> pp kc t", kc=NKC)).then_inc(syd.h, 16)
            p.on("gpsimd", f); syd.v += 16
            p.wait("scalar", syd, syd.v)
        p.wait("gpsimd", syd, syd.v)
        p.wait("gpsimd", sx1d, sx1d.v)
        blk = es.enter_context(nc.Block("p3"))
        p.emit(blk)


DBG_P4 = []
def _p4(nc, y_s, x1_s, wg_s, wu_s, wd_s, out, ident):
    TT4 = 1024
    NT4 = S // TT4
    NG = 1
    n_groups = (NIC + NG - 1) // NG
    with contextlib.ExitStack() as es:
        yfm = es.enter_context(nc.sbuf_tensor("f_yfm", [128, NKC * TT4], BF16))
        hb = es.enter_context(nc.sbuf_tensor("f_h", [128, NIC * TT4], BF16))
        wgu = [es.enter_context(nc.sbuf_tensor(f"f_wgu{i}", [128, 2 * NKC * NG * 128], BF16)) for i in range(2)]
        wdb = [es.enter_context(nc.sbuf_tensor(f"f_wd{i}", [128, INTER], BF16)) for i in range(2)]
        sgl = es.enter_context(nc.sbuf_tensor("f_sgl", [128, 512], F32))
        dstage = es.enter_context(nc.sbuf_tensor("f_dst", [128, 2 * NKC * 512], BF16))
        dT = es.enter_context(nc.sbuf_tensor("f_dT", [128, H], BF16))
        x1b = [es.enter_context(nc.sbuf_tensor("f_x1", [128, H], F32))] * 2
        
        pgu = es.enter_context(nc.psum_tensor("f_pgu", [128, 4 * 512], F32))
        pd = es.enter_context(nc.psum_tensor("f_pd", [128, 2 * 512], F32))
        ptd = es.enter_context(nc.psum_tensor("f_ptd", [128, 2048], BF16))
        ldy = Sem(es, nc, "f_ldy")
        ldw = Sem(es, nc, "f_ldw")
        ldd = Sem(es, nc, "f_ldd")
        ldx1 = Sem(es, nc, "f_ldx1")
        mmg = Sem(es, nc, "f_mmg")
        sil = Sem(es, nc, "f_sil")
        hwr = Sem(es, nc, "f_hwr")
        mmd = Sem(es, nc, "f_mmd")
        dev = Sem(es, nc, "f_dev")
        dtr = Sem(es, nc, "f_dtr")
        dte = Sem(es, nc, "f_dte")
        sou = Sem(es, nc, "f_sou")
        sod = Sem(es, nc, "f_sod")
        p = Prog()

        grp_ics = [list(range(g * NG, min((g + 1) * NG, NIC))) for g in range(n_groups)]
        grp_units = [2 * len(g) for g in grp_ics]
        cumu = [0]
        for g in grp_units:
            cumu.append(cumu[-1] + g)
        units_per_tile = cumu[-1]
        gcum = [0]
        for t4_ in range(NT4):
            for g_ in range(n_groups):
                gcum.append(gcum[-1] + grp_units[g_])

        for t4 in range(NT4):
            if t4 >= 1:
                p.wait("sync", hwr, t4 * units_per_tile)
                p.wait("vector", mmd, t4 * NKC * 2)
            def f(e, t4=t4):
                e.dma_start(out=yfm[:].rearrange("pp (kc t) -> pp kc t", kc=NKC), in_=y_s[:, t4 * TT4:(t4 + 1) * TT4].rearrange("(kc pp) t -> pp kc t", pp=128)).then_inc(ldy.h, 16)
            p.on("sync", f); ldy.v += 16
            for gi in range(n_groups):
                ics = grp_ics[gi]
                n = len(ics)
                gj = t4 * n_groups + gi
                gslot = gj % 2
                if gj >= 2:
                    p.wait("sync", hwr, gcum[gj - 1])
                def f(e, ics=ics, n=n, buf=wgu[gslot]):
                    e.dma_start(out=buf[:, 0:NKC * n * 128].rearrange("pp (kc j) -> pp kc j", kc=NKC),
                                in_=wg_s[:, :, ics[0] * 128:ics[0] * 128 + n * 128].rearrange("kc pp j -> pp kc j")).then_inc(ldw.h, 16)
                    e.dma_start(out=buf[:, NKC * NG * 128:NKC * NG * 128 + NKC * n * 128].rearrange("pp (kc j) -> pp kc j", kc=NKC),
                                in_=wu_s[:, :, ics[0] * 128:ics[0] * 128 + n * 128].rearrange("kc pp j -> pp kc j")).then_inc(ldw.h, 16)
                p.on("sync", f); ldw.v += 32
                p.wait("tensor", ldw, ldw.v)
                if gi == 0:
                    p.wait("tensor", ldy, ldy.v)
                for icl, ic in enumerate(ics):
                    for tc in range(2):
                        ui = t4 * units_per_tile + cumu[gi] + icl * 2 + tc
                        gs = ui % 2
                        if ui >= 2:
                            p.wait("tensor", sil, ui - 1)
                            p.wait("tensor", hwr, ui - 1)
                        for kc in range(NKC):
                            last = (kc == NKC - 1)
                            def f(e, icl=icl, kc=kc, tc=tc, gs=gs, gslot=gslot, last=last, n=n):
                                e.matmul(pgu[:, gs * 512:gs * 512 + 512],
                                         wgu[gslot][:, (kc * n + icl) * 128:(kc * n + icl) * 128 + 128],
                                         yfm[:, kc * TT4 + tc * 512:kc * TT4 + tc * 512 + 512],
                                         start=(kc == 0), stop=last)
                                i_ = e.matmul(pgu[:, (2 + gs) * 512:(2 + gs) * 512 + 512],
                                              wgu[gslot][:, NKC * NG * 128 + (kc * n + icl) * 128:NKC * NG * 128 + (kc * n + icl) * 128 + 128],
                                              yfm[:, kc * TT4 + tc * 512:kc * TT4 + tc * 512 + 512],
                                              start=(kc == 0), stop=last)
                                if last:
                                    i_.then_inc(mmg.h, 1)
                            p.on("tensor", f)
                        mmg.v += 1
                        p.wait("scalar", mmg, mmg.v)
                        if ui >= 1:
                            p.wait("scalar", hwr, ui)
                        def f(e, gs=gs):
                            e.activation(out=sgl[:], in_=pgu[:, gs * 512:gs * 512 + 512], func=AF.Silu).then_inc(sil.h, 1)
                        p.on("scalar", f); sil.v += 1
                        p.wait("vector", sil, sil.v)
                        def f(e, ic=ic, tc=tc, gs=gs):
                            e.tensor_tensor(out=hb[:, ic * TT4 + tc * 512:ic * TT4 + tc * 512 + 512],
                                            in0=sgl[:], in1=pgu[:, (2 + gs) * 512:(2 + gs) * 512 + 512],
                                            op=ALU.mult).then_inc(hwr.h, 1)
                        p.on("vector", f); hwr.v += 1
            for m in range(NKC):
                md = t4 * NKC + m
                if md >= 2:
                    p.wait("sync", mmd, (md - 1) * 2)
                def f(e, m=m, buf=wdb[md % 2]):
                    e.dma_start(out=buf[:], in_=wd_s[m, :, :]).then_inc(ldd.h, 16)
                p.on("sync", f); ldd.v += 16
                p.wait("tensor", ldd, ldd.v)
                if m == 0:
                    p.wait("tensor", hwr, (t4 + 1) * units_per_tile)
                for tc in range(2):
                    di = md * 2 + tc
                    if di >= 2:
                        p.wait("tensor", dev, di - 1)
                    for ic in range(NIC):
                        last = (ic == NIC - 1)
                        def f(e, ic=ic, tc=tc, slot=di % 2, buf=wdb[md % 2], last=last):
                            i_ = e.matmul(pd[:, slot * 512:slot * 512 + 512],
                                          buf[:, ic * 128:(ic + 1) * 128],
                                          hb[:, ic * TT4 + tc * 512:ic * TT4 + tc * 512 + 512],
                                          start=(ic == 0), stop=last)
                            if last:
                                i_.then_inc(mmd.h, 1)
                        p.on("tensor", f)
                    mmd.v += 1
                    p.wait("scalar", mmd, mmd.v)
                    if t4 >= 1 and di < 2:
                        p.wait("scalar", dtr, t4 * 8 * NKC)
                    def f(e, m=m, tc=tc, slot=di % 2):
                        e.copy(dstage[:, tc * NKC * 512 + m * 512:tc * NKC * 512 + m * 512 + 512],
                               pd[:, slot * 512:slot * 512 + 512]).then_inc(dev.h, 1)
                    p.on("scalar", f); dev.v += 1
            p.wait("tensor", dev, dev.v)
            for tc in range(2):
                for c in range(4):
                    k = t4 * 8 + tc * 4 + c
                    if k >= 1:
                        p.wait("sync", sou, k)
                        p.wait("sync", sod, 16 * k)
                    def f(e, t4=t4, tc=tc, c=c, buf=x1b[k % 2]):
                        e.dma_start(out=buf[:], in_=x1_s[t4 * TT4 + tc * 512 + c * 128:t4 * TT4 + tc * 512 + (c + 1) * 128, :]).then_inc(ldx1.h, 16)
                    p.on("sync", f); ldx1.v += 16
                    for m in range(NKC):
                        ti = k * NKC + m
                        if ti >= 2:
                            p.wait("tensor", dte, ti - 1)
                        def f(e, m=m, tc=tc, c=c, slot=ti % 2):
                            e.transpose(ptd[:, slot * 1024:slot * 1024 + 128],
                                        dstage[:, tc * NKC * 512 + m * 512 + c * 128:tc * NKC * 512 + m * 512 + c * 128 + 128],
                                        ident[:]).then_inc(dtr.h, 1)
                        p.on("tensor", f)
                        p.wait("scalar", dtr, ti + 1)
                        def f(e, m=m, slot=ti % 2):
                            e.copy(dT[:, m * 128:(m + 1) * 128],
                                   ptd[:, slot * 1024:slot * 1024 + 128]).then_inc(dte.h, 1)
                        p.on("scalar", f); dte.v += 1
                    dtr.v = (k + 1) * NKC
                    p.wait("vector", dte, dte.v)
                    p.wait("vector", ldx1, 16 * (k + 1))
                    if k >= 1:
                        p.wait("vector", sod, 16 * k)
                    def f(e, k=k):
                        e.tensor_tensor(out=x1b[k % 2][:], in0=x1b[k % 2][:], in1=dT[:], op=ALU.add).then_inc(sou.h, 1)
                    p.on("vector", f); sou.v += 1
                    p.wait("scalar", sou, sou.v)
                    p.wait("gpsimd", sou, sou.v)
                    def f(e, t4=t4, tc=tc, c=c, k=k):
                        e.dma_start(out=out[t4 * TT4 + tc * 512 + c * 128:t4 * TT4 + tc * 512 + (c + 1) * 128, :],
                                    in_=x1b[k % 2][:]).then_inc(sod.h, 16)
                    p.on("gpsimd", f); sod.v += 16
        p.wait("gpsimd", sod, sod.v)
        if DBG_P4:
            hdmp = nc.dram_tensor("dbg_h_last", [128, NIC * TT4], BF16, kind="ExternalOutput").ap()
            ddmp = nc.dram_tensor("dbg_dst_last", [128, 2 * NKC * 512], BF16, kind="ExternalOutput").ap()
            hst = Sem(es, nc, "f_hdmp")
            def f(e):
                e.dma_start(out=hdmp[:, :], in_=hb[:]).then_inc(hst.h, 16)
                e.dma_start(out=ddmp[:, :], in_=dstage[:]).then_inc(hst.h, 16)
                e.wait_ge(hst.h, 32)
            p.on("gpsimd", f)
        blk = es.enter_context(nc.Block("p4"))
        p.emit(blk)


from concourse.bass_utils import run_bass_kernel_spmd

_NC_CACHE = []

def kernel(L0_attn_hidden_states, L0_attn_Wq, L0_attn_Wk, L0_attn_Wv, L0_attn_Wo,
           L0_ffn_lut, L0_ffn_gate_idx, L0_ffn_up_idx, L0_ffn_down_idx):
    x = np.ascontiguousarray(np.asarray(L0_attn_hidden_states, dtype=np.float32))
    wq_a = np.ascontiguousarray(np.asarray(L0_attn_Wq, dtype=np.float32))
    wk_a = np.ascontiguousarray(np.asarray(L0_attn_Wk, dtype=np.float32))
    wv_a = np.ascontiguousarray(np.asarray(L0_attn_Wv, dtype=np.float32))
    wo_a = np.ascontiguousarray(np.asarray(L0_attn_Wo, dtype=np.float32))
    lut_a = np.asarray(L0_ffn_lut, dtype=np.float32)
    lutr = np.ascontiguousarray(np.repeat(lut_a[None, :], 128, axis=0))
    gi = np.ascontiguousarray(np.asarray(L0_ffn_gate_idx, dtype=np.int32))
    ui = np.ascontiguousarray(np.asarray(L0_ffn_up_idx, dtype=np.int32))
    di = np.ascontiguousarray(np.asarray(L0_ffn_down_idx, dtype=np.int32))

    if not _NC_CACHE:
        _NC_CACHE.append(build()[0])
    nc = _NC_CACHE[0]
    in_maps = []
    for core in range(8):
        in_maps.append({
            "x": np.ascontiguousarray(x[core]),
            "wq": wq_a, "wk": wk_a, "wv": wv_a, "wo": wo_a,
            "lutr": lutr, "gidx": gi, "uidx": ui, "didx": di,
        })
    res = run_bass_kernel_spmd(nc, in_maps, core_ids=list(range(8)))
    return np.stack([np.asarray(res.results[i]["out"], dtype=np.float32) for i in range(8)], axis=0)


# version 12 (final)
# speedup vs baseline: 1.1630x; 1.0295x over previous
import contextlib
import numpy as np
import concourse.bass as bass
import concourse.mybir as mybir

F32 = mybir.dt.float32
BF16 = mybir.dt.bfloat16
I32 = mybir.dt.int32
AF = mybir.ActivationFunctionType
ALU = mybir.AluOpType

S = 4096
H = 2048
D = 128
NH = 16
KVH = 4
WIN = 1024
INTER = 5504
NIC = INTER // 128
NKC = H // 128
TT = 512
NTILE = S // TT
SCALE = 1.0 / float(np.sqrt(D))
NEG = -1.0e9
EPS = 1e-5

ENGINES = ("sync", "vector", "scalar", "tensor", "gpsimd")


class Sem:
    def __init__(self, es, nc, name):
        self.h = es.enter_context(nc.semaphore(name))
        self.v = 0


class Prog:
    def __init__(self):
        self.ops = {e: [] for e in ENGINES}

    def on(self, eng, fn):
        self.ops[eng].append(fn)

    def wait(self, eng, sem, val):
        if val > 0:
            self.ops[eng].append(lambda e, s=sem.h, v=val: e.wait_ge(s, v))

    def emit(self, blk):
        for eng in ENGINES:
            if not self.ops[eng]:
                continue
            def body(e, lst=tuple(self.ops[eng])):
                for fn in lst:
                    fn(e)
            getattr(blk, eng)(body)


def build(phases=("p0", "p1a", "p1b", "p2", "p3", "p4"), dbg=()):
    nc = bass.Bass("TRN2", target_bir_lowering=False)

    x = nc.dram_tensor("x", [S, H], F32, kind="ExternalInput").ap()
    wq = nc.dram_tensor("wq", [H, H], F32, kind="ExternalInput").ap()
    wk = nc.dram_tensor("wk", [H, KVH * D], F32, kind="ExternalInput").ap()
    wv = nc.dram_tensor("wv", [H, KVH * D], F32, kind="ExternalInput").ap()
    wo = nc.dram_tensor("wo", [H, H], F32, kind="ExternalInput").ap()
    lutr = nc.dram_tensor("lutr", [128, 16], F32, kind="ExternalInput").ap()
    gidx = nc.dram_tensor("gidx", [INTER, H], I32, kind="ExternalInput").ap()
    uidx = nc.dram_tensor("uidx", [INTER, H], I32, kind="ExternalInput").ap()
    didx = nc.dram_tensor("didx", [H, INTER], I32, kind="ExternalInput").ap()
    out = nc.dram_tensor("out", [S, H], F32, kind="ExternalOutput").ap()

    xln_s = nc.dram_tensor("xln_s", [H, S], BF16, kind="Internal").ap()
    q_s = nc.dram_tensor("q_s", [H, S], BF16, kind="Internal").ap()
    ao_s = nc.dram_tensor("ao_s", [H, S], BF16, kind="Internal").ap()
    p4_standalone = ("p4" in phases and "p3" not in phases)
    x1_s = nc.dram_tensor("x1_s", [S, H], F32, kind="ExternalInput" if p4_standalone else "Internal").ap()
    y_s = nc.dram_tensor("y_s", [H, S], BF16, kind="ExternalInput" if p4_standalone else "Internal").ap()
    wg_s = nc.dram_tensor("wg_s", [NKC, 128, INTER], BF16, kind="Internal").ap()
    wu_s = nc.dram_tensor("wu_s", [NKC, 128, INTER], BF16, kind="Internal").ap()
    wd_s = nc.dram_tensor("wd_s", [NKC, 128, INTER], BF16, kind="Internal").ap()

    dbg_aps = {}
    with contextlib.ExitStack() as top:
        ident = top.enter_context(nc.sbuf_tensor("ident", [128, 128], BF16))
        slut = top.enter_context(nc.sbuf_tensor("slut", [128, 16], F32))
        ones_col = top.enter_context(nc.sbuf_tensor("ones_col", [128, 1], BF16))
        ones_row = top.enter_context(nc.sbuf_tensor("ones_row", [1, 128], F32))

        dlut = top.enter_context(nc.sbuf_tensor("dlut", [128, 16], F32))
        _setup(nc, lutr, slut, ident, ones_col, ones_row, dlut)

        gath = None
        if "p0" in phases:
            g_es = top.enter_context(contextlib.ExitStack())
            gath = Gather(nc, g_es, slut, ident, gidx, uidx, didx, wg_s, wu_s, wd_s, dlut)
        with contextlib.ExitStack() as kvscope:
            k_cache = kvscope.enter_context(nc.sbuf_tensor("k_cache", [128, KVH * S], BF16))
            v_cache = kvscope.enter_context(nc.sbuf_tensor("v_cache", [128, KVH * S], BF16))
            if "p1a" in phases:
                _p1a(nc, x, wk, wv, xln_s, k_cache, v_cache, ident)
            if "p1b" in phases:
                _p1b(nc, wq, xln_s, q_s)
            if "p2" in phases:
                _p2(nc, q_s, ao_s, k_cache, v_cache, ones_col, ones_row, gath, ident)
            if "kv" in dbg:
                for nm, t in (("k_cache", k_cache), ("v_cache", v_cache)):
                    dst = nc.dram_tensor("dbg_" + nm, [128, KVH * S], BF16, kind="ExternalOutput").ap()
                    dbg_aps[nm] = dst
                    _sbuf_dump(nc, dst, t)
        if "p3" in phases:
            _p3(nc, x, wo, ao_s, x1_s, y_s, ident, gath)
        if gath is not None:
            _p0(nc, gath)
        if "p4" in phases:
            _p4(nc, y_s, x1_s, wg_s, wu_s, wd_s, out, ident)
        else:
            _dump_stub(nc, out)

        for name, src, shp, dt in (
            ("xln_s", xln_s, [H, S], BF16), ("q_s", q_s, [H, S], BF16),
            ("ao_s", ao_s, [H, S], BF16), ("x1_s", x1_s, [S, H], F32),
            ("y_s", y_s, [H, S], BF16),
            ("wg_s", wg_s, [NKC, 128, INTER], BF16),
            ("wu_s", wu_s, [NKC, 128, INTER], BF16),
            ("wd_s", wd_s, [NKC, 128, INTER], BF16),
        ):
            if name in dbg:
                dst = nc.dram_tensor("dbg_" + name, shp, dt, kind="ExternalOutput").ap()
                dbg_aps[name] = dst
                _dram_copy(nc, dst, src, shp, dt)
    return nc, dbg_aps


def _setup(nc, lutr, slut, ident, ones_col, ones_row, dlut):
    with contextlib.ExitStack() as es:
        sidi = es.enter_context(nc.sbuf_tensor("sidi", [128, 128], I32))
        ld = Sem(es, nc, "su_ld")
        dn = Sem(es, nc, "su_dn")
        p = Prog()
        def f(e):
            e.dma_start(out=slut[:], in_=lutr[:, :]).then_inc(ld.h, 16)
        p.on("sync", f); ld.v += 16
        def f(e):
            e.iota(sidi[:], pattern=[[-1, 128]], base=0, channel_multiplier=1).then_inc(dn.h, 1)
        p.on("gpsimd", f); dn.v += 1
        p.wait("vector", dn, dn.v)
        p.wait("vector", ld, ld.v)
        def f(e):
            e.tensor_scalar(out=ident[:], in0=sidi[:], scalar1=0, scalar2=None, op0=ALU.is_equal)
            e.memset(ones_col[:], 1.0)
            e.memset(ones_row[:], 1.0)
            e.drain()
            e.tensor_scalar(out=dlut[:], in0=slut[:], scalar1=slut[:, 15:16], scalar2=None, op0=ALU.subtract)
        p.on("vector", f)
        blk = es.enter_context(nc.Block("setup"))
        p.emit(blk)


def _dump_stub(nc, out):
    with contextlib.ExitStack() as es:
        z = es.enter_context(nc.sbuf_tensor("zz", [128, H], F32))
        st = Sem(es, nc, "z_st")
        dn = Sem(es, nc, "z_dn")
        p = Prog()
        def f(e):
            e.memset(z[:], 0.0).then_inc(dn.h, 1)
        p.on("vector", f); dn.v += 1
        p.wait("gpsimd", dn, dn.v)
        def f(e):
            for i in range(S // 128):
                e.dma_start(out=out[i * 128:(i + 1) * 128, :], in_=z[:]).then_inc(st.h, 16)
            e.wait_ge(st.h, 16 * (S // 128))
        p.on("gpsimd", f); st.v += 16 * (S // 128)
        blk = es.enter_context(nc.Block("zout"))
        p.emit(blk)


_DP_N = [0]
def _sbuf_dump(nc, dst, src):
    _DP_N[0] += 1
    uid = _DP_N[0]
    with contextlib.ExitStack() as es:
        st = Sem(es, nc, f"dmp_st{uid}")
        p = Prog()
        def f(e):
            e.dma_start(out=dst[:, :], in_=src[:]).then_inc(st.h, 16)
            e.wait_ge(st.h, 16)
        p.on("gpsimd", f)
        blk = es.enter_context(nc.Block(f"dmp{uid}"))
        p.emit(blk)


_CP_N = [0]
def _dram_copy(nc, dst, src, shp, dt):
    total = int(np.prod(shp))
    FREE = 8192
    _CP_N[0] += 1
    uid = _CP_N[0]
    with contextlib.ExitStack() as es:
        buf = es.enter_context(nc.sbuf_tensor(f"cp_buf{uid}", [128, FREE], dt))
        ld = Sem(es, nc, f"cp_ld{uid}")
        st = Sem(es, nc, f"cp_st{uid}")
        p = Prog()
        if len(shp) == 2:
            fsrc = src.rearrange("a b -> (a b)")
            fdst = dst.rearrange("a b -> (a b)")
        else:
            fsrc = src.rearrange("a b c -> (a b c)")
            fdst = dst.rearrange("a b c -> (a b c)")
        chunk = 128 * FREE
        nch = (total + chunk - 1) // chunk
        def f(e):
            for i in range(nch):
                n = min(chunk, total - i * chunk)
                rows = n // FREE
                e.wait_ge(st.h, 16 * i)
                e.dma_start(out=buf[0:rows, :], in_=fsrc[i * chunk:i * chunk + n].rearrange("(pp ff) -> pp ff", ff=FREE)).then_inc(ld.h, 16)
                e.wait_ge(ld.h, 16 * (i + 1))
                e.dma_start(out=fdst[i * chunk:i * chunk + n].rearrange("(pp ff) -> pp ff", ff=FREE), in_=buf[0:rows, :]).then_inc(st.h, 16)
            e.wait_ge(st.h, 16 * nch)
        p.on("gpsimd", f)
        blk = es.enter_context(nc.Block(f"cp{uid}"))
        p.emit(blk)


class Gather:
    def __init__(self, nc, es, slut, ident, gidx, uidx, didx, wg_s, wu_s, wd_s, dlut):
        self.nc = nc
        self.slut = slut
        self.dlut = dlut
        self.ident = ident
        units = []
        for b_ in range(NIC):
            units.append((gidx[b_ * 128:(b_ + 1) * 128, :], wg_s, b_, 0, H, "gu"))
        for b_ in range(NIC):
            units.append((uidx[b_ * 128:(b_ + 1) * 128, :], wu_s, b_, 0, H, "gu"))
        for b_ in range(NKC):
            for c0, w in ((0, 2048), (2048, 2048), (4096, 1408)):
                units.append((didx[b_ * 128:(b_ + 1) * 128, c0:c0 + w], wd_s, b_, c0, w, "d"))
        self.units = units
        self.cursor = 0
        self.ld = Sem(es, nc, "G_ld")
        self.gat = Sem(es, nc, "G_gat")
        self.trn = Sem(es, nc, "G_trn")
        self.evc = Sem(es, nc, "G_evc")
        self.stg = Sem(es, nc, "G_stg")
        self.npc_hist = []
        self.phase_first = 0
        self._t = None

    def start_phase(self, es, width=2048):
        nc = self.nc
        W = width
        self.bw = W
        self.sidx = [es.enter_context(nc.sbuf_tensor(f"G_idx{i}_{self.cursor}", [128, W], I32)) for i in range(2)]
        self.sidxb = [es.enter_context(nc.sbuf_tensor(f"G_idxb{i}_{self.cursor}", [128, W], BF16)) for i in range(2)]
        self.smask = [es.enter_context(nc.sbuf_tensor(f"G_msk{i}_{self.cursor}", [128, W // 2], BF16)) for i in range(2)]
        self.smlut = [es.enter_context(nc.sbuf_tensor(f"G_mlt{i}_{self.cursor}", [128, W // 2], BF16)) for i in range(2)]
        self.sacc = [es.enter_context(nc.sbuf_tensor(f"G_acc{i}_{self.cursor}", [128, W], BF16)) for i in range(2)]
        self.sstage = [es.enter_context(nc.sbuf_tensor(f"G_stg{i}_{self.cursor}", [128, W], BF16)) for i in range(2)]
        self.pt = es.enter_context(nc.psum_tensor(f"G_pt_{self.cursor}", [128, 2048], BF16))
        self.phase_first = self.cursor

    def remaining(self):
        return len(self.units) - self.cursor

    def emit_units(self, p, n):
        for _ in range(min(n, self.remaining())):
            self._unit_head(p)
            while self._t < 16:
                self._emit_slice(p, 16)
            self._unit_tail(p)

    def emit_slice(self, p, nt=4):
        if self.cursor >= len(self.units):
            return False
        if self._t is None:
            self._unit_head(p)
        self._emit_slice(p, nt)
        if self._t >= 16:
            self._unit_tail(p)
        return True

    def flush_phase(self, p):
        if self._t is not None:
            while self._t < 16:
                self._emit_slice(p, 16)
            self._unit_tail(p)

    def _unit_head(self, p):
        u = self.cursor
        src_ap, dst, b_, c0, fw, kind = self.units[u]
        assert fw <= self.bw
        pl = u - self.phase_first
        slot = pl % 2
        if pl >= 2:
            p.wait("sync", self.gat, u - 1)
        def f(e, src=src_ap, buf=self.sidx[slot], fw=fw, ld=self.ld):
            e.dma_start(out=buf[:, 0:fw], in_=src).then_inc(ld.h, 16)
        p.on("sync", f); self.ld.v += 16
        p.wait("vector", self.ld, self.ld.v)
        if pl >= 2:
            p.wait("vector", self.trn, sum(self.npc_hist) - self.npc_hist[-1])
        self._t = 0

    def _emit_slice(self, p, nt):
        u = self.cursor
        src_ap, dst, b_, c0, fw, kind = self.units[u]
        hw = fw // 2
        pl = u - self.phase_first
        slot = pl % 2
        t0, t1 = self._t, min(15, self._t + nt)
        dlut = self.dlut
        def f(e, idx=self.sidx[slot], idxb=self.sidxb[slot], acc=self.sacc[slot],
              hw=hw, msk=self.smask, mlt=self.smlut, gat=self.gat, t0=t0, t1=t1):
            for t in range(t0, t1):
                if t == 0:
                    for hh in range(2):
                        e.tensor_scalar(out=idxb[:, hh * hw:hh * hw + hw], in0=idx[:, hh * hw:hh * hw + hw],
                                        scalar1=0.0, scalar2=None, op0=ALU.add)
                for hh in range(2):
                    e.tensor_scalar(out=msk[hh][:, 0:hw], in0=idxb[:, hh * hw:hh * hw + hw],
                                    scalar1=float(t), scalar2=None, op0=ALU.is_equal)
                for hh in range(2):
                    if t == 0:
                        e.tensor_scalar(out=acc[:, hh * hw:hh * hw + hw], in0=msk[hh][:, 0:hw],
                                        scalar1=dlut[:, 0:1], scalar2=self.slut[:, 15:16],
                                        op0=ALU.mult, op1=ALU.add)
                    else:
                        e.tensor_scalar(out=mlt[hh][:, 0:hw], in0=msk[hh][:, 0:hw],
                                        scalar1=dlut[:, t:t + 1], scalar2=None, op0=ALU.mult)
                if t > 0:
                    for hh in range(2):
                        i_ = e.tensor_tensor(out=acc[:, hh * hw:hh * hw + hw],
                                             in0=acc[:, hh * hw:hh * hw + hw],
                                             in1=mlt[hh][:, 0:hw], op=ALU.add)
                        if t == 14 and hh == 1:
                            i_.then_inc(gat.h, 1)
        p.on("vector", f)
        if t1 == 15:
            self.gat.v += 1
        self._t = t1 if t1 < 15 else 16

    def _unit_tail(self, p):
        u = self.cursor
        src_ap, dst, b_, c0, fw, kind = self.units[u]
        npc = fw // 128
        pl = u - self.phase_first
        slot = pl % 2
        prior_npc = sum(self.npc_hist)
        ident = self.ident
        p.wait("tensor", self.gat, self.gat.v)
        for pc in range(npc):
            ti = prior_npc + pc
            if ti >= 2:
                p.wait("tensor", self.evc, ti - 1)
            def f(e, acc=self.sacc[slot], pc=pc, pslot=ti % 2, pt=self.pt):
                e.transpose(pt[:, pslot * 1024:pslot * 1024 + 128],
                            acc[:, pc * 128:pc * 128 + 128], ident[:]).then_inc(self.trn.h, 1)
            p.on("tensor", f)
        self.trn.v = prior_npc + npc
        if pl >= 2:
            p.wait("scalar", self.stg, 16 * (u - 1))
        for pc in range(npc):
            ti = prior_npc + pc
            p.wait("scalar", self.trn, ti + 1)
            def f(e, stage=self.sstage[slot], pc=pc, pslot=ti % 2, pt=self.pt):
                e.copy(stage[:, pc * 128:pc * 128 + 128],
                       pt[:, pslot * 1024:pslot * 1024 + 128]).then_inc(self.evc.h, 1)
            p.on("scalar", f)
        self.evc.v = prior_npc + npc
        if kind == "gu":
            def f(e, dst=dst, b_=b_, stage=self.sstage[slot]):
                e.dma_start(out=dst[:, :, b_ * 128:(b_ + 1) * 128].rearrange("kc pp j -> pp kc j"),
                            in_=stage[:, 0:H].rearrange("pp (kc j) -> pp kc j", kc=NKC)).then_inc(self.stg.h, 16)
        else:
            def f(e, dst=dst, b_=b_, c0=c0, fw=fw, stage=self.sstage[slot]):
                e.dma_start(out=dst[b_, :, c0:c0 + fw], in_=stage[:, 0:fw]).then_inc(self.stg.h, 16)
        p.on("scalar", f); self.stg.v += 16
        self.npc_hist.append(npc)
        self.cursor += 1
        self._t = None

    def finish_wait(self, p, eng="scalar"):
        p.wait(eng, self.stg, self.stg.v)


def _p0(nc, g, n_units=None):
    if g.remaining() == 0:
        return
    n = g.remaining() if n_units is None else min(n_units, g.remaining())
    with contextlib.ExitStack() as es:
        g.start_phase(es, width=2048)
        p = Prog()
        g.emit_units(p, n)
        if g.remaining() == 0:
            g.finish_wait(p)
        blk = es.enter_context(nc.Block("p0"))
        p.emit(blk)


def _p1a(nc, x, wk, wv, xln_s, k_cache, v_cache, ident):
    with contextlib.ExitStack() as es:
        wkb = es.enter_context(nc.sbuf_tensor("wkb", [128, NKC * KVH * D], BF16))
        wvb = es.enter_context(nc.sbuf_tensor("wvb", [128, NKC * KVH * D], BF16))
        wtmp = [es.enter_context(nc.sbuf_tensor(f"wtmp{i}", [128, KVH * D], F32)) for i in range(2)]
        xbuf = [es.enter_context(nc.sbuf_tensor(f"xbuf{i}", [128, 4 * H], F32)) for i in range(2)]
        xstat = es.enter_context(nc.sbuf_tensor("xstat", [128, 24], F32))
        xaggr = es.enter_context(nc.sbuf_tensor("xaggr", [128, 2], F32))
        nmean = es.enter_context(nc.sbuf_tensor("nmean", [128, 1], F32))
        rstd = es.enter_context(nc.sbuf_tensor("rstd", [128, 1], F32))
        xln_tm = es.enter_context(nc.sbuf_tensor("xln_tm", [128, 2 * H], BF16))
        xlnfm = es.enter_context(nc.sbuf_tensor("xlnfm", [128, NKC * TT], BF16))
        vtmp = es.enter_context(nc.sbuf_tensor("vtmp", [128, TT], BF16))
        ptx = es.enter_context(nc.psum_tensor("p1_ptx", [128, 2048], BF16))
        ptv = es.enter_context(nc.psum_tensor("p1_ptv", [128, 2048], BF16))
        pkv = es.enter_context(nc.psum_tensor("p1_pkv", [128, 2 * TT], F32))
        ld = Sem(es, nc, "p1_ld")
        ldx = Sem(es, nc, "p1_ldx")
        cst = Sem(es, nc, "p1_cst")
        lnq = Sem(es, nc, "p1_lnq")
        lnr = Sem(es, nc, "p1_lnr")
        ntm = Sem(es, nc, "p1_ntm")
        trn = Sem(es, nc, "p1_trn")
        evc = Sem(es, nc, "p1_evc")
        mmk = Sem(es, nc, "p1_mmk")
        kve = Sem(es, nc, "p1_kve")
        vtr = Sem(es, nc, "p1_vtr")
        vev = Sem(es, nc, "p1_vev")
        sx = Sem(es, nc, "p1_sx")
        p = Prog()
        NCH = 4

        for w_i, (wsrc, wdst) in enumerate(((wk, wkb), (wv, wvb))):
            for kc in range(NKC):
                j = w_i * NKC + kc
                if j >= 2:
                    p.wait("sync", cst, j - 1)
                def f(e, wsrc=wsrc, kc=kc, buf=wtmp[j % 2], ld=ld):
                    e.dma_start(out=buf[:], in_=wsrc[kc * 128:(kc + 1) * 128, :]).then_inc(ld.h, 16)
                p.on("sync", f); ld.v += 16
                p.wait("vector", ld, 16 * (j + 1))
                def f(e, wdst=wdst, kc=kc, buf=wtmp[j % 2], cst=cst):
                    e.tensor_copy(wdst[:, kc * (KVH * D):(kc + 1) * (KVH * D)], buf[:]).then_inc(cst.h, 1)
                p.on("vector", f); cst.v += 1

        for tt in range(NTILE):
            if tt >= 2:
                p.wait("sync", ntm, (tt - 1) * NCH)
            def f(e, tt=tt, buf=xbuf[tt % 2], ldx=ldx):
                e.dma_start(out=buf[:].rearrange("pp (c hh) -> pp c hh", c=4), in_=x[tt * TT:(tt + 1) * TT, :].rearrange("(c pp) hh -> pp c hh", pp=128)).then_inc(ldx.h, 16)
            p.on("sync", f); ldx.v += 16
            p.wait("vector", ldx, 16 * (tt + 1))
            for c in range(NCH):
                k = tt * NCH + c
                def f(e, xc_buf=xbuf[tt % 2], c=c):
                    xc = xc_buf[:, c * H:(c + 1) * H]
                    for i in range(4):
                        e.bn_stats(xstat[:, 6 * i:6 * (i + 1)], xc[:, i * 512:(i + 1) * 512])
                    e.drain()
                    e.bn_aggr(xaggr[:], xstat[:])
                    e.drain()
                    e.tensor_scalar(out=nmean[:], in0=xaggr[:, 0:1], scalar1=-1.0, scalar2=None, op0=ALU.mult)
                    e.tensor_scalar(out=rstd[:], in0=xaggr[:, 1:2], scalar1=EPS, scalar2=None, op0=ALU.add).then_inc(lnq.h, 1)
                p.on("vector", f); lnq.v += 1
                p.wait("scalar", lnq, lnq.v)
                def f(e):
                    e.activation(out=rstd[:], in_=rstd[:], func=AF.Ln)
                    e.drain()
                    e.activation(out=rstd[:], in_=rstd[:], func=AF.Exp, scale=-0.5).then_inc(lnr.h, 1)
                p.on("scalar", f); lnr.v += 1
                p.wait("vector", lnr, lnr.v)
                if k >= 2:
                    p.wait("vector", trn, (k - 1) * NKC)
                def f(e, xc_buf=xbuf[tt % 2], c=c):
                    xc = xc_buf[:, c * H:(c + 1) * H]
                    e.tensor_scalar(out=xln_tm[:, (c % 2) * H:(c % 2) * H + H], in0=xc,
                                    scalar1=nmean[:], scalar2=rstd[:], op0=ALU.add, op1=ALU.mult).then_inc(ntm.h, 1)
                p.on("vector", f); ntm.v += 1
                p.wait("tensor", ntm, ntm.v)
                for kc in range(NKC):
                    ti = k * NKC + kc
                    if ti >= 2:
                        p.wait("tensor", evc, ti - 1)
                    def f(e, c=c, kc=kc, slot=ti % 2):
                        e.transpose(ptx[:, slot * 1024:slot * 1024 + 128],
                                    xln_tm[:, (c % 2) * H + kc * 128:(c % 2) * H + (kc + 1) * 128],
                                    ident[:]).then_inc(trn.h, 1)
                    p.on("tensor", f)
                trn.v = (k + 1) * NKC
                for kc in range(NKC):
                    ti = k * NKC + kc
                    p.wait("scalar", trn, ti + 1)
                    def f(e, c=c, kc=kc, slot=ti % 2):
                        e.copy(xlnfm[:, kc * TT + c * 128:kc * TT + c * 128 + 128],
                               ptx[:, slot * 1024:slot * 1024 + 128]).then_inc(evc.h, 1)
                    p.on("scalar", f)
                evc.v = (k + 1) * NKC
            p.wait("tensor", evc, evc.v)
            if tt == 0:
                p.wait("tensor", cst, 2 * NKC)
            for w_i, wsb in enumerate((wkb, wvb)):
                for kvh in range(KVH):
                    mi = tt * 2 * KVH + w_i * KVH + kvh
                    if mi >= 2:
                        p.wait("tensor", kve, mi - 1)
                    for kc in range(NKC):
                        last = (kc == NKC - 1)
                        def f(e, wsb=wsb, kvh=kvh, kc=kc, slot=mi % 2, last=last):
                            i_ = e.matmul(pkv[:, slot * TT:slot * TT + TT],
                                          wsb[:, kc * (KVH * D) + kvh * D:kc * (KVH * D) + (kvh + 1) * D],
                                          xlnfm[:, kc * TT:(kc + 1) * TT], start=(kc == 0), stop=last)
                            if last:
                                i_.then_inc(mmk.h, 1)
                        p.on("tensor", f)
                    mmk.v += 1
                    p.wait("scalar", mmk, mmk.v)
                    if w_i == 0:
                        def f(e, kvh=kvh, tt=tt, slot=mi % 2):
                            e.copy(k_cache[:, kvh * S + tt * TT:kvh * S + (tt + 1) * TT],
                                   pkv[:, slot * TT:slot * TT + TT]).then_inc(kve.h, 1)
                        p.on("scalar", f); kve.v += 1
                    else:
                        def f(e, slot=mi % 2):
                            e.copy(vtmp[:], pkv[:, slot * TT:slot * TT + TT]).then_inc(kve.h, 1)
                        p.on("scalar", f); kve.v += 1
                        p.wait("tensor", kve, kve.v)
                        for c in range(NCH):
                            vi = vtr.v
                            if vi >= 2:
                                p.wait("tensor", vev, vi - 1)
                            def f(e, c=c, slot=vi % 2):
                                e.transpose(ptv[:, slot * 1024:slot * 1024 + 128],
                                            vtmp[:, c * 128:(c + 1) * 128], ident[:]).then_inc(vtr.h, 1)
                            p.on("tensor", f); vtr.v += 1
                            p.wait("scalar", vtr, vtr.v)
                            def f(e, kvh=kvh, tt=tt, c=c, slot=vi % 2):
                                e.copy(v_cache[:, kvh * S + tt * TT + c * 128:kvh * S + tt * TT + (c + 1) * 128],
                                       ptv[:, slot * 1024:slot * 1024 + 128]).then_inc(vev.h, 1)
                            p.on("scalar", f); vev.v += 1
                        p.wait("scalar", vtr, vtr.v)
            p.wait("gpsimd", evc, evc.v)
            def f(e, tt=tt):
                e.dma_start(out=xln_s[:, tt * TT:(tt + 1) * TT].rearrange("(kc pp) t -> pp kc t", pp=128),
                            in_=xlnfm[:].rearrange("pp (kc t) -> pp kc t", kc=NKC)).then_inc(sx.h, 16)
            p.on("gpsimd", f); sx.v += 16
            p.wait("scalar", sx, sx.v)
            p.wait("scalar", mmk, mmk.v)
        p.wait("gpsimd", sx, sx.v)
        blk = es.enter_context(nc.Block("p1a"))
        p.emit(blk)


def _p1b(nc, wq, xln_s, q_s):
    with contextlib.ExitStack() as es:
        wqb = es.enter_context(nc.sbuf_tensor("wqb", [128, NKC * H], BF16))
        wtmp = [es.enter_context(nc.sbuf_tensor(f"qwt{i}", [128, H], F32)) for i in range(2)]
        xt = [es.enter_context(nc.sbuf_tensor(f"qxt{i}", [128, NKC * TT], BF16)) for i in range(2)]
        qstage = es.enter_context(nc.sbuf_tensor("qstage", [128, NH * TT], BF16))
        pq = es.enter_context(nc.psum_tensor("pq", [128, 2 * TT], F32))
        ld = Sem(es, nc, "q_ld")
        cst = Sem(es, nc, "q_cst")
        ldx = Sem(es, nc, "q_ldx")
        mm = Sem(es, nc, "q_mm")
        qe = Sem(es, nc, "q_qe")
        st = Sem(es, nc, "q_st")
        p = Prog()
        for kc in range(NKC):
            if kc >= 2:
                p.wait("sync", cst, kc - 1)
            def f(e, kc=kc, buf=wtmp[kc % 2], ld=ld):
                e.dma_start(out=buf[:], in_=wq[kc * 128:(kc + 1) * 128, :]).then_inc(ld.h, 16)
            p.on("sync", f); ld.v += 16
            p.wait("vector", ld, 16 * (kc + 1))
            def f(e, kc=kc, buf=wtmp[kc % 2], cst=cst):
                e.tensor_copy(wqb[:, kc * H:(kc + 1) * H], buf[:]).then_inc(cst.h, 1)
            p.on("vector", f); cst.v += 1

        for tt in range(NTILE):
            if tt >= 2:
                p.wait("sync", mm, (tt - 1) * NH)
            def f(e, tt=tt, buf=xt[tt % 2], ldx=ldx):
                e.dma_start(out=buf[:].rearrange("pp (kc t) -> pp kc t", kc=NKC), in_=xln_s[:, tt * TT:(tt + 1) * TT].rearrange("(kc pp) t -> pp kc t", pp=128)).then_inc(ldx.h, 16)
            p.on("sync", f); ldx.v += 16
            p.wait("tensor", ldx, 16 * (tt + 1))
            if tt == 0:
                p.wait("tensor", cst, NKC)
            if tt >= 1:
                p.wait("scalar", st, 16 * tt)
            for m in range(NH):
                mi = tt * NH + m
                if mi >= 2:
                    p.wait("tensor", qe, mi - 1)
                for kc in range(NKC):
                    last = (kc == NKC - 1)
                    def f(e, m=m, kc=kc, buf=xt[tt % 2], slot=mi % 2, last=last):
                        i_ = e.matmul(pq[:, slot * TT:slot * TT + TT],
                                      wqb[:, kc * H + m * 128:kc * H + (m + 1) * 128],
                                      buf[:, kc * TT:(kc + 1) * TT], start=(kc == 0), stop=last)
                        if last:
                            i_.then_inc(mm.h, 1)
                    p.on("tensor", f)
                mm.v += 1
                p.wait("scalar", mm, mm.v)
                def f(e, m=m, slot=mi % 2):
                    e.copy(qstage[:, m * TT:(m + 1) * TT], pq[:, slot * TT:slot * TT + TT]).then_inc(qe.h, 1)
                p.on("scalar", f); qe.v += 1
            p.wait("gpsimd", qe, qe.v)
            def f(e, tt=tt):
                e.dma_start(out=q_s[:, tt * TT:(tt + 1) * TT].rearrange("(kc pp) t -> pp kc t", pp=128),
                            in_=qstage[:].rearrange("pp (kc t) -> pp kc t", kc=NKC)).then_inc(st.h, 16)
            p.on("gpsimd", f); st.v += 16
        p.wait("gpsimd", st, st.v)
        blk = es.enter_context(nc.Block("p1b"))
        p.emit(blk)


def _p2(nc, q_s, ao_s, k_cache, v_cache, ones_col, ones_row, gath=None, ident=None):
    partial_D = (-384, -256, -128, 0, 640, 768, 896, 1024)
    with contextlib.ExitStack() as es:
        qt_b = [es.enter_context(nc.sbuf_tensor(f"qt{i}", [128, NH * TT], BF16)) for i in range(2)]
        probs = es.enter_context(nc.sbuf_tensor("probs", [128, 3 * TT], BF16))
        masks = es.enter_context(nc.sbuf_tensor("masks", [128, 8 * TT], BF16))
        m_io = es.enter_context(nc.sbuf_tensor("m_io", [128, TT], I32))
        m_t1 = es.enter_context(nc.sbuf_tensor("m_t1", [128, TT], F32))
        m_t2 = es.enter_context(nc.sbuf_tensor("m_t2", [128, TT], F32))
        onehot = es.enter_context(nc.sbuf_tensor("onehot", [128, NH * NH], BF16))
        oh_io = es.enter_context(nc.sbuf_tensor("oh_io", [128, NH], I32))
        ohrow = es.enter_context(nc.sbuf_tensor("ohrow", [16, NH * 128], F32))
        ohr_io = es.enter_context(nc.sbuf_tensor("ohr_io", [16, 128], I32))
        srec = es.enter_context(nc.sbuf_tensor("srec", [NH, TT], F32))
        ao_stage = es.enter_context(nc.sbuf_tensor("ao_stage", [128, NH * TT], BF16))
        pS = es.enter_context(nc.psum_tensor("pS", [128, 3 * TT], F32))
        pAV = es.enter_context(nc.psum_tensor("pAV", [128, TT], F32))
        pDen = es.enter_context(nc.psum_tensor("pDen", [128, TT], F32))
        pB = es.enter_context(nc.psum_tensor("pB", [128, TT], F32))
        ldq = Sem(es, nc, "a_ldq")
        msk = Sem(es, nc, "a_msk")
        sS = Sem(es, nc, "a_sS")
        sM = Sem(es, nc, "a_sM")
        sP = Sem(es, nc, "a_sP")
        sAV = Sem(es, nc, "a_sAV")
        sAVe = Sem(es, nc, "a_sAVe")
        sDen = Sem(es, nc, "a_sDen")
        sRec = Sem(es, nc, "a_sRec")
        sB = Sem(es, nc, "a_sB")
        sSc = Sem(es, nc, "a_sSc")
        sAOD = Sem(es, nc, "a_sAOD")
        p = Prog()
        if gath is not None:
            gath.start_phase(es, width=2048)

        D_list = sorted(partial_D)
        mask_off = {Dv: i * TT for i, Dv in enumerate(D_list)}
        def f(e):
            e.iota(m_io[:], pattern=[[1, TT]], base=128, channel_multiplier=-1).then_inc(msk.h, 1)
            e.iota(oh_io[:], pattern=[[1, NH]], base=0, channel_multiplier=0).then_inc(msk.h, 1)
            e.iota(ohr_io[:], pattern=[[0, 128]], base=0, channel_multiplier=1).then_inc(msk.h, 1)
        p.on("gpsimd", f); msk.v += 3
        p.wait("vector", msk, msk.v)
        def f(e):
            for i, Dv in enumerate(D_list):
                e.tensor_scalar(out=m_t1[:], in0=m_io[:], scalar1=float(128 - Dv), scalar2=None, op0=ALU.is_ge)
                e.tensor_scalar(out=m_t2[:], in0=m_io[:], scalar1=float(1152 - Dv), scalar2=None, op0=ALU.is_le)
                e.drain()
                e.tensor_tensor(out=m_t1[:], in0=m_t1[:], in1=m_t2[:], op=ALU.mult)
                e.drain()
                e.tensor_scalar(out=masks[:, i * TT:(i + 1) * TT], in0=m_t1[:],
                                scalar1=-1.0, scalar2=-NEG, op0=ALU.add, op1=ALU.mult)
                e.drain()
            for h in range(NH):
                e.tensor_scalar(out=onehot[:, h * NH:(h + 1) * NH], in0=oh_io[:],
                                scalar1=float(h), scalar2=None, op0=ALU.is_equal)
                e.tensor_scalar(out=ohrow[:, h * 128:(h + 1) * 128], in0=ohr_io[:],
                                scalar1=float(h), scalar2=None, op0=ALU.is_equal)
            e.drain()
            e.nop().then_inc(msk.h, 1)
        p.on("vector", f); msk.v += 1

        chunk_i = 0
        for qt in range(NTILE):
            qt0 = qt * TT
            if qt >= 2:
                p.wait("sync", sAOD, 16 * (qt - 1))
            def f(e, qt=qt, buf=qt_b[qt % 2], ldq=ldq):
                e.dma_start(out=buf[:].rearrange("pp (kc t) -> pp kc t", kc=NKC), in_=q_s[:, qt * TT:(qt + 1) * TT].rearrange("(kc pp) t -> pp kc t", pp=128)).then_inc(ldq.h, 16)
            p.on("sync", f); ldq.v += 16
            p.wait("tensor", ldq, 16 * (qt + 1))
            if qt == 0:
                p.wait("tensor", msk, msk.v)
            if qt >= 1:
                p.wait("scalar", sAOD, 16 * qt)
            k0s = list(range(max(0, qt0 - WIN), qt0 + TT - 128 + 1, 128))
            nch = len(k0s)
            for h in range(NH):
                kv = h // (NH // KVH)

                def emit_qk(ci, base_ci):
                    k0 = k0s[ci]
                    gci = base_ci + ci
                    slot = gci % 3
                    Dv = qt0 - k0
                    masked = Dv in partial_D
                    if gci >= 3:
                        p.wait("tensor", sP, gci - 2)
                    def f(e, kv=kv, k0=k0, h=h, buf=qt_b[qt % 2], slot=slot, masked=masked,
                          moff=(mask_off[Dv] if masked else 0)):
                        i_ = e.matmul(pS[:, slot * TT:slot * TT + TT],
                                      k_cache[:, kv * S + k0:kv * S + k0 + 128],
                                      buf[:, h * TT:(h + 1) * TT], start=True, stop=not masked)
                        if masked:
                            i_ = e.matmul(pS[:, slot * TT:slot * TT + TT], ident[:],
                                          masks[:, moff:moff + TT], start=False, stop=True)
                        i_.then_inc(sS.h, 1)
                    p.on("tensor", f); sS.v += 1
                    p.wait("scalar", sS, sS.v)
                    if gci >= 3:
                        p.wait("scalar", sAV, gci - 2)
                    def f(e, slot=slot):
                        e.activation(out=probs[:, slot * TT:slot * TT + TT],
                                     in_=pS[:, slot * TT:slot * TT + TT], func=AF.Exp,
                                     scale=SCALE).then_inc(sP.h, 1)
                    p.on("scalar", f); sP.v += 1

                def emit_av(ci, base_ci):
                    k0 = k0s[ci]
                    gci = base_ci + ci
                    slot = gci % 3
                    p.wait("tensor", sP, gci + 1)
                    is_first = (ci == 0)
                    is_last = (ci == nch - 1)
                    if is_first and qt * NH + h >= 1:
                        p.wait("tensor", sAVe, qt * NH + h)
                    def f(e, slot=slot, kv=kv, k0=k0, h=h,
                          is_first=is_first, is_last=is_last):
                        i_d = e.matmul(pDen[0:NH, :], onehot[:, h * NH:(h + 1) * NH],
                                       probs[:, slot * TT:slot * TT + TT],
                                       start=(h == 0 and is_first), stop=(h == NH - 1 and is_last))
                        if h == NH - 1 and is_last:
                            i_d.then_inc(sDen.h, 1)
                        e.matmul(pAV[:, :], v_cache[:, kv * S + k0:kv * S + k0 + 128],
                                 probs[:, slot * TT:slot * TT + TT],
                                 start=is_first, stop=is_last).then_inc(sAV.h, 1)
                    p.on("tensor", f); sAV.v += 1
                    if h == NH - 1 and is_last:
                        sDen.v += 1

                base_ci = chunk_i
                emit_qk(0, base_ci)
                if nch > 1:
                    emit_qk(1, base_ci)
                for ci in range(nch):
                    if ci + 2 < nch:
                        emit_qk(ci + 2, base_ci)
                    emit_av(ci, base_ci)
                chunk_i += nch
                p.wait("scalar", sAV, sAV.v)
                def f(e, h=h):
                    e.copy(ao_stage[:, h * TT:(h + 1) * TT], pAV[:, :]).then_inc(sAVe.h, 1)
                p.on("scalar", f); sAVe.v += 1
                if gath is not None:
                    for _ in range(4):
                        gath.emit_slice(p, 2)
            p.wait("vector", sDen, sDen.v)
            def f(e):
                e.reciprocal(srec[:], pDen[0:NH, :]).then_inc(sRec.h, 1)
            p.on("vector", f); sRec.v += 1
            p.wait("tensor", sRec, sRec.v)
            for h in range(NH):
                if sB.v >= 1:
                    p.wait("tensor", sSc, sB.v)
                def f(e, h=h):
                    e.matmul(pB[:, :], ohrow[:, h * 128:(h + 1) * 128], srec[:],
                             start=True, stop=True).then_inc(sB.h, 1)
                p.on("tensor", f); sB.v += 1
                p.wait("vector", sB, sB.v)
                p.wait("vector", sAVe, sAVe.v - (NH - 1 - h))
                def f(e, h=h):
                    e.tensor_tensor(out=ao_stage[:, h * TT:(h + 1) * TT],
                                    in0=ao_stage[:, h * TT:(h + 1) * TT],
                                    in1=pB[:, :], op=ALU.mult).then_inc(sSc.h, 1)
                p.on("vector", f); sSc.v += 1
            p.wait("gpsimd", sSc, sSc.v)
            def f(e, qt=qt):
                e.dma_start(out=ao_s[:, qt * TT:(qt + 1) * TT].rearrange("(kc pp) t -> pp kc t", pp=128),
                            in_=ao_stage[:].rearrange("pp (kc t) -> pp kc t", kc=NKC)).then_inc(sAOD.h, 16)
            p.on("gpsimd", f); sAOD.v += 16
            p.wait("vector", sAOD, sAOD.v)
        if gath is not None:
            gath.flush_phase(p)
        p.wait("gpsimd", sAOD, sAOD.v)
        blk = es.enter_context(nc.Block("p2"))
        p.emit(blk)


def _p3(nc, x, wo, ao_s, x1_s, y_s, ident, gath=None):
    with contextlib.ExitStack() as es:
        wob = es.enter_context(nc.sbuf_tensor("wob", [128, NKC * H], BF16))
        wtmp = [es.enter_context(nc.sbuf_tensor(f"owt{i}", [128, H], F32)) for i in range(2)]
        aot = [es.enter_context(nc.sbuf_tensor(f"aot{i}", [128, NKC * TT], BF16)) for i in range(2)]
        obuf = es.enter_context(nc.sbuf_tensor("obuf", [128, NH * TT], BF16))
        oT = es.enter_context(nc.sbuf_tensor("oT", [128, H], BF16))
        xc_b = [es.enter_context(nc.sbuf_tensor(f"p3x{i}", [128, H], F32)) for i in range(2)]
        x1c = [es.enter_context(nc.sbuf_tensor(f"x1c{i}", [128, H], F32)) for i in range(2)]
        ytm = es.enter_context(nc.sbuf_tensor("ytm", [128, H], BF16))
        yfm = es.enter_context(nc.sbuf_tensor("yfm", [128, NKC * TT], BF16))
        xstat = es.enter_context(nc.sbuf_tensor("p3stat", [128, 24], F32))
        xaggr = es.enter_context(nc.sbuf_tensor("p3aggr", [128, 2], F32))
        nmean = es.enter_context(nc.sbuf_tensor("p3nm", [128, 1], F32))
        rstd = es.enter_context(nc.sbuf_tensor("p3rs", [128, 1], F32))
        po = es.enter_context(nc.psum_tensor("po", [128, 2 * TT], F32))
        pto = es.enter_context(nc.psum_tensor("p3pto", [128, 2048], BF16))
        pty = es.enter_context(nc.psum_tensor("p3pty", [128, 2048], BF16))
        ld = Sem(es, nc, "o_ld")
        cst = Sem(es, nc, "o_cst")
        lda = Sem(es, nc, "o_lda")
        ldxc = Sem(es, nc, "o_ldxc")
        mm = Sem(es, nc, "o_mm")
        oev = Sem(es, nc, "o_oev")
        otr = Sem(es, nc, "o_otr")
        ote = Sem(es, nc, "o_ote")
        sx1 = Sem(es, nc, "o_sx1")
        sx1d = Sem(es, nc, "o_sx1d")
        lnq = Sem(es, nc, "o_lnq")
        lnr = Sem(es, nc, "o_lnr")
        syt = Sem(es, nc, "o_syt")
        ytr = Sem(es, nc, "o_ytr")
        yev = Sem(es, nc, "o_yev")
        syd = Sem(es, nc, "o_syd")
        p = Prog()
        NCH = 4

        for kc in range(NKC):
            if kc >= 2:
                p.wait("sync", cst, kc - 1)
            def f(e, kc=kc, buf=wtmp[kc % 2], ld=ld):
                e.dma_start(out=buf[:], in_=wo[kc * 128:(kc + 1) * 128, :]).then_inc(ld.h, 16)
            p.on("sync", f); ld.v += 16
            p.wait("vector", ld, 16 * (kc + 1))
            def f(e, kc=kc, buf=wtmp[kc % 2], cst=cst):
                e.tensor_copy(wob[:, kc * H:(kc + 1) * H], buf[:]).then_inc(cst.h, 1)
            p.on("vector", f); cst.v += 1

        for tt in range(NTILE):
            if tt >= 2:
                p.wait("sync", mm, (tt - 1) * NH)
            def f(e, tt=tt, buf=aot[tt % 2], lda=lda):
                e.dma_start(out=buf[:].rearrange("pp (kc t) -> pp kc t", kc=NKC), in_=ao_s[:, tt * TT:(tt + 1) * TT].rearrange("(kc pp) t -> pp kc t", pp=128)).then_inc(lda.h, 16)
            p.on("sync", f); lda.v += 16
            for c in range(NCH):
                k = tt * NCH + c
                if k >= 2:
                    p.wait("sync", sx1, k - 1)
                def f(e, tt=tt, c=c, buf=xc_b[k % 2], ldxc=ldxc):
                    e.dma_start(out=buf[:], in_=x[tt * TT + c * 128:tt * TT + (c + 1) * 128, :]).then_inc(ldxc.h, 16)
                p.on("sync", f); ldxc.v += 16
            p.wait("tensor", lda, 16 * (tt + 1))
            if tt == 0:
                p.wait("tensor", cst, NKC)
            if tt >= 1:
                p.wait("scalar", otr, tt * NCH * NKC)
            for m in range(NH):
                mi = tt * NH + m
                if mi >= 2:
                    p.wait("tensor", oev, mi - 1)
                for kc in range(NKC):
                    last = (kc == NKC - 1)
                    def f(e, m=m, kc=kc, buf=aot[tt % 2], slot=mi % 2, last=last):
                        i_ = e.matmul(po[:, slot * TT:slot * TT + TT],
                                      wob[:, kc * H + m * 128:kc * H + (m + 1) * 128],
                                      buf[:, kc * TT:(kc + 1) * TT], start=(kc == 0), stop=last)
                        if last:
                            i_.then_inc(mm.h, 1)
                    p.on("tensor", f)
                mm.v += 1
                p.wait("scalar", mm, mm.v)
                def f(e, m=m, slot=mi % 2):
                    e.copy(obuf[:, m * TT:(m + 1) * TT], po[:, slot * TT:slot * TT + TT]).then_inc(oev.h, 1)
                p.on("scalar", f); oev.v += 1
            p.wait("tensor", oev, oev.v)
            for c in range(NCH):
                k = tt * NCH + c
                for mch in range(NKC):
                    ti = k * NKC + mch
                    if ti >= 2:
                        p.wait("tensor", ote, ti - 1)
                    def f(e, c=c, mch=mch, slot=ti % 2):
                        e.transpose(pto[:, slot * 1024:slot * 1024 + 128],
                                    obuf[:, mch * TT + c * 128:mch * TT + c * 128 + 128],
                                    ident[:]).then_inc(otr.h, 1)
                    p.on("tensor", f)
                    p.wait("scalar", otr, ti + 1)
                    def f(e, mch=mch, slot=ti % 2):
                        e.copy(oT[:, mch * 128:(mch + 1) * 128],
                               pto[:, slot * 1024:slot * 1024 + 128]).then_inc(ote.h, 1)
                    p.on("scalar", f); ote.v += 1
                otr.v = (k + 1) * NKC
                p.wait("vector", ote, ote.v)
                p.wait("vector", ldxc, 16 * (k + 1))
                if k >= 2:
                    p.wait("vector", sx1d, 16 * (k - 1))
                def f(e, k=k):
                    e.tensor_tensor(out=x1c[k % 2][:], in0=xc_b[k % 2][:], in1=oT[:], op=ALU.add).then_inc(sx1.h, 1)
                    e.drain()
                    for i in range(4):
                        e.bn_stats(xstat[:, 6 * i:6 * (i + 1)], x1c[k % 2][:, i * 512:(i + 1) * 512])
                    e.drain()
                    e.bn_aggr(xaggr[:], xstat[:])
                    e.drain()
                    e.tensor_scalar(out=nmean[:], in0=xaggr[:, 0:1], scalar1=-1.0, scalar2=None, op0=ALU.mult)
                    e.tensor_scalar(out=rstd[:], in0=xaggr[:, 1:2], scalar1=EPS, scalar2=None, op0=ALU.add).then_inc(lnq.h, 1)
                p.on("vector", f); sx1.v += 1; lnq.v += 1
                p.wait("scalar", sx1, sx1.v)
                p.wait("scalar", lnq, lnq.v)
                def f(e):
                    e.activation(out=rstd[:], in_=rstd[:], func=AF.Ln)
                    e.drain()
                    e.activation(out=rstd[:], in_=rstd[:], func=AF.Exp, scale=-0.5).then_inc(lnr.h, 1)
                p.on("scalar", f); lnr.v += 1
                p.wait("vector", lnr, lnr.v)
                if k >= 1:
                    p.wait("vector", ytr, k * NKC)
                def f(e, k=k):
                    e.tensor_scalar(out=ytm[:], in0=x1c[k % 2][:], scalar1=nmean[:], scalar2=rstd[:],
                                    op0=ALU.add, op1=ALU.mult).then_inc(syt.h, 1)
                p.on("vector", f); syt.v += 1
                p.wait("gpsimd", sx1, sx1.v)
                def f(e, tt=tt, c=c, k=k):
                    e.dma_start(out=x1_s[tt * TT + c * 128:tt * TT + (c + 1) * 128, :], in_=x1c[k % 2][:]).then_inc(sx1d.h, 16)
                p.on("gpsimd", f); sx1d.v += 16
                p.wait("tensor", syt, syt.v)
                for kc in range(NKC):
                    ti = k * NKC + kc
                    if ti >= 2:
                        p.wait("tensor", yev, ti - 1)
                    def f(e, kc=kc, slot=ti % 2):
                        e.transpose(pty[:, slot * 1024:slot * 1024 + 128],
                                    ytm[:, kc * 128:(kc + 1) * 128], ident[:]).then_inc(ytr.h, 1)
                    p.on("tensor", f)
                    p.wait("scalar", ytr, ti + 1)
                    def f(e, kc=kc, c=c, slot=ti % 2):
                        e.copy(yfm[:, kc * TT + c * 128:kc * TT + c * 128 + 128],
                               pty[:, slot * 1024:slot * 1024 + 128]).then_inc(yev.h, 1)
                    p.on("scalar", f); yev.v += 1
                ytr.v = (k + 1) * NKC
            p.wait("gpsimd", yev, yev.v)
            def f(e, tt=tt):
                e.dma_start(out=y_s[:, tt * TT:(tt + 1) * TT].rearrange("(kc pp) t -> pp kc t", pp=128),
                            in_=yfm[:].rearrange("pp (kc t) -> pp kc t", kc=NKC)).then_inc(syd.h, 16)
            p.on("gpsimd", f); syd.v += 16
            p.wait("scalar", syd, syd.v)
        p.wait("gpsimd", syd, syd.v)
        p.wait("gpsimd", sx1d, sx1d.v)
        blk = es.enter_context(nc.Block("p3"))
        p.emit(blk)


DBG_P4 = []
def _p4(nc, y_s, x1_s, wg_s, wu_s, wd_s, out, ident):
    TT4 = 1024
    NT4 = S // TT4
    NG = 1
    n_groups = (NIC + NG - 1) // NG
    with contextlib.ExitStack() as es:
        yfm = es.enter_context(nc.sbuf_tensor("f_yfm", [128, NKC * TT4], BF16))
        hb = es.enter_context(nc.sbuf_tensor("f_h", [128, NIC * TT4], BF16))
        wgu = [es.enter_context(nc.sbuf_tensor(f"f_wgu{i}", [128, 2 * NKC * NG * 128], BF16)) for i in range(2)]
        wdb = [es.enter_context(nc.sbuf_tensor(f"f_wd{i}", [128, INTER], BF16)) for i in range(2)]
        sgl = es.enter_context(nc.sbuf_tensor("f_sgl", [128, 512], F32))
        dstage = es.enter_context(nc.sbuf_tensor("f_dst", [128, 2 * NKC * 512], BF16))
        dT = es.enter_context(nc.sbuf_tensor("f_dT", [128, H], BF16))
        x1b = [es.enter_context(nc.sbuf_tensor("f_x1", [128, H], F32))] * 2
        
        pgu = es.enter_context(nc.psum_tensor("f_pgu", [128, 4 * 512], F32))
        pd = es.enter_context(nc.psum_tensor("f_pd", [128, 2 * 512], F32))
        ptd = es.enter_context(nc.psum_tensor("f_ptd", [128, 2048], BF16))
        ldy = Sem(es, nc, "f_ldy")
        ldw = Sem(es, nc, "f_ldw")
        ldd = Sem(es, nc, "f_ldd")
        ldx1 = Sem(es, nc, "f_ldx1")
        mmg = Sem(es, nc, "f_mmg")
        sil = Sem(es, nc, "f_sil")
        hwr = Sem(es, nc, "f_hwr")
        mmd = Sem(es, nc, "f_mmd")
        dev = Sem(es, nc, "f_dev")
        dtr = Sem(es, nc, "f_dtr")
        dte = Sem(es, nc, "f_dte")
        sou = Sem(es, nc, "f_sou")
        sod = Sem(es, nc, "f_sod")
        p = Prog()

        grp_ics = [list(range(g * NG, min((g + 1) * NG, NIC))) for g in range(n_groups)]
        grp_units = [2 * len(g) for g in grp_ics]
        cumu = [0]
        for g in grp_units:
            cumu.append(cumu[-1] + g)
        units_per_tile = cumu[-1]
        gcum = [0]
        for t4_ in range(NT4):
            for g_ in range(n_groups):
                gcum.append(gcum[-1] + grp_units[g_])

        for t4 in range(NT4):
            if t4 >= 1:
                p.wait("sync", hwr, t4 * units_per_tile)
                p.wait("vector", mmd, t4 * NKC * 2)
            def f(e, t4=t4):
                e.dma_start(out=yfm[:].rearrange("pp (kc t) -> pp kc t", kc=NKC), in_=y_s[:, t4 * TT4:(t4 + 1) * TT4].rearrange("(kc pp) t -> pp kc t", pp=128)).then_inc(ldy.h, 16)
            p.on("sync", f); ldy.v += 16
            for gi in range(n_groups):
                ics = grp_ics[gi]
                n = len(ics)
                gj = t4 * n_groups + gi
                gslot = gj % 2
                if gj >= 2:
                    p.wait("sync", hwr, gcum[gj - 1])
                def f(e, ics=ics, n=n, buf=wgu[gslot]):
                    e.dma_start(out=buf[:, 0:NKC * n * 128].rearrange("pp (kc j) -> pp kc j", kc=NKC),
                                in_=wg_s[:, :, ics[0] * 128:ics[0] * 128 + n * 128].rearrange("kc pp j -> pp kc j")).then_inc(ldw.h, 16)
                    e.dma_start(out=buf[:, NKC * NG * 128:NKC * NG * 128 + NKC * n * 128].rearrange("pp (kc j) -> pp kc j", kc=NKC),
                                in_=wu_s[:, :, ics[0] * 128:ics[0] * 128 + n * 128].rearrange("kc pp j -> pp kc j")).then_inc(ldw.h, 16)
                p.on("sync", f); ldw.v += 32
                p.wait("tensor", ldw, ldw.v)
                if gi == 0:
                    p.wait("tensor", ldy, ldy.v)
                for icl, ic in enumerate(ics):
                    for tc in range(2):
                        ui = t4 * units_per_tile + cumu[gi] + icl * 2 + tc
                        gs = ui % 2
                        if ui >= 2:
                            p.wait("tensor", sil, ui - 1)
                            p.wait("tensor", hwr, ui - 1)
                        for kc in range(NKC):
                            last = (kc == NKC - 1)
                            def f(e, icl=icl, kc=kc, tc=tc, gs=gs, gslot=gslot, last=last, n=n):
                                e.matmul(pgu[:, gs * 512:gs * 512 + 512],
                                         wgu[gslot][:, (kc * n + icl) * 128:(kc * n + icl) * 128 + 128],
                                         yfm[:, kc * TT4 + tc * 512:kc * TT4 + tc * 512 + 512],
                                         start=(kc == 0), stop=last)
                                i_ = e.matmul(pgu[:, (2 + gs) * 512:(2 + gs) * 512 + 512],
                                              wgu[gslot][:, NKC * NG * 128 + (kc * n + icl) * 128:NKC * NG * 128 + (kc * n + icl) * 128 + 128],
                                              yfm[:, kc * TT4 + tc * 512:kc * TT4 + tc * 512 + 512],
                                              start=(kc == 0), stop=last)
                                if last:
                                    i_.then_inc(mmg.h, 1)
                            p.on("tensor", f)
                        mmg.v += 1
                        p.wait("scalar", mmg, mmg.v)
                        if ui >= 1:
                            p.wait("scalar", hwr, ui)
                        def f(e, gs=gs):
                            e.activation(out=sgl[:], in_=pgu[:, gs * 512:gs * 512 + 512], func=AF.Silu).then_inc(sil.h, 1)
                        p.on("scalar", f); sil.v += 1
                        p.wait("vector", sil, sil.v)
                        def f(e, ic=ic, tc=tc, gs=gs):
                            e.tensor_tensor(out=hb[:, ic * TT4 + tc * 512:ic * TT4 + tc * 512 + 512],
                                            in0=sgl[:], in1=pgu[:, (2 + gs) * 512:(2 + gs) * 512 + 512],
                                            op=ALU.mult).then_inc(hwr.h, 1)
                        p.on("vector", f); hwr.v += 1
            for m in range(NKC):
                md = t4 * NKC + m
                if md >= 2:
                    p.wait("sync", mmd, (md - 1) * 2)
                def f(e, m=m, buf=wdb[md % 2]):
                    e.dma_start(out=buf[:], in_=wd_s[m, :, :]).then_inc(ldd.h, 16)
                p.on("sync", f); ldd.v += 16
                p.wait("tensor", ldd, ldd.v)
                if m == 0:
                    p.wait("tensor", hwr, (t4 + 1) * units_per_tile)
                for tc in range(2):
                    di = md * 2 + tc
                    if di >= 2:
                        p.wait("tensor", dev, di - 1)
                    for ic in range(NIC):
                        last = (ic == NIC - 1)
                        def f(e, ic=ic, tc=tc, slot=di % 2, buf=wdb[md % 2], last=last):
                            i_ = e.matmul(pd[:, slot * 512:slot * 512 + 512],
                                          buf[:, ic * 128:(ic + 1) * 128],
                                          hb[:, ic * TT4 + tc * 512:ic * TT4 + tc * 512 + 512],
                                          start=(ic == 0), stop=last)
                            if last:
                                i_.then_inc(mmd.h, 1)
                        p.on("tensor", f)
                    mmd.v += 1
                    p.wait("scalar", mmd, mmd.v)
                    if t4 >= 1 and di < 2:
                        p.wait("scalar", dtr, t4 * 8 * NKC)
                    def f(e, m=m, tc=tc, slot=di % 2):
                        e.copy(dstage[:, tc * NKC * 512 + m * 512:tc * NKC * 512 + m * 512 + 512],
                               pd[:, slot * 512:slot * 512 + 512]).then_inc(dev.h, 1)
                    p.on("scalar", f); dev.v += 1
            p.wait("tensor", dev, dev.v)
            for tc in range(2):
                for c in range(4):
                    k = t4 * 8 + tc * 4 + c
                    if k >= 1:
                        p.wait("sync", sou, k)
                        p.wait("sync", sod, 16 * k)
                    def f(e, t4=t4, tc=tc, c=c, buf=x1b[k % 2]):
                        e.dma_start(out=buf[:], in_=x1_s[t4 * TT4 + tc * 512 + c * 128:t4 * TT4 + tc * 512 + (c + 1) * 128, :]).then_inc(ldx1.h, 16)
                    p.on("sync", f); ldx1.v += 16
                    for m in range(NKC):
                        ti = k * NKC + m
                        if ti >= 2:
                            p.wait("tensor", dte, ti - 1)
                        def f(e, m=m, tc=tc, c=c, slot=ti % 2):
                            e.transpose(ptd[:, slot * 1024:slot * 1024 + 128],
                                        dstage[:, tc * NKC * 512 + m * 512 + c * 128:tc * NKC * 512 + m * 512 + c * 128 + 128],
                                        ident[:]).then_inc(dtr.h, 1)
                        p.on("tensor", f)
                        p.wait("scalar", dtr, ti + 1)
                        def f(e, m=m, slot=ti % 2):
                            e.copy(dT[:, m * 128:(m + 1) * 128],
                                   ptd[:, slot * 1024:slot * 1024 + 128]).then_inc(dte.h, 1)
                        p.on("scalar", f); dte.v += 1
                    dtr.v = (k + 1) * NKC
                    p.wait("vector", dte, dte.v)
                    p.wait("vector", ldx1, 16 * (k + 1))
                    if k >= 1:
                        p.wait("vector", sod, 16 * k)
                    def f(e, k=k):
                        e.tensor_tensor(out=x1b[k % 2][:], in0=x1b[k % 2][:], in1=dT[:], op=ALU.add).then_inc(sou.h, 1)
                    p.on("vector", f); sou.v += 1
                    p.wait("scalar", sou, sou.v)
                    p.wait("gpsimd", sou, sou.v)
                    def f(e, t4=t4, tc=tc, c=c, k=k):
                        e.dma_start(out=out[t4 * TT4 + tc * 512 + c * 128:t4 * TT4 + tc * 512 + (c + 1) * 128, :],
                                    in_=x1b[k % 2][:]).then_inc(sod.h, 16)
                    p.on("gpsimd", f); sod.v += 16
        p.wait("gpsimd", sod, sod.v)
        if DBG_P4:
            hdmp = nc.dram_tensor("dbg_h_last", [128, NIC * TT4], BF16, kind="ExternalOutput").ap()
            ddmp = nc.dram_tensor("dbg_dst_last", [128, 2 * NKC * 512], BF16, kind="ExternalOutput").ap()
            hst = Sem(es, nc, "f_hdmp")
            def f(e):
                e.dma_start(out=hdmp[:, :], in_=hb[:]).then_inc(hst.h, 16)
                e.dma_start(out=ddmp[:, :], in_=dstage[:]).then_inc(hst.h, 16)
                e.wait_ge(hst.h, 32)
            p.on("gpsimd", f)
        blk = es.enter_context(nc.Block("p4"))
        p.emit(blk)


from concourse.bass_utils import run_bass_kernel_spmd

_NC_CACHE = []

def kernel(L0_attn_hidden_states, L0_attn_Wq, L0_attn_Wk, L0_attn_Wv, L0_attn_Wo,
           L0_ffn_lut, L0_ffn_gate_idx, L0_ffn_up_idx, L0_ffn_down_idx):
    x = np.ascontiguousarray(np.asarray(L0_attn_hidden_states, dtype=np.float32))
    wq_a = np.ascontiguousarray(np.asarray(L0_attn_Wq, dtype=np.float32))
    wk_a = np.ascontiguousarray(np.asarray(L0_attn_Wk, dtype=np.float32))
    wv_a = np.ascontiguousarray(np.asarray(L0_attn_Wv, dtype=np.float32))
    wo_a = np.ascontiguousarray(np.asarray(L0_attn_Wo, dtype=np.float32))
    lut_a = np.asarray(L0_ffn_lut, dtype=np.float32)
    lutr = np.ascontiguousarray(np.repeat(lut_a[None, :], 128, axis=0))
    gi = np.ascontiguousarray(np.asarray(L0_ffn_gate_idx, dtype=np.int32))
    ui = np.ascontiguousarray(np.asarray(L0_ffn_up_idx, dtype=np.int32))
    di = np.ascontiguousarray(np.asarray(L0_ffn_down_idx, dtype=np.int32))

    if not _NC_CACHE:
        _NC_CACHE.append(build()[0])
    nc = _NC_CACHE[0]
    in_maps = []
    for core in range(8):
        in_maps.append({
            "x": np.ascontiguousarray(x[core]),
            "wq": wq_a, "wk": wk_a, "wv": wv_a, "wo": wo_a,
            "lutr": lutr, "gidx": gi, "uidx": ui, "didx": di,
        })
    res = run_bass_kernel_spmd(nc, in_maps, core_ids=list(range(8)))
    return np.stack([np.asarray(res.results[i]["out"], dtype=np.float32) for i in range(8)], axis=0)


# version 13
# speedup vs baseline: 1.1873x; 1.0208x over previous
import contextlib
import numpy as np
import concourse.bass as bass
import concourse.mybir as mybir

F32 = mybir.dt.float32
BF16 = mybir.dt.bfloat16
I32 = mybir.dt.int32
AF = mybir.ActivationFunctionType
ALU = mybir.AluOpType

S = 4096
H = 2048
D = 128
NH = 16
KVH = 4
WIN = 1024
INTER = 5504
NIC = INTER // 128
NKC = H // 128
TT = 512
NTILE = S // TT
SCALE = 1.0 / float(np.sqrt(D))
NEG = -1.0e9
EPS = 1e-5

ENGINES = ("sync", "vector", "scalar", "tensor", "gpsimd")


class Sem:
    def __init__(self, es, nc, name):
        self.h = es.enter_context(nc.semaphore(name))
        self.v = 0


class Prog:
    def __init__(self):
        self.ops = {e: [] for e in ENGINES}

    def on(self, eng, fn):
        self.ops[eng].append(fn)

    def wait(self, eng, sem, val):
        if val > 0:
            self.ops[eng].append(lambda e, s=sem.h, v=val: e.wait_ge(s, v))

    def emit(self, blk):
        for eng in ENGINES:
            if not self.ops[eng]:
                continue
            def body(e, lst=tuple(self.ops[eng])):
                for fn in lst:
                    fn(e)
            getattr(blk, eng)(body)


def build(phases=("p0", "p1a", "p1b", "p2", "p3", "p4"), dbg=()):
    nc = bass.Bass("TRN2", target_bir_lowering=False)

    x = nc.dram_tensor("x", [S, H], F32, kind="ExternalInput").ap()
    wq = nc.dram_tensor("wq", [H, H], F32, kind="ExternalInput").ap()
    wk = nc.dram_tensor("wk", [H, KVH * D], F32, kind="ExternalInput").ap()
    wv = nc.dram_tensor("wv", [H, KVH * D], F32, kind="ExternalInput").ap()
    wo = nc.dram_tensor("wo", [H, H], F32, kind="ExternalInput").ap()
    lutr = nc.dram_tensor("lutr", [128, 16], F32, kind="ExternalInput").ap()
    gidx = nc.dram_tensor("gidx", [INTER, H], I32, kind="ExternalInput").ap()
    uidx = nc.dram_tensor("uidx", [INTER, H], I32, kind="ExternalInput").ap()
    didx = nc.dram_tensor("didx", [H, INTER], I32, kind="ExternalInput").ap()
    out = nc.dram_tensor("out", [S, H], F32, kind="ExternalOutput").ap()

    xln_s = nc.dram_tensor("xln_s", [H, S], BF16, kind="Internal").ap()
    q_s = nc.dram_tensor("q_s", [H, S], BF16, kind="Internal").ap()
    ao_s = nc.dram_tensor("ao_s", [H, S], BF16, kind="Internal").ap()
    p4_standalone = ("p4" in phases and "p3" not in phases)
    x1_s = nc.dram_tensor("x1_s", [S, H], F32, kind="ExternalInput" if p4_standalone else "Internal").ap()
    y_s = nc.dram_tensor("y_s", [H, S], BF16, kind="ExternalInput" if p4_standalone else "Internal").ap()
    wq_b = nc.dram_tensor("wq_b", [H, H], BF16, kind="Internal").ap()
    wo_b = nc.dram_tensor("wo_b", [H, H], BF16, kind="Internal").ap()
    wg_s = nc.dram_tensor("wg_s", [NKC, 128, INTER], BF16, kind="Internal").ap()
    wu_s = nc.dram_tensor("wu_s", [NKC, 128, INTER], BF16, kind="Internal").ap()
    wd_s = nc.dram_tensor("wd_s", [NKC, 128, INTER], BF16, kind="Internal").ap()

    dbg_aps = {}
    with contextlib.ExitStack() as top:
        ident = top.enter_context(nc.sbuf_tensor("ident", [128, 128], BF16))
        slut = top.enter_context(nc.sbuf_tensor("slut", [128, 16], F32))
        ones_col = top.enter_context(nc.sbuf_tensor("ones_col", [128, 1], BF16))
        ones_row = top.enter_context(nc.sbuf_tensor("ones_row", [1, 128], F32))

        dlut = top.enter_context(nc.sbuf_tensor("dlut", [128, 16], F32))
        _setup(nc, lutr, slut, ident, ones_col, ones_row, dlut, wq, wq_b, wo, wo_b)

        gath = None
        if "p0" in phases:
            g_es = top.enter_context(contextlib.ExitStack())
            gath = Gather(nc, g_es, slut, ident, gidx, uidx, didx, wg_s, wu_s, wd_s, dlut)
        with contextlib.ExitStack() as kvscope:
            k_cache = kvscope.enter_context(nc.sbuf_tensor("k_cache", [128, KVH * S], BF16))
            v_cache = kvscope.enter_context(nc.sbuf_tensor("v_cache", [128, KVH * S], BF16))
            if "p1a" in phases:
                _p1a(nc, x, wk, wv, xln_s, k_cache, v_cache, ident)
            if "p1b" in phases:
                _p1b(nc, wq_b, xln_s, q_s, gath)
            if "p2" in phases:
                _p2(nc, q_s, ao_s, k_cache, v_cache, ones_col, ones_row, gath, ident)
            if "kv" in dbg:
                for nm, t in (("k_cache", k_cache), ("v_cache", v_cache)):
                    dst = nc.dram_tensor("dbg_" + nm, [128, KVH * S], BF16, kind="ExternalOutput").ap()
                    dbg_aps[nm] = dst
                    _sbuf_dump(nc, dst, t)
        if "p3" in phases:
            _p3(nc, x, wo_b, ao_s, x1_s, y_s, ident, gath)
        if gath is not None:
            _p0(nc, gath)
        if "p4" in phases:
            _p4(nc, y_s, x1_s, wg_s, wu_s, wd_s, out, ident)
        else:
            _dump_stub(nc, out)

        for name, src, shp, dt in (
            ("xln_s", xln_s, [H, S], BF16), ("q_s", q_s, [H, S], BF16),
            ("ao_s", ao_s, [H, S], BF16), ("x1_s", x1_s, [S, H], F32),
            ("y_s", y_s, [H, S], BF16),
            ("wg_s", wg_s, [NKC, 128, INTER], BF16),
            ("wu_s", wu_s, [NKC, 128, INTER], BF16),
            ("wd_s", wd_s, [NKC, 128, INTER], BF16),
        ):
            if name in dbg:
                dst = nc.dram_tensor("dbg_" + name, shp, dt, kind="ExternalOutput").ap()
                dbg_aps[name] = dst
                _dram_copy(nc, dst, src, shp, dt)
    return nc, dbg_aps


def _setup(nc, lutr, slut, ident, ones_col, ones_row, dlut, wq, wq_b, wo, wo_b):
    with contextlib.ExitStack() as es:
        sidi = es.enter_context(nc.sbuf_tensor("sidi", [128, 128], I32))
        wtmp = [es.enter_context(nc.sbuf_tensor(f"su_wt{i}", [128, H], F32)) for i in range(2)]
        wbb = [es.enter_context(nc.sbuf_tensor(f"su_wb{i}", [128, H], BF16)) for i in range(2)]
        ldw = Sem(es, nc, "su_ldw")
        cst = Sem(es, nc, "su_cst")
        stw = Sem(es, nc, "su_stw")
        ld = Sem(es, nc, "su_ld")
        dn = Sem(es, nc, "su_dn")
        p = Prog()
        for w_i, (wsrc, wdst) in enumerate(((wq, wq_b), (wo, wo_b))):
            for kc in range(NKC):
                j = w_i * NKC + kc
                if j >= 2:
                    p.wait("sync", cst, j - 1)
                def f(e, wsrc=wsrc, kc=kc, buf=wtmp[j % 2]):
                    e.dma_start(out=buf[:], in_=wsrc[kc * 128:(kc + 1) * 128, :]).then_inc(ldw.h, 16)
                p.on("sync", f); ldw.v += 16
                p.wait("vector", ldw, ldw.v)
                if j >= 2:
                    p.wait("vector", stw, 16 * (j - 1))
                def f(e, kc=kc, src_b=wtmp[j % 2], dst_b=wbb[j % 2]):
                    e.tensor_copy(dst_b[:], src_b[:]).then_inc(cst.h, 1)
                p.on("vector", f); cst.v += 1
                p.wait("gpsimd", cst, cst.v)
                def f(e, wdst=wdst, kc=kc, buf=wbb[j % 2]):
                    e.dma_start(out=wdst[kc * 128:(kc + 1) * 128, :], in_=buf[:]).then_inc(stw.h, 16)
                p.on("gpsimd", f); stw.v += 16
        p.wait("gpsimd", stw, stw.v)
        def f(e):
            e.dma_start(out=slut[:], in_=lutr[:, :]).then_inc(ld.h, 16)
        p.on("sync", f); ld.v += 16
        def f(e):
            e.iota(sidi[:], pattern=[[-1, 128]], base=0, channel_multiplier=1).then_inc(dn.h, 1)
        p.on("gpsimd", f); dn.v += 1
        p.wait("vector", dn, dn.v)
        p.wait("vector", ld, ld.v)
        def f(e):
            e.tensor_scalar(out=ident[:], in0=sidi[:], scalar1=0, scalar2=None, op0=ALU.is_equal)
            e.memset(ones_col[:], 1.0)
            e.memset(ones_row[:], 1.0)
            e.drain()
            e.tensor_scalar(out=dlut[:], in0=slut[:], scalar1=slut[:, 15:16], scalar2=None, op0=ALU.subtract)
        p.on("vector", f)
        blk = es.enter_context(nc.Block("setup"))
        p.emit(blk)


def _dump_stub(nc, out):
    with contextlib.ExitStack() as es:
        z = es.enter_context(nc.sbuf_tensor("zz", [128, H], F32))
        st = Sem(es, nc, "z_st")
        dn = Sem(es, nc, "z_dn")
        p = Prog()
        def f(e):
            e.memset(z[:], 0.0).then_inc(dn.h, 1)
        p.on("vector", f); dn.v += 1
        p.wait("gpsimd", dn, dn.v)
        def f(e):
            for i in range(S // 128):
                e.dma_start(out=out[i * 128:(i + 1) * 128, :], in_=z[:]).then_inc(st.h, 16)
            e.wait_ge(st.h, 16 * (S // 128))
        p.on("gpsimd", f); st.v += 16 * (S // 128)
        blk = es.enter_context(nc.Block("zout"))
        p.emit(blk)


_DP_N = [0]
def _sbuf_dump(nc, dst, src):
    _DP_N[0] += 1
    uid = _DP_N[0]
    with contextlib.ExitStack() as es:
        st = Sem(es, nc, f"dmp_st{uid}")
        p = Prog()
        def f(e):
            e.dma_start(out=dst[:, :], in_=src[:]).then_inc(st.h, 16)
            e.wait_ge(st.h, 16)
        p.on("gpsimd", f)
        blk = es.enter_context(nc.Block(f"dmp{uid}"))
        p.emit(blk)


_CP_N = [0]
def _dram_copy(nc, dst, src, shp, dt):
    total = int(np.prod(shp))
    FREE = 8192
    _CP_N[0] += 1
    uid = _CP_N[0]
    with contextlib.ExitStack() as es:
        buf = es.enter_context(nc.sbuf_tensor(f"cp_buf{uid}", [128, FREE], dt))
        ld = Sem(es, nc, f"cp_ld{uid}")
        st = Sem(es, nc, f"cp_st{uid}")
        p = Prog()
        if len(shp) == 2:
            fsrc = src.rearrange("a b -> (a b)")
            fdst = dst.rearrange("a b -> (a b)")
        else:
            fsrc = src.rearrange("a b c -> (a b c)")
            fdst = dst.rearrange("a b c -> (a b c)")
        chunk = 128 * FREE
        nch = (total + chunk - 1) // chunk
        def f(e):
            for i in range(nch):
                n = min(chunk, total - i * chunk)
                rows = n // FREE
                e.wait_ge(st.h, 16 * i)
                e.dma_start(out=buf[0:rows, :], in_=fsrc[i * chunk:i * chunk + n].rearrange("(pp ff) -> pp ff", ff=FREE)).then_inc(ld.h, 16)
                e.wait_ge(ld.h, 16 * (i + 1))
                e.dma_start(out=fdst[i * chunk:i * chunk + n].rearrange("(pp ff) -> pp ff", ff=FREE), in_=buf[0:rows, :]).then_inc(st.h, 16)
            e.wait_ge(st.h, 16 * nch)
        p.on("gpsimd", f)
        blk = es.enter_context(nc.Block(f"cp{uid}"))
        p.emit(blk)


class Gather:
    def __init__(self, nc, es, slut, ident, gidx, uidx, didx, wg_s, wu_s, wd_s, dlut):
        self.nc = nc
        self.slut = slut
        self.dlut = dlut
        self.ident = ident
        units = []
        for b_ in range(NIC):
            units.append((gidx[b_ * 128:(b_ + 1) * 128, :], wg_s, b_, 0, H, "gu"))
        for b_ in range(NIC):
            units.append((uidx[b_ * 128:(b_ + 1) * 128, :], wu_s, b_, 0, H, "gu"))
        for b_ in range(NKC):
            for c0, w in ((0, 2048), (2048, 2048), (4096, 1408)):
                units.append((didx[b_ * 128:(b_ + 1) * 128, c0:c0 + w], wd_s, b_, c0, w, "d"))
        self.units = units
        self.cursor = 0
        self.ld = Sem(es, nc, "G_ld")
        self.gat = Sem(es, nc, "G_gat")
        self.trn = Sem(es, nc, "G_trn")
        self.evc = Sem(es, nc, "G_evc")
        self.stg = Sem(es, nc, "G_stg")
        self.npc_hist = []
        self.phase_first = 0
        self._t = None

    def start_phase(self, es, width=2048):
        nc = self.nc
        W = width
        self.bw = W
        self.sidx = [es.enter_context(nc.sbuf_tensor(f"G_idx{i}_{self.cursor}", [128, W], I32)) for i in range(2)]
        self.sidxb = [es.enter_context(nc.sbuf_tensor(f"G_idxb{i}_{self.cursor}", [128, W], BF16)) for i in range(2)]
        self.smask = [es.enter_context(nc.sbuf_tensor(f"G_msk{i}_{self.cursor}", [128, W // 2], BF16)) for i in range(2)]
        self.smlut = [es.enter_context(nc.sbuf_tensor(f"G_mlt{i}_{self.cursor}", [128, W // 2], BF16)) for i in range(2)]
        self.sacc = [es.enter_context(nc.sbuf_tensor(f"G_acc{i}_{self.cursor}", [128, W], BF16)) for i in range(2)]
        self.sstage = [es.enter_context(nc.sbuf_tensor(f"G_stg{i}_{self.cursor}", [128, W], BF16)) for i in range(2)]
        self.pt = es.enter_context(nc.psum_tensor(f"G_pt_{self.cursor}", [128, 2048], BF16))
        self.phase_first = self.cursor

    def remaining(self):
        return len(self.units) - self.cursor

    def emit_units(self, p, n):
        for _ in range(min(n, self.remaining())):
            self._unit_head(p)
            while self._t < 16:
                self._emit_slice(p, 16)
            self._unit_tail(p)

    def emit_slice(self, p, nt=4):
        if self.cursor >= len(self.units):
            return False
        if self._t is None:
            self._unit_head(p)
        self._emit_slice(p, nt)
        if self._t >= 16:
            self._unit_tail(p)
        return True

    def flush_phase(self, p):
        if self._t is not None:
            while self._t < 16:
                self._emit_slice(p, 16)
            self._unit_tail(p)

    def _unit_head(self, p):
        u = self.cursor
        src_ap, dst, b_, c0, fw, kind = self.units[u]
        assert fw <= self.bw
        pl = u - self.phase_first
        slot = pl % 2
        if pl >= 2:
            p.wait("sync", self.gat, u - 1)
        def f(e, src=src_ap, buf=self.sidx[slot], fw=fw, ld=self.ld):
            e.dma_start(out=buf[:, 0:fw], in_=src).then_inc(ld.h, 16)
        p.on("sync", f); self.ld.v += 16
        p.wait("vector", self.ld, self.ld.v)
        if pl >= 2:
            p.wait("vector", self.trn, sum(self.npc_hist) - self.npc_hist[-1])
        self._t = 0

    def _emit_slice(self, p, nt):
        u = self.cursor
        src_ap, dst, b_, c0, fw, kind = self.units[u]
        hw = fw // 2
        pl = u - self.phase_first
        slot = pl % 2
        t0, t1 = self._t, min(15, self._t + nt)
        dlut = self.dlut
        def f(e, idx=self.sidx[slot], idxb=self.sidxb[slot], acc=self.sacc[slot],
              hw=hw, msk=self.smask, mlt=self.smlut, gat=self.gat, t0=t0, t1=t1):
            for t in range(t0, t1):
                if t == 0:
                    for hh in range(2):
                        e.tensor_scalar(out=idxb[:, hh * hw:hh * hw + hw], in0=idx[:, hh * hw:hh * hw + hw],
                                        scalar1=0.0, scalar2=None, op0=ALU.add)
                for hh in range(2):
                    e.tensor_scalar(out=msk[hh][:, 0:hw], in0=idxb[:, hh * hw:hh * hw + hw],
                                    scalar1=float(t), scalar2=None, op0=ALU.is_equal)
                for hh in range(2):
                    if t == 0:
                        e.tensor_scalar(out=acc[:, hh * hw:hh * hw + hw], in0=msk[hh][:, 0:hw],
                                        scalar1=dlut[:, 0:1], scalar2=self.slut[:, 15:16],
                                        op0=ALU.mult, op1=ALU.add)
                    else:
                        e.tensor_scalar(out=mlt[hh][:, 0:hw], in0=msk[hh][:, 0:hw],
                                        scalar1=dlut[:, t:t + 1], scalar2=None, op0=ALU.mult)
                if t > 0:
                    for hh in range(2):
                        i_ = e.tensor_tensor(out=acc[:, hh * hw:hh * hw + hw],
                                             in0=acc[:, hh * hw:hh * hw + hw],
                                             in1=mlt[hh][:, 0:hw], op=ALU.add)
                        if t == 14 and hh == 1:
                            i_.then_inc(gat.h, 1)
        p.on("vector", f)
        if t1 == 15:
            self.gat.v += 1
        self._t = t1 if t1 < 15 else 16

    def _unit_tail(self, p):
        u = self.cursor
        src_ap, dst, b_, c0, fw, kind = self.units[u]
        npc = fw // 128
        pl = u - self.phase_first
        slot = pl % 2
        prior_npc = sum(self.npc_hist)
        ident = self.ident
        p.wait("tensor", self.gat, self.gat.v)
        for pc in range(npc):
            ti = prior_npc + pc
            if ti >= 2:
                p.wait("tensor", self.evc, ti - 1)
            def f(e, acc=self.sacc[slot], pc=pc, pslot=ti % 2, pt=self.pt):
                e.transpose(pt[:, pslot * 1024:pslot * 1024 + 128],
                            acc[:, pc * 128:pc * 128 + 128], ident[:]).then_inc(self.trn.h, 1)
            p.on("tensor", f)
        self.trn.v = prior_npc + npc
        if pl >= 2:
            p.wait("scalar", self.stg, 16 * (u - 1))
        for pc in range(npc):
            ti = prior_npc + pc
            p.wait("scalar", self.trn, ti + 1)
            def f(e, stage=self.sstage[slot], pc=pc, pslot=ti % 2, pt=self.pt):
                e.copy(stage[:, pc * 128:pc * 128 + 128],
                       pt[:, pslot * 1024:pslot * 1024 + 128]).then_inc(self.evc.h, 1)
            p.on("scalar", f)
        self.evc.v = prior_npc + npc
        if kind == "gu":
            def f(e, dst=dst, b_=b_, stage=self.sstage[slot]):
                e.dma_start(out=dst[:, :, b_ * 128:(b_ + 1) * 128].rearrange("kc pp j -> pp kc j"),
                            in_=stage[:, 0:H].rearrange("pp (kc j) -> pp kc j", kc=NKC)).then_inc(self.stg.h, 16)
        else:
            def f(e, dst=dst, b_=b_, c0=c0, fw=fw, stage=self.sstage[slot]):
                e.dma_start(out=dst[b_, :, c0:c0 + fw], in_=stage[:, 0:fw]).then_inc(self.stg.h, 16)
        p.on("scalar", f); self.stg.v += 16
        self.npc_hist.append(npc)
        self.cursor += 1
        self._t = None

    def finish_wait(self, p, eng="scalar"):
        p.wait(eng, self.stg, self.stg.v)


def _p0(nc, g, n_units=None):
    if g.remaining() == 0:
        return
    n = g.remaining() if n_units is None else min(n_units, g.remaining())
    with contextlib.ExitStack() as es:
        g.start_phase(es, width=2048)
        p = Prog()
        g.emit_units(p, n)
        if g.remaining() == 0:
            g.finish_wait(p)
        blk = es.enter_context(nc.Block("p0"))
        p.emit(blk)


def _p1a(nc, x, wk, wv, xln_s, k_cache, v_cache, ident):
    with contextlib.ExitStack() as es:
        wkb = es.enter_context(nc.sbuf_tensor("wkb", [128, NKC * KVH * D], BF16))
        wvb = es.enter_context(nc.sbuf_tensor("wvb", [128, NKC * KVH * D], BF16))
        wtmp = [es.enter_context(nc.sbuf_tensor(f"wtmp{i}", [128, KVH * D], F32)) for i in range(2)]
        xbuf = [es.enter_context(nc.sbuf_tensor(f"xbuf{i}", [128, 4 * H], F32)) for i in range(2)]
        xstat = es.enter_context(nc.sbuf_tensor("xstat", [128, 24], F32))
        xaggr = es.enter_context(nc.sbuf_tensor("xaggr", [128, 2], F32))
        nmean = es.enter_context(nc.sbuf_tensor("nmean", [128, 1], F32))
        rstd = es.enter_context(nc.sbuf_tensor("rstd", [128, 1], F32))
        xln_tm = es.enter_context(nc.sbuf_tensor("xln_tm", [128, 2 * H], BF16))
        xlnfm = es.enter_context(nc.sbuf_tensor("xlnfm", [128, NKC * TT], BF16))
        vtmp = es.enter_context(nc.sbuf_tensor("vtmp", [128, TT], BF16))
        ptx = es.enter_context(nc.psum_tensor("p1_ptx", [128, 2048], BF16))
        ptv = es.enter_context(nc.psum_tensor("p1_ptv", [128, 2048], BF16))
        pkv = es.enter_context(nc.psum_tensor("p1_pkv", [128, 2 * TT], F32))
        ld = Sem(es, nc, "p1_ld")
        ldx = Sem(es, nc, "p1_ldx")
        cst = Sem(es, nc, "p1_cst")
        lnq = Sem(es, nc, "p1_lnq")
        lnr = Sem(es, nc, "p1_lnr")
        ntm = Sem(es, nc, "p1_ntm")
        trn = Sem(es, nc, "p1_trn")
        evc = Sem(es, nc, "p1_evc")
        mmk = Sem(es, nc, "p1_mmk")
        kve = Sem(es, nc, "p1_kve")
        vtr = Sem(es, nc, "p1_vtr")
        vev = Sem(es, nc, "p1_vev")
        sx = Sem(es, nc, "p1_sx")
        p = Prog()
        NCH = 4

        for w_i, (wsrc, wdst) in enumerate(((wk, wkb), (wv, wvb))):
            for kc in range(NKC):
                j = w_i * NKC + kc
                if j >= 2:
                    p.wait("sync", cst, j - 1)
                def f(e, wsrc=wsrc, kc=kc, buf=wtmp[j % 2], ld=ld):
                    e.dma_start(out=buf[:], in_=wsrc[kc * 128:(kc + 1) * 128, :]).then_inc(ld.h, 16)
                p.on("sync", f); ld.v += 16
                p.wait("vector", ld, 16 * (j + 1))
                def f(e, wdst=wdst, kc=kc, buf=wtmp[j % 2], cst=cst):
                    e.tensor_copy(wdst[:, kc * (KVH * D):(kc + 1) * (KVH * D)], buf[:]).then_inc(cst.h, 1)
                p.on("vector", f); cst.v += 1

        for tt in range(NTILE):
            if tt >= 2:
                p.wait("sync", ntm, (tt - 1) * NCH)
            def f(e, tt=tt, buf=xbuf[tt % 2], ldx=ldx):
                e.dma_start(out=buf[:].rearrange("pp (c hh) -> pp c hh", c=4), in_=x[tt * TT:(tt + 1) * TT, :].rearrange("(c pp) hh -> pp c hh", pp=128)).then_inc(ldx.h, 16)
            p.on("sync", f); ldx.v += 16
            p.wait("vector", ldx, 16 * (tt + 1))
            for c in range(NCH):
                k = tt * NCH + c
                def f(e, xc_buf=xbuf[tt % 2], c=c):
                    xc = xc_buf[:, c * H:(c + 1) * H]
                    for i in range(4):
                        e.bn_stats(xstat[:, 6 * i:6 * (i + 1)], xc[:, i * 512:(i + 1) * 512])
                    e.drain()
                    e.bn_aggr(xaggr[:], xstat[:])
                    e.drain()
                    e.tensor_scalar(out=nmean[:], in0=xaggr[:, 0:1], scalar1=-1.0, scalar2=None, op0=ALU.mult)
                    e.tensor_scalar(out=rstd[:], in0=xaggr[:, 1:2], scalar1=EPS, scalar2=None, op0=ALU.add).then_inc(lnq.h, 1)
                p.on("vector", f); lnq.v += 1
                p.wait("scalar", lnq, lnq.v)
                def f(e):
                    e.activation(out=rstd[:], in_=rstd[:], func=AF.Ln)
                    e.drain()
                    e.activation(out=rstd[:], in_=rstd[:], func=AF.Exp, scale=-0.5).then_inc(lnr.h, 1)
                p.on("scalar", f); lnr.v += 1
                p.wait("vector", lnr, lnr.v)
                if k >= 2:
                    p.wait("vector", trn, (k - 1) * NKC)
                def f(e, xc_buf=xbuf[tt % 2], c=c):
                    xc = xc_buf[:, c * H:(c + 1) * H]
                    e.tensor_scalar(out=xln_tm[:, (c % 2) * H:(c % 2) * H + H], in0=xc,
                                    scalar1=nmean[:], scalar2=rstd[:], op0=ALU.add, op1=ALU.mult).then_inc(ntm.h, 1)
                p.on("vector", f); ntm.v += 1
                p.wait("tensor", ntm, ntm.v)
                for kc in range(NKC):
                    ti = k * NKC + kc
                    if ti >= 2:
                        p.wait("tensor", evc, ti - 1)
                    def f(e, c=c, kc=kc, slot=ti % 2):
                        e.transpose(ptx[:, slot * 1024:slot * 1024 + 128],
                                    xln_tm[:, (c % 2) * H + kc * 128:(c % 2) * H + (kc + 1) * 128],
                                    ident[:]).then_inc(trn.h, 1)
                    p.on("tensor", f)
                trn.v = (k + 1) * NKC
                for kc in range(NKC):
                    ti = k * NKC + kc
                    p.wait("scalar", trn, ti + 1)
                    def f(e, c=c, kc=kc, slot=ti % 2):
                        e.copy(xlnfm[:, kc * TT + c * 128:kc * TT + c * 128 + 128],
                               ptx[:, slot * 1024:slot * 1024 + 128]).then_inc(evc.h, 1)
                    p.on("scalar", f)
                evc.v = (k + 1) * NKC
            p.wait("tensor", evc, evc.v)
            if tt == 0:
                p.wait("tensor", cst, 2 * NKC)
            for w_i, wsb in enumerate((wkb, wvb)):
                for kvh in range(KVH):
                    mi = tt * 2 * KVH + w_i * KVH + kvh
                    if mi >= 2:
                        p.wait("tensor", kve, mi - 1)
                    for kc in range(NKC):
                        last = (kc == NKC - 1)
                        def f(e, wsb=wsb, kvh=kvh, kc=kc, slot=mi % 2, last=last):
                            i_ = e.matmul(pkv[:, slot * TT:slot * TT + TT],
                                          wsb[:, kc * (KVH * D) + kvh * D:kc * (KVH * D) + (kvh + 1) * D],
                                          xlnfm[:, kc * TT:(kc + 1) * TT], start=(kc == 0), stop=last)
                            if last:
                                i_.then_inc(mmk.h, 1)
                        p.on("tensor", f)
                    mmk.v += 1
                    p.wait("scalar", mmk, mmk.v)
                    if w_i == 0:
                        def f(e, kvh=kvh, tt=tt, slot=mi % 2):
                            e.copy(k_cache[:, kvh * S + tt * TT:kvh * S + (tt + 1) * TT],
                                   pkv[:, slot * TT:slot * TT + TT]).then_inc(kve.h, 1)
                        p.on("scalar", f); kve.v += 1
                    else:
                        def f(e, slot=mi % 2):
                            e.copy(vtmp[:], pkv[:, slot * TT:slot * TT + TT]).then_inc(kve.h, 1)
                        p.on("scalar", f); kve.v += 1
                        p.wait("tensor", kve, kve.v)
                        for c in range(NCH):
                            vi = vtr.v
                            if vi >= 2:
                                p.wait("tensor", vev, vi - 1)
                            def f(e, c=c, slot=vi % 2):
                                e.transpose(ptv[:, slot * 1024:slot * 1024 + 128],
                                            vtmp[:, c * 128:(c + 1) * 128], ident[:]).then_inc(vtr.h, 1)
                            p.on("tensor", f); vtr.v += 1
                            p.wait("scalar", vtr, vtr.v)
                            def f(e, kvh=kvh, tt=tt, c=c, slot=vi % 2):
                                e.copy(v_cache[:, kvh * S + tt * TT + c * 128:kvh * S + tt * TT + (c + 1) * 128],
                                       ptv[:, slot * 1024:slot * 1024 + 128]).then_inc(vev.h, 1)
                            p.on("scalar", f); vev.v += 1
                        p.wait("scalar", vtr, vtr.v)
            p.wait("gpsimd", evc, evc.v)
            def f(e, tt=tt):
                e.dma_start(out=xln_s[:, tt * TT:(tt + 1) * TT].rearrange("(kc pp) t -> pp kc t", pp=128),
                            in_=xlnfm[:].rearrange("pp (kc t) -> pp kc t", kc=NKC)).then_inc(sx.h, 16)
            p.on("gpsimd", f); sx.v += 16
            p.wait("scalar", sx, sx.v)
            p.wait("scalar", mmk, mmk.v)
        p.wait("gpsimd", sx, sx.v)
        blk = es.enter_context(nc.Block("p1a"))
        p.emit(blk)


def _p1b(nc, wq_b, xln_s, q_s, gath=None):
    with contextlib.ExitStack() as es:
        wqs = [es.enter_context(nc.sbuf_tensor(f"qws{i}", [128, NKC * 128], BF16)) for i in range(2)]
        xt = [es.enter_context(nc.sbuf_tensor(f"qxt{i}", [128, NKC * TT], BF16)) for i in range(2)]
        qstage = es.enter_context(nc.sbuf_tensor("qstage", [128, NH * TT], BF16))
        pq = es.enter_context(nc.psum_tensor("pq", [128, 2 * TT], F32))
        ldw = Sem(es, nc, "q_ldw")
        ldx = Sem(es, nc, "q_ldx")
        mm = Sem(es, nc, "q_mm")
        qe = Sem(es, nc, "q_qe")
        st = Sem(es, nc, "q_st")
        p = Prog()
        if gath is not None:
            gath.start_phase(es, width=2048)

        for tt in range(NTILE):
            if tt >= 2:
                p.wait("sync", mm, (tt - 1) * NH)
            def f(e, tt=tt, buf=xt[tt % 2]):
                e.dma_start(out=buf[:].rearrange("pp (kc t) -> pp kc t", kc=NKC), in_=xln_s[:, tt * TT:(tt + 1) * TT].rearrange("(kc pp) t -> pp kc t", pp=128)).then_inc(ldx.h, 16)
            p.on("sync", f); ldx.v += 16
            if tt >= 1:
                p.wait("scalar", st, 16 * tt)
            for m in range(NH):
                mi = tt * NH + m
                if mi >= 2:
                    p.wait("sync", mm, mi - 1)
                def f(e, m=m, buf=wqs[mi % 2]):
                    e.dma_start(out=buf[:].rearrange("pp (kc j) -> pp kc j", kc=NKC),
                                in_=wq_b[:, m * 128:(m + 1) * 128].rearrange("(kc pp) j -> pp kc j", pp=128)).then_inc(ldw.h, 16)
                p.on("sync", f); ldw.v += 16
                p.wait("tensor", ldw, ldw.v)
                if m == 0:
                    p.wait("tensor", ldx, 16 * (tt + 1))
                if mi >= 2:
                    p.wait("tensor", qe, mi - 1)
                for kc in range(NKC):
                    last = (kc == NKC - 1)
                    def f(e, m=m, kc=kc, buf=xt[tt % 2], wbuf=wqs[mi % 2], slot=mi % 2, last=last):
                        i_ = e.matmul(pq[:, slot * TT:slot * TT + TT],
                                      wbuf[:, kc * 128:(kc + 1) * 128],
                                      buf[:, kc * TT:(kc + 1) * TT], start=(kc == 0), stop=last)
                        if last:
                            i_.then_inc(mm.h, 1)
                    p.on("tensor", f)
                mm.v += 1
                p.wait("scalar", mm, mm.v)
                def f(e, m=m, slot=mi % 2):
                    e.copy(qstage[:, m * TT:(m + 1) * TT], pq[:, slot * TT:slot * TT + TT]).then_inc(qe.h, 1)
                p.on("scalar", f); qe.v += 1
                if gath is not None and m % 4 == 3:
                    gath.emit_slice(p, 3)
            p.wait("gpsimd", qe, qe.v)
            def f(e, tt=tt):
                e.dma_start(out=q_s[:, tt * TT:(tt + 1) * TT].rearrange("(kc pp) t -> pp kc t", pp=128),
                            in_=qstage[:].rearrange("pp (kc t) -> pp kc t", kc=NKC)).then_inc(st.h, 16)
            p.on("gpsimd", f); st.v += 16
        if gath is not None:
            gath.flush_phase(p)
        p.wait("gpsimd", st, st.v)
        blk = es.enter_context(nc.Block("p1b"))
        p.emit(blk)


def _p2(nc, q_s, ao_s, k_cache, v_cache, ones_col, ones_row, gath=None, ident=None):
    partial_D = (-384, -256, -128, 0, 640, 768, 896, 1024)
    with contextlib.ExitStack() as es:
        qt_b = [es.enter_context(nc.sbuf_tensor(f"qt{i}", [128, NH * TT], BF16)) for i in range(2)]
        probs = es.enter_context(nc.sbuf_tensor("probs", [128, 3 * TT], BF16))
        masks = es.enter_context(nc.sbuf_tensor("masks", [128, 8 * TT], BF16))
        m_io = es.enter_context(nc.sbuf_tensor("m_io", [128, TT], I32))
        m_t1 = es.enter_context(nc.sbuf_tensor("m_t1", [128, TT], F32))
        m_t2 = es.enter_context(nc.sbuf_tensor("m_t2", [128, TT], F32))
        onehot = es.enter_context(nc.sbuf_tensor("onehot", [128, NH * NH], BF16))
        oh_io = es.enter_context(nc.sbuf_tensor("oh_io", [128, NH], I32))
        ohrow = es.enter_context(nc.sbuf_tensor("ohrow", [16, NH * 128], F32))
        ohr_io = es.enter_context(nc.sbuf_tensor("ohr_io", [16, 128], I32))
        srec = es.enter_context(nc.sbuf_tensor("srec", [NH, TT], F32))
        ao_stage = es.enter_context(nc.sbuf_tensor("ao_stage", [128, NH * TT], BF16))
        pS = es.enter_context(nc.psum_tensor("pS", [128, 3 * TT], F32))
        pAV = es.enter_context(nc.psum_tensor("pAV", [128, TT], F32))
        pDen = es.enter_context(nc.psum_tensor("pDen", [128, TT], F32))
        pB = es.enter_context(nc.psum_tensor("pB", [128, TT], F32))
        ldq = Sem(es, nc, "a_ldq")
        msk = Sem(es, nc, "a_msk")
        sS = Sem(es, nc, "a_sS")
        sM = Sem(es, nc, "a_sM")
        sP = Sem(es, nc, "a_sP")
        sAV = Sem(es, nc, "a_sAV")
        sAVe = Sem(es, nc, "a_sAVe")
        sDen = Sem(es, nc, "a_sDen")
        sRec = Sem(es, nc, "a_sRec")
        sB = Sem(es, nc, "a_sB")
        sSc = Sem(es, nc, "a_sSc")
        sAOD = Sem(es, nc, "a_sAOD")
        p = Prog()
        if gath is not None:
            gath.start_phase(es, width=2048)

        D_list = sorted(partial_D)
        mask_off = {Dv: i * TT for i, Dv in enumerate(D_list)}
        def f(e):
            e.iota(m_io[:], pattern=[[1, TT]], base=128, channel_multiplier=-1).then_inc(msk.h, 1)
            e.iota(oh_io[:], pattern=[[1, NH]], base=0, channel_multiplier=0).then_inc(msk.h, 1)
            e.iota(ohr_io[:], pattern=[[0, 128]], base=0, channel_multiplier=1).then_inc(msk.h, 1)
        p.on("gpsimd", f); msk.v += 3
        p.wait("vector", msk, msk.v)
        def f(e):
            for i, Dv in enumerate(D_list):
                e.tensor_scalar(out=m_t1[:], in0=m_io[:], scalar1=float(128 - Dv), scalar2=None, op0=ALU.is_ge)
                e.tensor_scalar(out=m_t2[:], in0=m_io[:], scalar1=float(1152 - Dv), scalar2=None, op0=ALU.is_le)
                e.drain()
                e.tensor_tensor(out=m_t1[:], in0=m_t1[:], in1=m_t2[:], op=ALU.mult)
                e.drain()
                e.tensor_scalar(out=masks[:, i * TT:(i + 1) * TT], in0=m_t1[:],
                                scalar1=-1.0, scalar2=-NEG, op0=ALU.add, op1=ALU.mult)
                e.drain()
            for h in range(NH):
                e.tensor_scalar(out=onehot[:, h * NH:(h + 1) * NH], in0=oh_io[:],
                                scalar1=float(h), scalar2=None, op0=ALU.is_equal)
                e.tensor_scalar(out=ohrow[:, h * 128:(h + 1) * 128], in0=ohr_io[:],
                                scalar1=float(h), scalar2=None, op0=ALU.is_equal)
            e.drain()
            e.nop().then_inc(msk.h, 1)
        p.on("vector", f); msk.v += 1

        chunk_i = 0
        for qt in range(NTILE):
            qt0 = qt * TT
            if qt >= 2:
                p.wait("sync", sAOD, 16 * (qt - 1))
            def f(e, qt=qt, buf=qt_b[qt % 2], ldq=ldq):
                e.dma_start(out=buf[:].rearrange("pp (kc t) -> pp kc t", kc=NKC), in_=q_s[:, qt * TT:(qt + 1) * TT].rearrange("(kc pp) t -> pp kc t", pp=128)).then_inc(ldq.h, 16)
            p.on("sync", f); ldq.v += 16
            p.wait("tensor", ldq, 16 * (qt + 1))
            if qt == 0:
                p.wait("tensor", msk, msk.v)
            if qt >= 1:
                p.wait("scalar", sAOD, 16 * qt)
            k0s = list(range(max(0, qt0 - WIN), qt0 + TT - 128 + 1, 128))
            nch = len(k0s)
            for h in range(NH):
                kv = h // (NH // KVH)

                def emit_qk(ci, base_ci):
                    k0 = k0s[ci]
                    gci = base_ci + ci
                    slot = gci % 3
                    Dv = qt0 - k0
                    masked = Dv in partial_D
                    if gci >= 3:
                        p.wait("tensor", sP, gci - 2)
                    def f(e, kv=kv, k0=k0, h=h, buf=qt_b[qt % 2], slot=slot, masked=masked,
                          moff=(mask_off[Dv] if masked else 0)):
                        i_ = e.matmul(pS[:, slot * TT:slot * TT + TT],
                                      k_cache[:, kv * S + k0:kv * S + k0 + 128],
                                      buf[:, h * TT:(h + 1) * TT], start=True, stop=not masked)
                        if masked:
                            i_ = e.matmul(pS[:, slot * TT:slot * TT + TT], ident[:],
                                          masks[:, moff:moff + TT], start=False, stop=True)
                        i_.then_inc(sS.h, 1)
                    p.on("tensor", f); sS.v += 1
                    p.wait("scalar", sS, sS.v)
                    if gci >= 3:
                        p.wait("scalar", sAV, gci - 2)
                    def f(e, slot=slot):
                        e.activation(out=probs[:, slot * TT:slot * TT + TT],
                                     in_=pS[:, slot * TT:slot * TT + TT], func=AF.Exp,
                                     scale=SCALE).then_inc(sP.h, 1)
                    p.on("scalar", f); sP.v += 1

                def emit_av(ci, base_ci):
                    k0 = k0s[ci]
                    gci = base_ci + ci
                    slot = gci % 3
                    p.wait("tensor", sP, gci + 1)
                    is_first = (ci == 0)
                    is_last = (ci == nch - 1)
                    if is_first and qt * NH + h >= 1:
                        p.wait("tensor", sAVe, qt * NH + h)
                    def f(e, slot=slot, kv=kv, k0=k0, h=h,
                          is_first=is_first, is_last=is_last):
                        i_d = e.matmul(pDen[0:NH, :], onehot[:, h * NH:(h + 1) * NH],
                                       probs[:, slot * TT:slot * TT + TT],
                                       start=(h == 0 and is_first), stop=(h == NH - 1 and is_last))
                        if h == NH - 1 and is_last:
                            i_d.then_inc(sDen.h, 1)
                        e.matmul(pAV[:, :], v_cache[:, kv * S + k0:kv * S + k0 + 128],
                                 probs[:, slot * TT:slot * TT + TT],
                                 start=is_first, stop=is_last).then_inc(sAV.h, 1)
                    p.on("tensor", f); sAV.v += 1
                    if h == NH - 1 and is_last:
                        sDen.v += 1

                base_ci = chunk_i
                emit_qk(0, base_ci)
                if nch > 1:
                    emit_qk(1, base_ci)
                for ci in range(nch):
                    if ci + 2 < nch:
                        emit_qk(ci + 2, base_ci)
                    emit_av(ci, base_ci)
                chunk_i += nch
                p.wait("scalar", sAV, sAV.v)
                def f(e, h=h):
                    e.copy(ao_stage[:, h * TT:(h + 1) * TT], pAV[:, :]).then_inc(sAVe.h, 1)
                p.on("scalar", f); sAVe.v += 1
                if gath is not None:
                    for _ in range(4):
                        gath.emit_slice(p, 2)
            p.wait("vector", sDen, sDen.v)
            def f(e):
                e.reciprocal(srec[:], pDen[0:NH, :]).then_inc(sRec.h, 1)
            p.on("vector", f); sRec.v += 1
            p.wait("tensor", sRec, sRec.v)
            for h in range(NH):
                if sB.v >= 1:
                    p.wait("tensor", sSc, sB.v)
                def f(e, h=h):
                    e.matmul(pB[:, :], ohrow[:, h * 128:(h + 1) * 128], srec[:],
                             start=True, stop=True).then_inc(sB.h, 1)
                p.on("tensor", f); sB.v += 1
                p.wait("vector", sB, sB.v)
                p.wait("vector", sAVe, sAVe.v - (NH - 1 - h))
                def f(e, h=h):
                    e.tensor_tensor(out=ao_stage[:, h * TT:(h + 1) * TT],
                                    in0=ao_stage[:, h * TT:(h + 1) * TT],
                                    in1=pB[:, :], op=ALU.mult).then_inc(sSc.h, 1)
                p.on("vector", f); sSc.v += 1
            p.wait("gpsimd", sSc, sSc.v)
            def f(e, qt=qt):
                e.dma_start(out=ao_s[:, qt * TT:(qt + 1) * TT].rearrange("(kc pp) t -> pp kc t", pp=128),
                            in_=ao_stage[:].rearrange("pp (kc t) -> pp kc t", kc=NKC)).then_inc(sAOD.h, 16)
            p.on("gpsimd", f); sAOD.v += 16
            p.wait("vector", sAOD, sAOD.v)
        if gath is not None:
            gath.flush_phase(p)
        p.wait("gpsimd", sAOD, sAOD.v)
        blk = es.enter_context(nc.Block("p2"))
        p.emit(blk)


def _p3(nc, x, wo_b, ao_s, x1_s, y_s, ident, gath=None):
    with contextlib.ExitStack() as es:
        wos = [es.enter_context(nc.sbuf_tensor(f"ows{i}", [128, NKC * 128], BF16)) for i in range(2)]
        aot = [es.enter_context(nc.sbuf_tensor(f"aot{i}", [128, NKC * TT], BF16)) for i in range(2)]
        obuf = es.enter_context(nc.sbuf_tensor("obuf", [128, NH * TT], BF16))
        oT = es.enter_context(nc.sbuf_tensor("oT", [128, H], BF16))
        xc_b = [es.enter_context(nc.sbuf_tensor(f"p3x{i}", [128, H], F32)) for i in range(2)]
        x1c = [es.enter_context(nc.sbuf_tensor(f"x1c{i}", [128, H], F32)) for i in range(2)]
        ytm = es.enter_context(nc.sbuf_tensor("ytm", [128, H], BF16))
        yfm = es.enter_context(nc.sbuf_tensor("yfm", [128, NKC * TT], BF16))
        xstat = es.enter_context(nc.sbuf_tensor("p3stat", [128, 24], F32))
        xaggr = es.enter_context(nc.sbuf_tensor("p3aggr", [128, 2], F32))
        nmean = es.enter_context(nc.sbuf_tensor("p3nm", [128, 1], F32))
        rstd = es.enter_context(nc.sbuf_tensor("p3rs", [128, 1], F32))
        po = es.enter_context(nc.psum_tensor("po", [128, 2 * TT], F32))
        pto = es.enter_context(nc.psum_tensor("p3pto", [128, 2048], BF16))
        pty = es.enter_context(nc.psum_tensor("p3pty", [128, 2048], BF16))
        ldw = Sem(es, nc, "o_ldw")
        lda = Sem(es, nc, "o_lda")
        ldxc = Sem(es, nc, "o_ldxc")
        mm = Sem(es, nc, "o_mm")
        oev = Sem(es, nc, "o_oev")
        otr = Sem(es, nc, "o_otr")
        ote = Sem(es, nc, "o_ote")
        sx1 = Sem(es, nc, "o_sx1")
        sx1d = Sem(es, nc, "o_sx1d")
        lnq = Sem(es, nc, "o_lnq")
        lnr = Sem(es, nc, "o_lnr")
        syt = Sem(es, nc, "o_syt")
        ytr = Sem(es, nc, "o_ytr")
        yev = Sem(es, nc, "o_yev")
        syd = Sem(es, nc, "o_syd")
        p = Prog()
        NCH = 4
        if gath is not None:
            gath.start_phase(es, width=2048)

        for tt in range(NTILE):
            if tt >= 2:
                p.wait("sync", mm, (tt - 1) * NH)
            def f(e, tt=tt, buf=aot[tt % 2], lda=lda):
                e.dma_start(out=buf[:].rearrange("pp (kc t) -> pp kc t", kc=NKC), in_=ao_s[:, tt * TT:(tt + 1) * TT].rearrange("(kc pp) t -> pp kc t", pp=128)).then_inc(lda.h, 16)
            p.on("sync", f); lda.v += 16
            for c in range(NCH):
                k = tt * NCH + c
                if k >= 2:
                    p.wait("gpsimd", sx1, k - 1)
                def f(e, tt=tt, c=c, buf=xc_b[k % 2], ldxc=ldxc):
                    e.dma_start(out=buf[:], in_=x[tt * TT + c * 128:tt * TT + (c + 1) * 128, :]).then_inc(ldxc.h, 16)
                p.on("gpsimd", f); ldxc.v += 16
            if tt >= 1:
                p.wait("scalar", otr, tt * NCH * NKC)
            for m in range(NH):
                mi = tt * NH + m
                if mi >= 2:
                    p.wait("sync", mm, mi - 1)
                def f(e, m=m, buf=wos[mi % 2]):
                    e.dma_start(out=buf[:].rearrange("pp (kc j) -> pp kc j", kc=NKC),
                                in_=wo_b[:, m * 128:(m + 1) * 128].rearrange("(kc pp) j -> pp kc j", pp=128)).then_inc(ldw.h, 16)
                p.on("sync", f); ldw.v += 16
                p.wait("tensor", ldw, ldw.v)
                if m == 0:
                    p.wait("tensor", lda, 16 * (tt + 1))
                if mi >= 2:
                    p.wait("tensor", oev, mi - 1)
                for kc in range(NKC):
                    last = (kc == NKC - 1)
                    def f(e, m=m, kc=kc, buf=aot[tt % 2], wbuf=wos[mi % 2], slot=mi % 2, last=last):
                        i_ = e.matmul(po[:, slot * TT:slot * TT + TT],
                                      wbuf[:, kc * 128:(kc + 1) * 128],
                                      buf[:, kc * TT:(kc + 1) * TT], start=(kc == 0), stop=last)
                        if last:
                            i_.then_inc(mm.h, 1)
                    p.on("tensor", f)
                mm.v += 1
                if gath is not None and m % 4 == 3:
                    gath.emit_slice(p, 2)
                p.wait("scalar", mm, mm.v)
                def f(e, m=m, slot=mi % 2):
                    e.copy(obuf[:, m * TT:(m + 1) * TT], po[:, slot * TT:slot * TT + TT]).then_inc(oev.h, 1)
                p.on("scalar", f); oev.v += 1
            p.wait("tensor", oev, oev.v)
            for c in range(NCH):
                k = tt * NCH + c
                for mch in range(NKC):
                    ti = k * NKC + mch
                    if ti >= 2:
                        p.wait("tensor", ote, ti - 1)
                    def f(e, c=c, mch=mch, slot=ti % 2):
                        e.transpose(pto[:, slot * 1024:slot * 1024 + 128],
                                    obuf[:, mch * TT + c * 128:mch * TT + c * 128 + 128],
                                    ident[:]).then_inc(otr.h, 1)
                    p.on("tensor", f)
                    p.wait("scalar", otr, ti + 1)
                    def f(e, mch=mch, slot=ti % 2):
                        e.copy(oT[:, mch * 128:(mch + 1) * 128],
                               pto[:, slot * 1024:slot * 1024 + 128]).then_inc(ote.h, 1)
                    p.on("scalar", f); ote.v += 1
                otr.v = (k + 1) * NKC
                p.wait("vector", ote, ote.v)
                p.wait("vector", ldxc, 16 * (k + 1))
                if k >= 2:
                    p.wait("vector", sx1d, 16 * (k - 1))
                def f(e, k=k):
                    e.tensor_tensor(out=x1c[k % 2][:], in0=xc_b[k % 2][:], in1=oT[:], op=ALU.add).then_inc(sx1.h, 1)
                    e.drain()
                    for i in range(4):
                        e.bn_stats(xstat[:, 6 * i:6 * (i + 1)], x1c[k % 2][:, i * 512:(i + 1) * 512])
                    e.drain()
                    e.bn_aggr(xaggr[:], xstat[:])
                    e.drain()
                    e.tensor_scalar(out=nmean[:], in0=xaggr[:, 0:1], scalar1=-1.0, scalar2=None, op0=ALU.mult)
                    e.tensor_scalar(out=rstd[:], in0=xaggr[:, 1:2], scalar1=EPS, scalar2=None, op0=ALU.add).then_inc(lnq.h, 1)
                p.on("vector", f); sx1.v += 1; lnq.v += 1
                p.wait("scalar", sx1, sx1.v)
                p.wait("scalar", lnq, lnq.v)
                def f(e):
                    e.activation(out=rstd[:], in_=rstd[:], func=AF.Ln)
                    e.drain()
                    e.activation(out=rstd[:], in_=rstd[:], func=AF.Exp, scale=-0.5).then_inc(lnr.h, 1)
                p.on("scalar", f); lnr.v += 1
                p.wait("vector", lnr, lnr.v)
                if k >= 1:
                    p.wait("vector", ytr, k * NKC)
                def f(e, k=k):
                    e.tensor_scalar(out=ytm[:], in0=x1c[k % 2][:], scalar1=nmean[:], scalar2=rstd[:],
                                    op0=ALU.add, op1=ALU.mult).then_inc(syt.h, 1)
                p.on("vector", f); syt.v += 1
                p.wait("gpsimd", sx1, sx1.v)
                def f(e, tt=tt, c=c, k=k):
                    e.dma_start(out=x1_s[tt * TT + c * 128:tt * TT + (c + 1) * 128, :], in_=x1c[k % 2][:]).then_inc(sx1d.h, 16)
                p.on("gpsimd", f); sx1d.v += 16
                p.wait("tensor", syt, syt.v)
                for kc in range(NKC):
                    ti = k * NKC + kc
                    if ti >= 2:
                        p.wait("tensor", yev, ti - 1)
                    def f(e, kc=kc, slot=ti % 2):
                        e.transpose(pty[:, slot * 1024:slot * 1024 + 128],
                                    ytm[:, kc * 128:(kc + 1) * 128], ident[:]).then_inc(ytr.h, 1)
                    p.on("tensor", f)
                    p.wait("scalar", ytr, ti + 1)
                    def f(e, kc=kc, c=c, slot=ti % 2):
                        e.copy(yfm[:, kc * TT + c * 128:kc * TT + c * 128 + 128],
                               pty[:, slot * 1024:slot * 1024 + 128]).then_inc(yev.h, 1)
                    p.on("scalar", f); yev.v += 1
                ytr.v = (k + 1) * NKC
            p.wait("gpsimd", yev, yev.v)
            def f(e, tt=tt):
                e.dma_start(out=y_s[:, tt * TT:(tt + 1) * TT].rearrange("(kc pp) t -> pp kc t", pp=128),
                            in_=yfm[:].rearrange("pp (kc t) -> pp kc t", kc=NKC)).then_inc(syd.h, 16)
            p.on("gpsimd", f); syd.v += 16
            p.wait("scalar", syd, syd.v)
        if gath is not None:
            gath.flush_phase(p)
        p.wait("gpsimd", syd, syd.v)
        p.wait("gpsimd", sx1d, sx1d.v)
        blk = es.enter_context(nc.Block("p3"))
        p.emit(blk)


DBG_P4 = []
def _p4(nc, y_s, x1_s, wg_s, wu_s, wd_s, out, ident):
    TT4 = 1024
    NT4 = S // TT4
    NG = 1
    n_groups = (NIC + NG - 1) // NG
    with contextlib.ExitStack() as es:
        yfm = es.enter_context(nc.sbuf_tensor("f_yfm", [128, NKC * TT4], BF16))
        hb = es.enter_context(nc.sbuf_tensor("f_h", [128, NIC * TT4], BF16))
        wgu = [es.enter_context(nc.sbuf_tensor(f"f_wgu{i}", [128, 2 * NKC * NG * 128], BF16)) for i in range(2)]
        wdb = [es.enter_context(nc.sbuf_tensor(f"f_wd{i}", [128, INTER], BF16)) for i in range(2)]
        sgl = es.enter_context(nc.sbuf_tensor("f_sgl", [128, 512], F32))
        dstage = es.enter_context(nc.sbuf_tensor("f_dst", [128, 2 * NKC * 512], BF16))
        dT = es.enter_context(nc.sbuf_tensor("f_dT", [128, H], BF16))
        x1b = [es.enter_context(nc.sbuf_tensor("f_x1", [128, H], F32))] * 2
        
        pgu = es.enter_context(nc.psum_tensor("f_pgu", [128, 4 * 512], F32))
        pd = es.enter_context(nc.psum_tensor("f_pd", [128, 2 * 512], F32))
        ptd = es.enter_context(nc.psum_tensor("f_ptd", [128, 2048], BF16))
        ldy = Sem(es, nc, "f_ldy")
        ldw = Sem(es, nc, "f_ldw")
        ldd = Sem(es, nc, "f_ldd")
        ldx1 = Sem(es, nc, "f_ldx1")
        mmg = Sem(es, nc, "f_mmg")
        sil = Sem(es, nc, "f_sil")
        hwr = Sem(es, nc, "f_hwr")
        mmd = Sem(es, nc, "f_mmd")
        dev = Sem(es, nc, "f_dev")
        dtr = Sem(es, nc, "f_dtr")
        dte = Sem(es, nc, "f_dte")
        sou = Sem(es, nc, "f_sou")
        sod = Sem(es, nc, "f_sod")
        p = Prog()

        grp_ics = [list(range(g * NG, min((g + 1) * NG, NIC))) for g in range(n_groups)]
        grp_units = [2 * len(g) for g in grp_ics]
        cumu = [0]
        for g in grp_units:
            cumu.append(cumu[-1] + g)
        units_per_tile = cumu[-1]
        gcum = [0]
        for t4_ in range(NT4):
            for g_ in range(n_groups):
                gcum.append(gcum[-1] + grp_units[g_])

        for t4 in range(NT4):
            if t4 >= 1:
                p.wait("sync", hwr, t4 * units_per_tile)
                p.wait("vector", mmd, t4 * NKC * 2)
            def f(e, t4=t4):
                e.dma_start(out=yfm[:].rearrange("pp (kc t) -> pp kc t", kc=NKC), in_=y_s[:, t4 * TT4:(t4 + 1) * TT4].rearrange("(kc pp) t -> pp kc t", pp=128)).then_inc(ldy.h, 16)
            p.on("sync", f); ldy.v += 16
            for gi in range(n_groups):
                ics = grp_ics[gi]
                n = len(ics)
                gj = t4 * n_groups + gi
                gslot = gj % 2
                if gj >= 2:
                    p.wait("sync", hwr, gcum[gj - 1])
                def f(e, ics=ics, n=n, buf=wgu[gslot]):
                    e.dma_start(out=buf[:, 0:NKC * n * 128].rearrange("pp (kc j) -> pp kc j", kc=NKC),
                                in_=wg_s[:, :, ics[0] * 128:ics[0] * 128 + n * 128].rearrange("kc pp j -> pp kc j")).then_inc(ldw.h, 16)
                    e.dma_start(out=buf[:, NKC * NG * 128:NKC * NG * 128 + NKC * n * 128].rearrange("pp (kc j) -> pp kc j", kc=NKC),
                                in_=wu_s[:, :, ics[0] * 128:ics[0] * 128 + n * 128].rearrange("kc pp j -> pp kc j")).then_inc(ldw.h, 16)
                p.on("sync", f); ldw.v += 32
                p.wait("tensor", ldw, ldw.v)
                if gi == 0:
                    p.wait("tensor", ldy, ldy.v)
                for icl, ic in enumerate(ics):
                    for tc in range(2):
                        ui = t4 * units_per_tile + cumu[gi] + icl * 2 + tc
                        gs = ui % 2
                        if ui >= 2:
                            p.wait("tensor", sil, ui - 1)
                            p.wait("tensor", hwr, ui - 1)
                        for kc in range(NKC):
                            last = (kc == NKC - 1)
                            def f(e, icl=icl, kc=kc, tc=tc, gs=gs, gslot=gslot, last=last, n=n):
                                e.matmul(pgu[:, gs * 512:gs * 512 + 512],
                                         wgu[gslot][:, (kc * n + icl) * 128:(kc * n + icl) * 128 + 128],
                                         yfm[:, kc * TT4 + tc * 512:kc * TT4 + tc * 512 + 512],
                                         start=(kc == 0), stop=last)
                                i_ = e.matmul(pgu[:, (2 + gs) * 512:(2 + gs) * 512 + 512],
                                              wgu[gslot][:, NKC * NG * 128 + (kc * n + icl) * 128:NKC * NG * 128 + (kc * n + icl) * 128 + 128],
                                              yfm[:, kc * TT4 + tc * 512:kc * TT4 + tc * 512 + 512],
                                              start=(kc == 0), stop=last)
                                if last:
                                    i_.then_inc(mmg.h, 1)
                            p.on("tensor", f)
                        mmg.v += 1
                        p.wait("scalar", mmg, mmg.v)
                        if ui >= 1:
                            p.wait("scalar", hwr, ui)
                        def f(e, gs=gs):
                            e.activation(out=sgl[:], in_=pgu[:, gs * 512:gs * 512 + 512], func=AF.Silu).then_inc(sil.h, 1)
                        p.on("scalar", f); sil.v += 1
                        p.wait("vector", sil, sil.v)
                        def f(e, ic=ic, tc=tc, gs=gs):
                            e.tensor_tensor(out=hb[:, ic * TT4 + tc * 512:ic * TT4 + tc * 512 + 512],
                                            in0=sgl[:], in1=pgu[:, (2 + gs) * 512:(2 + gs) * 512 + 512],
                                            op=ALU.mult).then_inc(hwr.h, 1)
                        p.on("vector", f); hwr.v += 1
            for m in range(NKC):
                md = t4 * NKC + m
                if md >= 2:
                    p.wait("sync", mmd, (md - 1) * 2)
                def f(e, m=m, buf=wdb[md % 2]):
                    e.dma_start(out=buf[:], in_=wd_s[m, :, :]).then_inc(ldd.h, 16)
                p.on("sync", f); ldd.v += 16
                p.wait("tensor", ldd, ldd.v)
                if m == 0:
                    p.wait("tensor", hwr, (t4 + 1) * units_per_tile)
                for tc in range(2):
                    di = md * 2 + tc
                    if di >= 2:
                        p.wait("tensor", dev, di - 1)
                    for ic in range(NIC):
                        last = (ic == NIC - 1)
                        def f(e, ic=ic, tc=tc, slot=di % 2, buf=wdb[md % 2], last=last):
                            i_ = e.matmul(pd[:, slot * 512:slot * 512 + 512],
                                          buf[:, ic * 128:(ic + 1) * 128],
                                          hb[:, ic * TT4 + tc * 512:ic * TT4 + tc * 512 + 512],
                                          start=(ic == 0), stop=last)
                            if last:
                                i_.then_inc(mmd.h, 1)
                        p.on("tensor", f)
                    mmd.v += 1
                    p.wait("scalar", mmd, mmd.v)
                    if t4 >= 1 and di < 2:
                        p.wait("scalar", dtr, t4 * 8 * NKC)
                    def f(e, m=m, tc=tc, slot=di % 2):
                        e.copy(dstage[:, tc * NKC * 512 + m * 512:tc * NKC * 512 + m * 512 + 512],
                               pd[:, slot * 512:slot * 512 + 512]).then_inc(dev.h, 1)
                    p.on("scalar", f); dev.v += 1
            p.wait("tensor", dev, dev.v)
            for tc in range(2):
                for c in range(4):
                    k = t4 * 8 + tc * 4 + c
                    if k >= 1:
                        p.wait("sync", sou, k)
                        p.wait("sync", sod, 16 * k)
                    def f(e, t4=t4, tc=tc, c=c, buf=x1b[k % 2]):
                        e.dma_start(out=buf[:], in_=x1_s[t4 * TT4 + tc * 512 + c * 128:t4 * TT4 + tc * 512 + (c + 1) * 128, :]).then_inc(ldx1.h, 16)
                    p.on("sync", f); ldx1.v += 16
                    for m in range(NKC):
                        ti = k * NKC + m
                        if ti >= 2:
                            p.wait("tensor", dte, ti - 1)
                        def f(e, m=m, tc=tc, c=c, slot=ti % 2):
                            e.transpose(ptd[:, slot * 1024:slot * 1024 + 128],
                                        dstage[:, tc * NKC * 512 + m * 512 + c * 128:tc * NKC * 512 + m * 512 + c * 128 + 128],
                                        ident[:]).then_inc(dtr.h, 1)
                        p.on("tensor", f)
                        p.wait("scalar", dtr, ti + 1)
                        def f(e, m=m, slot=ti % 2):
                            e.copy(dT[:, m * 128:(m + 1) * 128],
                                   ptd[:, slot * 1024:slot * 1024 + 128]).then_inc(dte.h, 1)
                        p.on("scalar", f); dte.v += 1
                    dtr.v = (k + 1) * NKC
                    p.wait("vector", dte, dte.v)
                    p.wait("vector", ldx1, 16 * (k + 1))
                    if k >= 1:
                        p.wait("vector", sod, 16 * k)
                    def f(e, k=k):
                        e.tensor_tensor(out=x1b[k % 2][:], in0=x1b[k % 2][:], in1=dT[:], op=ALU.add).then_inc(sou.h, 1)
                    p.on("vector", f); sou.v += 1
                    p.wait("scalar", sou, sou.v)
                    p.wait("gpsimd", sou, sou.v)
                    def f(e, t4=t4, tc=tc, c=c, k=k):
                        e.dma_start(out=out[t4 * TT4 + tc * 512 + c * 128:t4 * TT4 + tc * 512 + (c + 1) * 128, :],
                                    in_=x1b[k % 2][:]).then_inc(sod.h, 16)
                    p.on("gpsimd", f); sod.v += 16
        p.wait("gpsimd", sod, sod.v)
        if DBG_P4:
            hdmp = nc.dram_tensor("dbg_h_last", [128, NIC * TT4], BF16, kind="ExternalOutput").ap()
            ddmp = nc.dram_tensor("dbg_dst_last", [128, 2 * NKC * 512], BF16, kind="ExternalOutput").ap()
            hst = Sem(es, nc, "f_hdmp")
            def f(e):
                e.dma_start(out=hdmp[:, :], in_=hb[:]).then_inc(hst.h, 16)
                e.dma_start(out=ddmp[:, :], in_=dstage[:]).then_inc(hst.h, 16)
                e.wait_ge(hst.h, 32)
            p.on("gpsimd", f)
        blk = es.enter_context(nc.Block("p4"))
        p.emit(blk)


from concourse.bass_utils import run_bass_kernel_spmd

_NC_CACHE = []

def kernel(L0_attn_hidden_states, L0_attn_Wq, L0_attn_Wk, L0_attn_Wv, L0_attn_Wo,
           L0_ffn_lut, L0_ffn_gate_idx, L0_ffn_up_idx, L0_ffn_down_idx):
    x = np.ascontiguousarray(np.asarray(L0_attn_hidden_states, dtype=np.float32))
    wq_a = np.ascontiguousarray(np.asarray(L0_attn_Wq, dtype=np.float32))
    wk_a = np.ascontiguousarray(np.asarray(L0_attn_Wk, dtype=np.float32))
    wv_a = np.ascontiguousarray(np.asarray(L0_attn_Wv, dtype=np.float32))
    wo_a = np.ascontiguousarray(np.asarray(L0_attn_Wo, dtype=np.float32))
    lut_a = np.asarray(L0_ffn_lut, dtype=np.float32)
    lutr = np.ascontiguousarray(np.repeat(lut_a[None, :], 128, axis=0))
    gi = np.ascontiguousarray(np.asarray(L0_ffn_gate_idx, dtype=np.int32))
    ui = np.ascontiguousarray(np.asarray(L0_ffn_up_idx, dtype=np.int32))
    di = np.ascontiguousarray(np.asarray(L0_ffn_down_idx, dtype=np.int32))

    if not _NC_CACHE:
        _NC_CACHE.append(build()[0])
    nc = _NC_CACHE[0]
    in_maps = []
    for core in range(8):
        in_maps.append({
            "x": np.ascontiguousarray(x[core]),
            "wq": wq_a, "wk": wk_a, "wv": wv_a, "wo": wo_a,
            "lutr": lutr, "gidx": gi, "uidx": ui, "didx": di,
        })
    res = run_bass_kernel_spmd(nc, in_maps, core_ids=list(range(8)))
    return np.stack([np.asarray(res.results[i]["out"], dtype=np.float32) for i in range(8)], axis=0)
